# version 01
# baseline (speedup 1.0000x reference)
import numpy as np
from contextlib import ExitStack, contextmanager

import concourse.bass as bass
import concourse.mybir as mybir
from concourse.bass_utils import run_bass_kernel_spmd

F32 = mybir.dt.float32
BF16 = mybir.dt.bfloat16
AF = mybir.ActivationFunctionType
ALU = mybir.AluOpType
AX = mybir.AxisListType

ATTN0_HALVES = 2
D = 2048
S = 2048
NK = 16
TW = 512
NT128 = S // 128
I0 = 5632
IC0 = I0 // 128
EPS = 1e-5



class Sem:
    def __init__(self, h):
        self.h = h
        self.v = 0

    def inc(self, instr, n=1):
        instr.then_inc(self.h, n)
        self.v += n
        return self.v


class _FakeIns:
    def then_inc(self, h, n=1):
        return self


class _FakeEng:
    def __getattr__(self, name):
        return lambda *a, **k: _FakeIns()


class E:

    def __init__(self, eng, tick):
        self.e = eng
        self.t = tick

    def __getattr__(self, name):
        f = getattr(self.e, name)

        def g(*a, **k):
            return self.t.inc(f(*a, **k))

        return g


class Em:

    def __init__(self, kb):
        self.kb = kb
        self.T = {}
        self.dry = True

    @contextmanager
    def block(self, name):
        if self.dry:
            self._blk = None
            yield
        else:
            with self.kb.nc.Block(name) as b:
                self._blk = b
                yield

    def on(self, engname, f):
        if self.dry:
            f(_FakeEng())
        else:
            getattr(self._blk, engname)(f)

    def rec(self, key, sem):
        if self.dry:
            self.T[key] = sem.v
        else:
            assert self.T.get(key) == sem.v, (key, self.T.get(key), sem.v)
        return sem.v

    def w(self, eng, sem, key, delta=0):
        if self.dry:
            return
        e = eng.e if isinstance(eng, E) else eng
        v = self.T[key] + delta
        if v > 0:
            e.wait_ge(sem.h, v)

    def wv(self, eng, sem, val):
        if self.dry or val <= 0:
            return
        e = eng.e if isinstance(eng, E) else eng
        e.wait_ge(sem.h, val)


class KB:
    pass


def _alloc(kb, stack):
    nc = kb.nc
    sb = lambda name, shape, dt: stack.enter_context(nc.sbuf_tensor(name, shape, dt))
    ps = lambda name, shape, dt: stack.enter_context(nc.psum_tensor(name, shape, dt))
    sem = lambda name: Sem(stack.enter_context(nc.semaphore(name)))

    kb.xn = sb("xn", [128, NK * D], BF16)
    kb.AR = sb("AR", [128, 57344], BF16)
    kb.ARf = kb.AR.bitcast(F32)
    kb.stage = sb("stage", [128, 4096], F32)
    kb.ones_col = sb("ones_col", [128, 1], BF16)
    kb.ones_row = sb("ones_row", [1, 512], BF16)
    kb.mones_row = sb("mones_row", [1, 128], BF16)
    kb.onesf_row = sb("onesf_row", [1, 128], F32)
    kb.ident = sb("ident", [128, 128], BF16)
    kb.masks = sb("masks", [128, 4 * TW], BF16)
    kb.gs_us = sb("gs_us", [128, 2 * IC0], F32)
    kb.stats = sb("stats", [128, 16], F32)
    kb.consts = sb("consts", [128, 4], F32)
    kb.ds_row = sb("ds_row", [1, D], F32)
    kb.router_b = sb("router_b", [128, NK * 8], BF16)
    kb.moe_s = sb("moe_s", [128, 192], F32)
    kb.iota_p = sb("iota_p", [128, 4], F32)

    kb.pA = ps("pA", [128, TW], F32)
    kb.pB = ps("pB", [128, TW], F32)
    kb.pC = ps("pC", [128, TW], F32)
    kb.pD = ps("pD", [128, TW], F32)
    kb.pE = ps("pE", [128, TW], F32)
    kb.pF = ps("pF", [128, TW], F32)
    kb.pR = ps("pR", [128, TW], F32)
    kb.pT = ps("pT", [128, TW], F32)
    kb.pTb = kb.pT.bitcast(BF16)

    kb.sIN = [sem("sIN0"), sem("sIN1")]
    kb.sWL = [sem("sWL0"), sem("sWL1")]
    kb.sSHK = sem("sSHK")
    kb.sSHQ = sem("sSHQ")
    kb.sSHC = sem("sSHC")
    kb.sMM = sem("sMM")
    kb.sAC = sem("sAC")
    kb.sDV = sem("sDV")
    kb.sPL = sem("sPL")
    kb.sAD = [sem("sAD0"), sem("sAD1")]
    kb.sXS = sem("sXS")
    kb.all_sems = ([kb.sIN[0], kb.sIN[1], kb.sWL[0], kb.sWL[1], kb.sSHK,
                    kb.sSHQ, kb.sSHC, kb.sMM, kb.sAC, kb.sDV, kb.sPL,
                    kb.sAD[0], kb.sAD[1], kb.sXS])


LN_XNT = 0
LN_SQD = 2048
LN_IOT = 8192

AT_CTX = 0
AT_KG = 16384
AT_VG = 24576
AT_Q = 28672
AT_PT = 34816
AT_KT = 35840
AT_W = 36864
AT_NEGB = 40960
AT_SUM = 45056
AT_RS = 46080
AT_RSB = 47104
AT_BIAF = 48128
AT_BIAB = 49664
AT_BOB = 50432
AT_EV = 52480

MO_YACC = 0
MO_H2 = 32768
MO_WFC = 49152
MO_WPJ = 53248
MO_TMP = 55296

FF_HDN = 0
FF_W = 45056
FF_DS = 53248
FF_SIL = 54272
FF_TMP = 55296


def host_prep(inputs):
    asc = np.ascontiguousarray

    def lhsT_blocks(wt):
        Fdim, O = wt.shape
        a = wt.reshape(Fdim // 128, 128, O // 128, 128)
        return asc(a.transpose(2, 1, 0, 3).reshape(O // 128, 128, Fdim))

    def rhs_blocks(w):
        K, O = w.shape
        a = w.reshape(K // 128, 128, O // 512, 512)
        return asc(a.transpose(2, 0, 1, 3))

    ins = {}
    ins["l0_wq"] = lhsT_blocks(asc(inputs["L0_attn_Wq"].T))
    ins["l0_wk"] = lhsT_blocks(asc(inputs["L0_attn_Wk"].T))
    ins["l0_wv"] = asc(inputs["L0_attn_Wv"].T)
    ins["l0_wo"] = rhs_blocks(asc(inputs["L0_attn_Wo"].T))
    ins["l0_wg"] = lhsT_blocks(asc(inputs["L0_ffn_gate_w"].T))
    ins["l0_wu"] = lhsT_blocks(asc(inputs["L0_ffn_up_w"].T))
    ins["l0_wd"] = rhs_blocks(asc(inputs["L0_ffn_down_w"].T))
    ins["l0_gs"] = asc(inputs["L0_ffn_gate_s"].reshape(IC0, 128).T)
    ins["l0_us"] = asc(inputs["L0_ffn_up_s"].reshape(IC0, 128).T)
    ins["l0_ds"] = asc(inputs["L0_ffn_down_s"].reshape(1, D))

    wqkv = inputs["L1_attn_Wqkv"].reshape(D, 3, 16, 128)
    groups = []
    for g in range(8):
        h0, h1 = 2 * g, 2 * g + 1
        blk = np.concatenate(
            [wqkv[:, 0, h0], wqkv[:, 0, h1], wqkv[:, 1, h0], wqkv[:, 1, h1],
             wqkv[:, 2, h0], wqkv[:, 2, h1]], axis=1)
        groups.append(lhsT_blocks(asc(blk)))
    ins["l1_wqkv"] = asc(np.stack(groups))
    bq = inputs["L1_attn_bqkv"].reshape(3, 16, 128)
    ins["l1_bqkv"] = asc(np.stack(
        [np.concatenate([bq[0, 2 * g], bq[0, 2 * g + 1], bq[1, 2 * g],
                         bq[1, 2 * g + 1], bq[2, 2 * g], bq[2, 2 * g + 1]])
         for g in range(8)]))
    ins["l1_wo"] = rhs_blocks(asc(inputs["L1_attn_Wo"]))
    ins["l1_bo"] = asc(inputs["L1_attn_bo"].reshape(1, D))
    ins["l1_router"] = asc(inputs["L1_ffn_w_router"].reshape(NK, 128, 8)
                           .transpose(1, 0, 2).reshape(128, NK * 8))
    ins["l1_fc"] = asc(np.stack([lhsT_blocks(inputs["L1_ffn_w_fc"][e])
                                 for e in range(8)]))
    ins["l1_pj"] = asc(np.stack([rhs_blocks(inputs["L1_ffn_w_proj"][e])
                                 for e in range(8)]))
    return ins


def declare_dram(kb):
    nc = kb.nc
    dr = lambda name, shape, kind="ExternalInput": nc.dram_tensor(
        name, list(shape), F32, kind=kind).ap()
    kb.d = {
        "x0": dr("x0", (S, D)),
        "l0_wq": dr("l0_wq", (16, 128, D)),
        "l0_wk": dr("l0_wk", (4, 128, D)),
        "l0_wv": dr("l0_wv", (D, 512)),
        "l0_wo": dr("l0_wo", (4, NK, 128, 512)),
        "l0_wg": dr("l0_wg", (IC0, 128, D)),
        "l0_wu": dr("l0_wu", (IC0, 128, D)),
        "l0_wd": dr("l0_wd", (4, IC0, 128, 512)),
        "l0_gs": dr("l0_gs", (128, IC0)),
        "l0_us": dr("l0_us", (128, IC0)),
        "l0_ds": dr("l0_ds", (1, D)),
        "l1_wqkv": dr("l1_wqkv", (8, 6, 128, D)),
        "l1_bqkv": dr("l1_bqkv", (8, 768)),
        "l1_wo": dr("l1_wo", (4, NK, 128, 512)),
        "l1_bo": dr("l1_bo", (1, D)),
        "l1_router": dr("l1_router", (128, NK * 8)),
        "l1_fc": dr("l1_fc", (8, 16, 128, D)),
        "l1_pj": dr("l1_pj", (8, 4, 16, 128, 512)),
        "xs": dr("xs", (S, D), kind="Internal"),
        "out": dr("out", (S, D), kind="ExternalOutput"),
    }




def phase_setup(kb, em):
    iotaf = kb.ARf[:, LN_IOT // 2:LN_IOT // 2 + TW]
    with em.block("setup"):
        def gp(g_raw):
            g = E(g_raw, kb.sPL)
            kb.sXS.inc(g_raw.dma_start(out=kb.d["xs"], in_=kb.d["x0"]), 16)
            g.memset(kb.ones_col[:], 1.0)
            g.memset(kb.ones_row[:], 1.0)
            g.memset(kb.mones_row[:], -1.0)
            g.memset(kb.onesf_row[:], 1.0)
            g.memset(kb.consts[:, 0:1], EPS)
            g.iota(iotaf, [[1, TW]], channel_multiplier=0,
                   allow_small_or_imprecise_dtypes=True)
            g.iota(kb.iota_p[:, 0:1], [[0, 1]], channel_multiplier=1,
                   allow_small_or_imprecise_dtypes=True)
            em.rec("setup.pl", kb.sPL)
        em.on("gpsimd", gp)

        def ve(v_raw):
            v = E(v_raw, kb.sDV)
            em.w(v_raw, kb.sPL, "setup.pl")
            for m in range(1, 4):
                v.tensor_scalar_add(kb.iota_p[:, m:m + 1], kb.iota_p[:, 0:1],
                                    float(128 * m))
            em.rec("setup.iop", kb.sDV)
            em.w(v_raw, kb.sDV, "setup.iop")
            v.tensor_scalar(kb.ident[:], iotaf[:, 0:128], kb.iota_p[:, 0:1],
                            None, ALU.is_equal)
            for m in range(4):
                v.tensor_scalar(kb.masks[:, m * TW:(m + 1) * TW], iotaf,
                                kb.iota_p[:, m:m + 1], None, ALU.is_ge)
            em.rec("setup.dv", kb.sDV)
        em.on("vector", ve)

        def sy(e):
            kb.sIN[0].inc(e.dma_start(out=kb.gs_us[:, 0:IC0], in_=kb.d["l0_gs"]), 16)
            kb.sIN[0].inc(e.dma_start(out=kb.gs_us[:, IC0:2 * IC0],
                                      in_=kb.d["l0_us"]), 16)
            kb.sIN[0].inc(e.dma_start(out=kb.stage[:, 0:NK * 8],
                                      in_=kb.d["l1_router"]), 16)
            kb.sIN[0].inc(e.dma_start(out=kb.ds_row[:], in_=kb.d["l0_ds"]), 16)
            em.rec("setup.in", kb.sIN[0])
        em.on("sync", sy)

        def gp2(g_raw):
            g = E(g_raw, kb.sPL)
            em.w(g_raw, kb.sIN[0], "setup.in")
            g.tensor_copy(kb.router_b[:], kb.stage[:, 0:NK * 8])
        em.on("gpsimd", gp2)


def phase_ln(kb, em, src, tag, wait_ad=False, copy_out=False):
    st = kb.stats
    P = tag
    with em.block(f"ln_{tag}"):
        if wait_ad:
            em.rec((P, "adpre0"), kb.sAD[0])
            em.rec((P, "adpre1"), kb.sAD[1])

        def sy(e):
            for t in range(NT128):
                par = t % 2
                if t == 0 and wait_ad:
                    em.w(e, kb.sAD[0], (P, "adpre0"))
                    em.w(e, kb.sAD[1], (P, "adpre1"))
                if t >= 2:
                    em.w(e, kb.sDV, (P, "affine", t - 2))
                    em.w(e, kb.sAC, (P, "sq", t - 2))
                kb.sIN[par].inc(e.dma_start(
                    out=kb.stage[:, par * D:(par + 1) * D],
                    in_=src[t * 128:(t + 1) * 128, :]), 16)
                em.rec((P, "load", t), kb.sIN[par])
        em.on("sync", sy)

        if copy_out:
            def gpc(g_raw):
                em.w(g_raw, kb.sAD[0], (P, "adpre0"))
                em.w(g_raw, kb.sAD[1], (P, "adpre1"))
                kb.sXS.inc(g_raw.dma_start(out=kb.d["out"], in_=kb.d["xs"]), 16)
            em.on("gpsimd", gpc)

        def acf(a_raw):
            a = E(a_raw, kb.sAC)
            for t in range(NT128):
                par = t % 2
                em.w(a, kb.sIN[par], (P, "load", t))
                if t >= 1:
                    em.w(a, kb.sDV, (P, "var", t - 1))
                a.activation(kb.AR[:, LN_SQD:LN_SQD + D],
                             kb.stage[:, par * D:(par + 1) * D],
                             AF.Square, accum_out=st[:, 1:2])
                em.rec((P, "sq", t), kb.sAC)
                em.w(a, kb.sDV, (P, "var", t))
                a.activation(st[:, 4:5], st[:, 3:4], AF.Sqrt,
                             bias=kb.consts[:, 0:1])
                em.rec((P, "sqrt", t), kb.sAC)
        em.on("scalar", acf)

        def ve(v_raw):
            v = E(v_raw, kb.sDV)
            for t in range(NT128):
                par = t % 2
                em.w(v, kb.sIN[par], (P, "load", t))
                if t >= 1:
                    em.w(v, kb.sAC, (P, "sqrt", t - 1))
                v.reduce_sum(st[:, 0:1], kb.stage[:, par * D:(par + 1) * D], AX.X)
                em.rec((P, "rsum", t), kb.sDV)
                em.w(v, kb.sDV, (P, "rsum", t))
                v.tensor_scalar_mul(st[:, 2:3], st[:, 0:1], 1.0 / D)
                em.w(v, kb.sAC, (P, "sq", t))
                v.tensor_scalar_mul(st[:, 3:4], st[:, 1:2], 1.0 / D)
                em.rec((P, "m", t), kb.sDV)
                em.w(v, kb.sDV, (P, "m", t))
                v.tensor_mul(st[:, 5:6], st[:, 2:3], st[:, 2:3])
                em.rec((P, "m2", t), kb.sDV)
                em.w(v, kb.sDV, (P, "m2", t))
                v.tensor_sub(st[:, 3:4], st[:, 3:4], st[:, 5:6])
                em.rec((P, "var", t), kb.sDV)
                em.w(v, kb.sAC, (P, "sqrt", t))
                v.reciprocal(st[:, 6:7], st[:, 4:5])
                em.rec((P, "recip", t), kb.sDV)
                em.w(v, kb.sDV, (P, "recip", t))
                if t >= 1:
                    em.w(v, kb.sMM, (P, "tr", t - 1, 3))
                v.tensor_scalar(
                    kb.AR[:, LN_XNT:LN_XNT + D],
                    kb.stage[:, par * D:(par + 1) * D],
                    st[:, 2:3], st[:, 6:7], ALU.subtract, ALU.mult)
                em.rec((P, "affine", t), kb.sDV)
                for b in range(4):
                    em.w(v, kb.sMM, (P, "tr", t, b))
                    v.tensor_copy(
                        bass.AP(kb.xn, (4 * b) * D + t * 128,
                                [[NK * D, 128], [D, 4], [1, 128]]),
                        kb.pTb[:, 0:TW])
                    em.rec((P, "evac", t, b), kb.sDV)
        em.on("vector", ve)

        def te(t_eng):
            for t in range(NT128):
                em.w(t_eng, kb.sDV, (P, "affine", t))
                for b in range(4):
                    gb = 4 * t + b
                    if gb >= 1:
                        pt, pb = divmod(gb - 1, 4)
                        em.w(t_eng, kb.sDV, (P, "evac", pt, pb))
                    ins = None
                    for j in range(4):
                        ins = t_eng.transpose(
                            kb.pTb[:, j * 128:(j + 1) * 128],
                            kb.AR[:, LN_XNT + (4 * b + j) * 128:
                                  LN_XNT + (4 * b + j + 1) * 128],
                            kb.ident[:])
                    kb.sMM.inc(ins)
                    em.rec((P, "tr", t, b), kb.sMM)
        em.on("tensor", te)


def phase_ffn0(kb, em):
    ARf = kb.ARf
    sil_f = ARf[:, FF_SIL // 2:FF_SIL // 2 + TW]
    ds_f = ARf[:, FF_DS // 2:FF_DS // 2 + TW]
    tmp_f = [ARf[:, FF_TMP // 2 + i * TW:FF_TMP // 2 + (i + 1) * TW]
             for i in range(2)]
    wslot = [kb.AR[:, FF_W + i * D:FF_W + (i + 1) * D] for i in range(4)]
    NSL = 11

    P = "f0"
    with em.block("ffn0"):
        def sy(e):
            for hf in range(2):
                for ob in range(IC0):
                    j = hf * IC0 + ob
                    if ob >= 1:
                        em.w(e, kb.sPL, (P, "castA", hf, ob - 1, 1))
                    elif hf == 1:
                        em.w(e, kb.sPL, (P, "castB", 0, 3, 1, NSL - 1))
                    kb.sWL[0].inc(e.dma_start(
                        out=kb.stage[:, 0:D], in_=kb.d["l0_wg"][ob]), 16)
                    em.rec((P, "loadA", hf, ob, 0), kb.sWL[0])
                    kb.sWL[1].inc(e.dma_start(
                        out=kb.stage[:, D:2 * D], in_=kb.d["l0_wu"][ob]), 16)
                    em.rec((P, "loadA", hf, ob, 1), kb.sWL[1])
                for o_sl in range(4):
                    for tg in range(2):
                        for sl in range(NSL):
                            gi = (o_sl * 2 + tg) * NSL + sl
                            par = gi % 2
                            if gi == 0:
                                em.w(e, kb.sPL, (P, "castA", hf, IC0 - 1, 1))
                                em.w(e, kb.sPL, (P, "castA", hf, IC0 - 2, 1))
                            elif gi == 1:
                                pass
                            else:
                                po, pt_, psl = _b_unit(gi - 2, NSL)
                                em.w(e, kb.sPL, (P, "castB", hf, po, pt_, psl))
                            for c in range(4):
                                kb.sWL[par].inc(e.dma_start(
                                    out=kb.stage[:, par * D + c * TW:
                                                 par * D + (c + 1) * TW],
                                    in_=kb.d["l0_wd"][o_sl, sl * 4 + c]), 16)
                            em.rec((P, "loadB", hf, o_sl, tg, sl), kb.sWL[par])
        em.on("sync", sy)

        def gp(g_raw):
            g = E(g_raw, kb.sPL)
            for hf in range(2):
                for ob in range(IC0):
                    j = hf * IC0 + ob
                    par2 = 2 * (j % 2)
                    em.w(g, kb.sWL[0], (P, "loadA", hf, ob, 0))
                    if j >= 2:
                        phf, pob = divmod(j - 2, IC0)
                        em.w(g, kb.sMM, (P, "mmA", phf, pob, 3))
                    if hf == 1 and ob < 2:
                        em.w(g, kb.sMM, (P, "mmB", 0, 3, 1, NSL - 1, 3))
                    g.tensor_copy(wslot[par2][:], kb.stage[:, 0:D])
                    em.rec((P, "castA", hf, ob, 0), kb.sPL)
                    em.w(g, kb.sWL[1], (P, "loadA", hf, ob, 1))
                    g.tensor_copy(wslot[par2 + 1][:], kb.stage[:, D:2 * D])
                    em.rec((P, "castA", hf, ob, 1), kb.sPL)
                for o_sl in range(4):
                    for tg in range(2):
                        for sl in range(NSL):
                            gi = (o_sl * 2 + tg) * NSL + sl
                            par = gi % 2
                            em.w(g, kb.sWL[par], (P, "loadB", hf, o_sl, tg, sl))
                            if gi >= 2:
                                po, pt_, psl = _b_unit(gi - 2, NSL)
                                em.w(g, kb.sMM, (P, "mmB", hf, po, pt_, psl, 3))
                            else:
                                em.w(g, kb.sMM, (P, "mmA", hf, IC0 - 2 + gi, 3))
                            g.tensor_copy(wslot[par][:],
                                          kb.stage[:, par * D:(par + 1) * D])
                            em.rec((P, "castB", hf, o_sl, tg, sl), kb.sPL)
                        for tl in range(4):
                            kslot = kb._ad_idx % 2
                            kb._ad_idx += 1
                            em.w(g_raw, kb.sDV, (P, "ev", hf, o_sl, tg, tl))
                            if hf == 0 and o_sl == 0 and tg == 0 and tl == 0:
                                g_raw.wait_ge(kb.sXS.h, 16)
                            t128 = hf * 8 + tg * 4 + tl
                            kb.sAD[kslot].inc(g_raw.dma_start(
                                out=kb.d["xs"][t128 * 128:(t128 + 1) * 128,
                                               o_sl * TW:(o_sl + 1) * TW],
                                in_=tmp_f[kslot], accum_op=ALU.add), 16)
                            em.rec((P, "ad", hf, o_sl, tg, tl), kb.sAD[kslot])
        em.on("gpsimd", gp)

        def te(t_eng):
            pss = [kb.pA, kb.pB, kb.pC, kb.pD]
            for hf in range(2):
                for ob in range(IC0):
                    j = hf * IC0 + ob
                    par2 = 2 * (j % 2)
                    em.w(t_eng, kb.sPL, (P, "castA", hf, ob, 1))
                    for grp in range(4):
                        wt = wslot[par2 + (0 if grp < 2 else 1)]
                        tw = grp % 2
                        ps = pss[grp]
                        if j >= 1:
                            if grp < 2:
                                em.w(t_eng, kb.sAC, (P, "silu", hf, ob - 1, tw)
                                     if ob >= 1 else (P, "silu", hf - 1, IC0 - 1, tw))
                            else:
                                em.w(t_eng, kb.sDV, (P, "stt", hf, ob - 1, tw)
                                     if ob >= 1 else (P, "stt", hf - 1, IC0 - 1, tw))
                        if hf == 1 and ob < 2:
                            em.w(t_eng, kb.sDV, (P, "ev", 0, 3, 1, grp))
                        ins = None
                        for k in range(NK):
                            ins = t_eng.matmul(
                                ps[:], wt[:, k * 128:(k + 1) * 128],
                                kb.xn[:, k * D + hf * 1024 + tw * TW:
                                      k * D + hf * 1024 + (tw + 1) * TW],
                                start=(k == 0), stop=(k == NK - 1))
                        kb.sMM.inc(ins)
                        em.rec((P, "mmA", hf, ob, grp), kb.sMM)
                for o_sl in range(4):
                    if o_sl > 0 or hf > 0:
                        em.w(t_eng, kb.sDV,
                             (P, "dscp", hf, o_sl - 1) if o_sl > 0
                             else (P, "dscp", hf - 1, 3))
                    kb.sMM.inc(t_eng.matmul(
                        kb.pR[:], kb.onesf_row[:],
                        kb.ds_row[0:1, o_sl * TW:(o_sl + 1) * TW],
                        start=True, stop=True))
                    em.rec((P, "dsmm", hf, o_sl), kb.sMM)
                    for tg in range(2):
                        for sl in range(NSL):
                            gi = (o_sl * 2 + tg) * NSL + sl
                            par = gi % 2
                            em.w(t_eng, kb.sPL, (P, "castB", hf, o_sl, tg, sl))
                            for tl in range(4):
                                ps = pss[tl]
                                if sl == 0 and (tg > 0 or o_sl > 0):
                                    prev = (hf, o_sl, tg - 1, tl) if tg > 0 \
                                        else (hf, o_sl - 1, 1, tl)
                                    em.w(t_eng, kb.sDV, (P, "ev", *prev))
                                elif sl == 0 and o_sl == 0 and tg == 0:
                                    if tl >= 2:
                                        em.w(t_eng, kb.sDV,
                                             (P, "stt", hf, IC0 - 1, tl - 2))
                                    else:
                                        em.w(t_eng, kb.sAC,
                                             (P, "silu", hf, IC0 - 1, tl))
                                    if hf == 1:
                                        em.w(t_eng, kb.sDV, (P, "ev", 0, 3, 1, tl))
                                ins = None
                                for c in range(4):
                                    ic = sl * 4 + c
                                    ins = t_eng.matmul(
                                        ps[:],
                                        kb.AR[:, FF_HDN + ic * 1024 + (tg * 4 + tl) * 128:
                                              FF_HDN + ic * 1024 + (tg * 4 + tl + 1) * 128],
                                        wslot[par][:, c * TW:(c + 1) * TW],
                                        start=(ic == 0), stop=(ic == IC0 - 1))
                                kb.sMM.inc(ins)
                                em.rec((P, "mmB", hf, o_sl, tg, sl, tl), kb.sMM)
        em.on("tensor", te)

        def acf(a_raw):
            a = E(a_raw, kb.sAC)
            for hf in range(2):
                for ob in range(IC0):
                    for tw in range(2):
                        em.w(a, kb.sMM, (P, "mmA", hf, ob, tw))
                        if ob > 0 or tw > 0 or hf > 0:
                            pj = (hf, ob, tw - 1) if tw > 0 else (
                                (hf, ob - 1, 1) if ob > 0 else (hf - 1, IC0 - 1, 1))
                            em.w(a, kb.sDV, (P, "stt", *pj))
                        a.activation(sil_f, [kb.pA, kb.pB][tw][:], AF.Silu,
                                     scale=kb.gs_us[:, ob:ob + 1])
                        em.rec((P, "silu", hf, ob, tw), kb.sAC)
        em.on("scalar", acf)

        def ve(v_raw):
            v = E(v_raw, kb.sDV)
            for hf in range(2):
                for ob in range(IC0):
                    for tw in range(2):
                        em.w(v, kb.sMM, (P, "mmA", hf, ob, 2 + tw))
                        em.w(v, kb.sAC, (P, "silu", hf, ob, tw))
                        v.scalar_tensor_tensor(
                            kb.AR[:, FF_HDN + ob * 1024 + tw * TW:
                                  FF_HDN + ob * 1024 + (tw + 1) * TW],
                            [kb.pC, kb.pD][tw][:],
                            kb.gs_us[:, IC0 + ob:IC0 + ob + 1],
                            sil_f, ALU.mult, ALU.mult)
                        em.rec((P, "stt", hf, ob, tw), kb.sDV)
                for o_sl in range(4):
                    em.w(v, kb.sMM, (P, "dsmm", hf, o_sl))
                    if o_sl > 0 or hf > 0:
                        pk = (hf, o_sl - 1, 1, 3) if o_sl > 0 else (hf - 1, 3, 1, 3)
                        em.w(v, kb.sDV, (P, "ev", *pk))
                    v.tensor_copy(ds_f, kb.pR[:])
                    em.rec((P, "dscp", hf, o_sl), kb.sDV)
                    for tg in range(2):
                        for tl in range(4):
                            kidx = ((hf * 4 + o_sl) * 2 + tg) * 4 + tl
                            kslot = (kb._ad_base + kidx) % 2
                            em.w(v, kb.sDV, (P, "dscp", hf, o_sl))
                            em.w(v, kb.sMM, (P, "mmB", hf, o_sl, tg, NSL - 1, tl))
                            if kidx >= 2:
                                pk = kidx - 2
                                pi = _ev_unit(pk)
                                em.w(v, kb.sAD[kslot], (P, "ad", *pi))
                            v.tensor_tensor(tmp_f[kslot],
                                            [kb.pA, kb.pB, kb.pC, kb.pD][tl][:],
                                            ds_f, ALU.mult)
                            em.rec((P, "ev", hf, o_sl, tg, tl), kb.sDV)
        em.on("vector", ve)


def _b_unit(gi, NSL):
    o_sl, r = divmod(gi, 2 * NSL)
    tg, sl = divmod(r, NSL)
    return o_sl, tg, sl


def _ev_unit(k):
    hf, r = divmod(k, 32)
    o_sl, r2 = divmod(r, 8)
    tg, tl = divmod(r2, 4)
    return hf, o_sl, tg, tl




def _head_pe(kb, em, t_eng, P, h, cfg):
    AR = kb.AR
    si = 0
    for qt in range(4):
        ctx_ps = [kb.pC, kb.pD][qt % 2]
        sum_ps = [kb.pE, kb.pF][qt % 2]
        nk = 4 * qt + 4
        for kc in range(nk):
            s_ps = [kb.pA, kb.pB][si % 2]
            if si >= 2:
                em.w(t_eng, kb.sAC, (P, "exp", h, si - 2))
            else:
                em.w(t_eng, kb.sAC, cfg["sbank_free"][si])
            if cfg["odd_shift"] and si == 0:
                em.w(t_eng, kb.sSHQ, (P, "qsh", h // 2, 3))
            if si == 0 and cfg.get("ksh_dep"):
                em.w(t_eng, kb.sSHK, cfg["ksh_dep"])
            ins = t_eng.matmul(s_ps[:], cfg["k_ap"](kc), cfg["q_ap"](qt),
                               start=True, stop=False)
            ins = t_eng.matmul(s_ps[:], kb.mones_row[0:1, 0:128],
                               cfg["cpos"][0:1, qt * TW:(qt + 1) * TW],
                               start=False, stop=True)
            kb.sMM.inc(ins)
            em.rec((P, "smm", h, si), kb.sMM)
            if kc >= 4 * qt:
                em.w(t_eng, kb.sPL, (P, "mask", h, si))
            else:
                em.w(t_eng, kb.sAC, (P, "exp", h, si))
            if kc == 0:
                if (h, qt) != cfg["first_hqt"]:
                    pq = (h, qt - 2) if qt >= 2 else (h - 1, qt + 2)
                    if pq[0] >= 0:
                        em.w(t_eng, kb.sDV, (P, "ctxev", *pq))
                em.w(t_eng, kb.sAC, cfg["sum_free"](h, qt))
            dh = cfg["dh"]
            ins = t_eng.matmul(ctx_ps[0:dh, 0:TW], cfg["v_ap"](kc),
                               kb.AR[:, AT_PT + (si % 2) * TW:
                                     AT_PT + (si % 2 + 1) * TW],
                               start=(kc == 0), stop=(kc == nk - 1))
            ins = t_eng.matmul(sum_ps[0:1, 0:TW], kb.ones_col[:, 0:1],
                               kb.AR[:, AT_PT + (si % 2) * TW:
                                     AT_PT + (si % 2 + 1) * TW],
                               start=(kc == 0), stop=(kc == nk - 1))
            kb.sMM.inc(ins)
            em.rec((P, "cs", h, si), kb.sMM)
            si += 1
        em.w(t_eng, kb.sDV, (P, "rs", h, qt))
        if (h, qt) != cfg["first_hqt"]:
            pq = (h, qt - 1) if qt >= 1 else (h - 1, 3)
            em.w(t_eng, kb.sDV, (P, "rsbcp", *pq))
        kb.sMM.inc(t_eng.matmul(
            kb.pR[:], kb.onesf_row[0:1, 0:128],
            kb.ARf[0:1, AT_RS // 2:AT_RS // 2 + TW],
            start=True, stop=True))
        em.rec((P, "rsmm", h, qt), kb.sMM)


def _head_act(kb, em, a, P, h, cfg):
    def emit_csh(qt):
        em.w(a, kb.sDV, (P, "ctxev", h, qt))
        kb.sSHC.inc(cfg["a_raw"].dma_start(
            out=cfg["csh_dst"](h, qt), in_=cfg["shtmp"][0:64, 0:TW]), 16)
        em.rec((P, "csh", h, qt), kb.sSHC)

    si = 0
    for qt in range(4):
        if cfg.get("csh_dst") and qt >= 1:
            emit_csh(qt - 1)
        sum_ps = [kb.pE, kb.pF][qt % 2]
        nk = 4 * qt + 4
        for kc in range(nk):
            em.w(a, kb.sMM, (P, "smm", h, si))
            if si >= 2:
                em.w(a, kb.sMM, (P, "cs", h, si - 2))
            elif h > cfg["first_h"]:
                em.w(a, kb.sMM, (P, "cs", h - 1, 38 + si))
            a.activation(kb.AR[:, AT_PT + (si % 2) * TW:
                               AT_PT + (si % 2 + 1) * TW],
                         [kb.pA, kb.pB][si % 2][:], AF.Exp)
            em.rec((P, "exp", h, si), kb.sAC)
            si += 1
        em.w(a, kb.sMM, (P, "cs", h, si - 1))
        if (h, qt) != cfg["first_hqt"]:
            pq = (h, qt - 1) if qt >= 1 else (h - 1, 3)
            em.w(a, kb.sDV, (P, "rs", *pq))
        a.activation(kb.ARf[0:1, AT_SUM // 2:AT_SUM // 2 + TW],
                     sum_ps[0:1, 0:TW], AF.Copy)
        em.rec((P, "sumev", h, qt), kb.sAC)
        if cfg.get("csh_dst") and qt == 3:
            emit_csh(3)


def _head_pool(kb, em, g, P, h):
    si = 0
    for qt in range(4):
        for kc in range(4 * qt + 4):
            if kc >= 4 * qt:
                m = kc - 4 * qt
                em.w(g, kb.sAC, (P, "exp", h, si))
                pt = kb.AR[:, AT_PT + (si % 2) * TW:AT_PT + (si % 2 + 1) * TW]
                g.tensor_tensor(pt, pt,
                                kb.masks[:, m * TW:(m + 1) * TW], ALU.mult)
                em.rec((P, "mask", h, si), kb.sPL)
            si += 1


def _head_dve(kb, em, v, P, h, cfg):
    for qt in range(4):
        ctx_ps = [kb.pC, kb.pD][qt % 2]
        em.w(v, kb.sAC, (P, "sumev", h, qt))
        if (h, qt) != cfg["first_hqt"]:
            pq = (h, qt - 1) if qt >= 1 else (h - 1, 3)
            em.w(v, kb.sMM, (P, "rsmm", *pq))
        v.reciprocal(kb.ARf[0:1, AT_RS // 2:AT_RS // 2 + TW],
                     kb.ARf[0:1, AT_SUM // 2:AT_SUM // 2 + TW])
        em.rec((P, "rs", h, qt), kb.sDV)
        em.w(v, kb.sMM, (P, "rsmm", h, qt))
        if (h, qt) != cfg["first_hqt"]:
            pq = (h, qt - 1) if qt >= 1 else (h - 1, 3)
            em.w(v, kb.sDV, (P, "ctxev", *pq))
        v.tensor_copy(kb.ARf[:, AT_RSB // 2:AT_RSB // 2 + TW], kb.pR[:])
        em.rec((P, "rsbcp", h, qt), kb.sDV)
        em.w(v, kb.sDV, (P, "rsbcp", h, qt))
        em.w(v, kb.sMM, (P, "cs", h, 4 * qt + 3 + sum(4 * q + 4 for q in range(qt))))
        dh = cfg["dh"]
        dst = cfg["ctx_dst"](h, qt)
        if dst is None:
            pv = cfg["shtmp_prev"](h, qt)
            if pv is not None:
                em.w(v, pv[0], pv[1])
            v.tensor_tensor(
                kb.AR[0:dh, AT_KT + TW:AT_KT + 2 * TW],
                ctx_ps[0:dh, 0:TW],
                kb.ARf[0:dh, AT_RSB // 2:AT_RSB // 2 + TW], ALU.mult)
        else:
            v.tensor_tensor(dst, ctx_ps[0:dh, 0:TW],
                            kb.ARf[0:dh, AT_RSB // 2:AT_RSB // 2 + TW],
                            ALU.mult)
        em.rec((P, "ctxev", h, qt), kb.sDV)


def phase_attn0(kb, em):
    AR, ARf = kb.AR, kb.ARf
    kbuf = [AR[:, AT_KG + i * D:AT_KG + (i + 1) * D] for i in range(4)]
    vw = AR[:, AT_W:AT_W + 4096]
    wslot = [AR[:, AT_W + i * D:AT_W + (i + 1) * D] for i in range(2)]
    qpair = [AR[:, AT_Q + i * D:AT_Q + (i + 1) * D] for i in range(2)]
    qodd = AR[:, AT_Q + 2 * D:AT_Q + 3 * D]
    ktmp = AR[:, AT_KT:AT_KT + TW]
    shtmp = AR[:, AT_KT + TW:AT_KT + 2 * TW]
    cpos = [AR[0:1, AT_NEGB:AT_NEGB + D], AR[0:1, AT_NEGB + D:AT_NEGB + 2 * D]]
    ev_f = [ARf[:, AT_EV // 2 + i * TW:AT_EV // 2 + (i + 1) * TW]
            for i in range(2)]
    st = kb.stats

    for hf in range(ATTN0_HALVES):
        P = ("a0", hf)
        cfg = {
            "dh": 64,
            "first_h": 0,
            "first_hqt": (0, 0),
        }
        with em.block(f"attn0_{hf}"):
            em.rec((P, "adpre0"), kb.sAD[0])
            em.rec((P, "adpre1"), kb.sAD[1])
            adbase = kb._ad_idx

            def sy(e):
                def load(par, c0, c1, src, key):
                    kb.sWL[par].inc(e.dma_start(
                        out=kb.stage[:, par * D + c0:par * D + c1], in_=src), 16)
                    em.rec(key, kb.sWL[par])

                for kp in range(2):
                    load(kp, 0, D, kb.d["l0_wk"][2 * hf + kp], (P, "wl", "k", kp))
                for c in range(16):
                    em.w(e, kb.sPL, (P, "wc", "k", c % 2) if c < 2
                         else (P, "wc", "v", c - 2))
                    load(c % 2, 0, 256,
                         kb.d["l0_wv"][c * 128:(c + 1) * 128,
                                       hf * 256:(hf + 1) * 256],
                         (P, "wl", "v", c))
                for p in range(8):
                    em.w(e, kb.sPL, (P, "wc", "v", 14 + (p % 2)) if p < 2
                         else (P, "wc", "q", p - 2))
                    load(p % 2, 0, D, kb.d["l0_wq"][8 * hf + p], (P, "wl", "q", p))
                for o_sl in range(4):
                    for sh in range(2):
                        em.w(e, kb.sPL, (P, "wc", "q", 6 + sh) if o_sl == 0
                             else (P, "wc", "wo", o_sl - 1, sh))
                        for c in range(4):
                            load(sh, c * TW, (c + 1) * TW,
                                 kb.d["l0_wo"][o_sl, hf * 8 + sh * 4 + c],
                                 (P, "wl", "wo", o_sl, sh, c))
            em.on("sync", sy)

            def gp(g_raw):
                g = E(g_raw, kb.sPL)
                for i in range(4):
                    g.memset(kbuf[i][64:128, :], 0.0)
                if hf == 0:
                    g.memset(qodd[64:128, :], 0.0)
                em.rec((P, "kz"), kb.sPL)
                for kp in range(2):
                    em.w(g, kb.sWL[kp], (P, "wl", "k", kp))
                    g.tensor_copy(wslot[kp][:], kb.stage[:, kp * D:(kp + 1) * D])
                    em.rec((P, "wc", "k", kp), kb.sPL)
                for c in range(16):
                    em.w(g, kb.sWL[c % 2], (P, "wl", "v", c))
                    em.w(g, kb.sMM, (P, "kmm", c // 8, 3))
                    g.tensor_copy(vw[:, c * 256:(c + 1) * 256],
                                  kb.stage[:, (c % 2) * D:(c % 2) * D + 256])
                    em.rec((P, "wc", "v", c), kb.sPL)
                for p in range(8):
                    em.w(g, kb.sWL[p % 2], (P, "wl", "q", p))
                    em.w(g, kb.sMM, (P, "vmm", 15) if p < 2
                         else (P, "qmm", p - 2, 3))
                    g.tensor_copy(wslot[p % 2][:],
                                  kb.stage[:, (p % 2) * D:(p % 2 + 1) * D])
                    em.rec((P, "wc", "q", p), kb.sPL)
                    for h in (2 * p, 2 * p + 1):
                        _head_pool(kb, em, g, P, h)
                for o_sl in range(4):
                    for sh in range(2):
                        em.w(g, kb.sWL[sh], (P, "wl", "wo", o_sl, sh, 3))
                        em.w(g, kb.sMM, (P, "qmm", 6 + sh, 3) if o_sl == 0
                             else (P, "womm", o_sl - 1, 15))
                        g.tensor_copy(wslot[sh][:],
                                      kb.stage[:, sh * D:(sh + 1) * D])
                        em.rec((P, "wc", "wo", o_sl, sh), kb.sPL)
                    for t in range(NT128):
                        kslot = kb._ad_idx % 2
                        kb._ad_idx += 1
                        em.w(g_raw, kb.sAC, (P, "woev", o_sl, t))
                        if o_sl == 0 and t == 0:
                            em.w(g_raw, kb.sAD[0], (P, "adpre0"))
                            em.w(g_raw, kb.sAD[1], (P, "adpre1"))
                        kb.sAD[kslot].inc(g_raw.dma_start(
                            out=kb.d["xs"][t * 128:(t + 1) * 128,
                                           o_sl * TW:(o_sl + 1) * TW],
                            in_=ev_f[kslot], accum_op=ALU.add), 16)
                        em.rec((P, "ad", o_sl, t), kb.sAD[kslot])
            em.on("gpsimd", gp)

            def te(t_eng):
                for kp in range(2):
                    em.w(t_eng, kb.sPL, (P, "wc", "k", kp))
                    em.w(t_eng, kb.sPL, (P, "kz"))
                    for tw in range(4):
                        ps = [kb.pA, kb.pB][tw % 2]
                        if tw >= 2:
                            em.w(t_eng, kb.sAC, (P, "ksq", kp, tw - 2, 1))
                        elif kp == 1:
                            em.w(t_eng, kb.sAC, (P, "ksq", 0, 2 + tw, 1))
                        ins = None
                        for k in range(NK):
                            ins = t_eng.matmul(
                                ps[:], wslot[kp][:, k * 128:(k + 1) * 128],
                                kb.xn[:, k * D + tw * TW:k * D + (tw + 1) * TW],
                                start=(k == 0), stop=(k == NK - 1))
                        kb.sMM.inc(ins)
                        em.rec((P, "kmm", kp, tw), kb.sMM)
                        for par in range(2):
                            em.w(t_eng, kb.sAC, (P, "ksq", kp, tw, par))
                            if (kp, tw) != (0, 0):
                                pv = (kp, tw - 1, par) if tw else (kp - 1, 3, par)
                                em.w(t_eng, kb.sDV, (P, "kred", *pv))
                            lo, hi = (0, 64) if par == 0 else (64, 128)
                            kb.sMM.inc(t_eng.matmul(
                                [kb.pE, kb.pF][par][0:1, 0:TW],
                                kb.ones_col[lo:hi, 0:1], ktmp[lo:hi, 0:TW],
                                start=True, stop=True))
                            em.rec((P, "knmm", kp, tw, par), kb.sMM)
                for tc in range(16):
                    if tc == 0:
                        em.w(t_eng, kb.sPL, (P, "wc", "v", 15))
                    ps = [kb.pA, kb.pB][tc % 2]
                    if tc >= 2:
                        em.w(t_eng, kb.sAC, (P, "vev", tc - 2))
                    else:
                        em.w(t_eng, kb.sAC, (P, "ksq", 1, 2 + tc, 1))
                    ins = None
                    for k in range(NK):
                        ins = t_eng.matmul(
                            ps[:, 0:256],
                            kb.xn[:, k * D + tc * 128:k * D + (tc + 1) * 128],
                            vw[:, k * 256:(k + 1) * 256],
                            start=(k == 0), stop=(k == NK - 1))
                    kb.sMM.inc(ins)
                    em.rec((P, "vmm", tc), kb.sMM)
                for p in range(8):
                    em.w(t_eng, kb.sPL, (P, "wc", "q", p))
                    for tw in range(4):
                        ps = [kb.pA, kb.pB][tw % 2]
                        if tw >= 2:
                            em.w(t_eng, kb.sAC, (P, "qsq", p, tw - 2, 1))
                        elif p == 0:
                            em.w(t_eng, kb.sAC, (P, "vev", 14 + tw))
                        else:
                            em.w(t_eng, kb.sAC, (P, "exp", 2 * p - 1, 38 + tw))
                        ins = None
                        for k in range(NK):
                            ins = t_eng.matmul(
                                ps[:], wslot[p % 2][:, k * 128:(k + 1) * 128],
                                kb.xn[:, k * D + tw * TW:k * D + (tw + 1) * TW],
                                start=(k == 0), stop=(k == NK - 1))
                        kb.sMM.inc(ins)
                        em.rec((P, "qmm", p, tw), kb.sMM)
                        for par in range(2):
                            em.w(t_eng, kb.sAC, (P, "qsq", p, tw, par))
                            if tw == 0:
                                em.w(t_eng, kb.sAC, (P, "cpos", p - 1, 3, par)
                                     if p else (P, "xk",))
                                if p == 0:
                                    em.w(t_eng, kb.sDV, (P, "kred", 1, 3, par))
                            else:
                                em.w(t_eng, kb.sAC, (P, "cpos", p, tw - 1, par))
                            if p >= 1:
                                em.w(t_eng, kb.sAC, (P, "sumev", 2 * p - 1, 2 + par))
                            lo, hi = (0, 64) if par == 0 else (64, 128)
                            kb.sMM.inc(t_eng.matmul(
                                [kb.pE, kb.pF][par][0:1, 0:TW],
                                kb.ones_col[lo:hi, 0:1], ktmp[lo:hi, 0:TW],
                                start=True, stop=True))
                            em.rec((P, "qnmm", p, tw, par), kb.sMM)
                    for h in (2 * p, 2 * p + 1):
                        kvl = p // 2
                        hcfg = dict(cfg)
                        hcfg["k_ap"] = (lambda kvl_: lambda kc: kbuf[kvl_][
                            :, kc * 128:(kc + 1) * 128])(kvl)
                        hcfg["ksh_dep"] = ((P, "ksh", kvl // 2, 3)
                                           if kvl % 2 else None)
                        hcfg["v_ap"] = (lambda kvl_: lambda kc: AR[
                            :, AT_VG + kc * 256 + kvl_ * 64:
                            AT_VG + kc * 256 + (kvl_ + 1) * 64])(kvl)
                        if h % 2 == 0:
                            hcfg["q_ap"] = (lambda p_: lambda qt: qpair[p_ % 2][
                                :, qt * TW:(qt + 1) * TW])(p)
                            hcfg["odd_shift"] = False
                        else:
                            hcfg["q_ap"] = lambda qt: qodd[:, qt * TW:(qt + 1) * TW]
                            hcfg["odd_shift"] = True
                        hcfg["cpos"] = cpos[h % 2]
                        hcfg["sbank_free"] = [
                            (P, "qsq", p, 3, 1), (P, "qev", p, 3)] if h % 2 == 0 \
                            else [(P, "exp", h - 1, 38), (P, "exp", h - 1, 39)]
                        hcfg["sum_free"] = (lambda p_: lambda hh, qt: (
                            (P, "cpos", p_, 3, qt % 2) if hh % 2 == 0
                            else (P, "sumev", hh - 1, 2 + qt % 2)))(p)
                        _head_pe(kb, em, t_eng, P, h, hcfg)
                for o_sl in range(4):
                    em.w(t_eng, kb.sPL, (P, "wc", "wo", o_sl, 1))
                    if o_sl == 0:
                        em.w(t_eng, kb.sDV, (P, "ctxev", 15, 3))
                        em.w(t_eng, kb.sSHC, (P, "csh", 15, 3))
                    for t in range(NT128):
                        ps = [kb.pA, kb.pB][t % 2]
                        if o_sl == 0 and t < 2:
                            em.w(t_eng, kb.sAC, (P, "exp", 15, 38 + t))
                        else:
                            pv = (o_sl, t - 2) if t >= 2 else (o_sl - 1, 14 + t)
                            em.w(t_eng, kb.sAC, (P, "woev", *pv))
                        ins = None
                        for c in range(8):
                            ins = t_eng.matmul(
                                ps[:],
                                AR[:, AT_CTX + c * D + t * 128:
                                   AT_CTX + c * D + (t + 1) * 128],
                                wslot[c // 4][:, (c % 4) * TW:(c % 4 + 1) * TW],
                                start=(c == 0), stop=(c == 7))
                        kb.sMM.inc(ins)
                        em.rec((P, "womm", o_sl, t), kb.sMM)
            em.on("tensor", te)

            def acf(a_raw):
                a = E(a_raw, kb.sAC)
                em.rec((P, "xk"), kb.sAC)
                for kp in range(2):
                    for tw in range(4):
                        ps = [kb.pA, kb.pB][tw % 2]
                        em.w(a, kb.sMM, (P, "kmm", kp, tw))
                        a.activation(kbuf[2 * kp][0:64, tw * TW:(tw + 1) * TW],
                                     ps[0:64, :], AF.Copy)
                        if (kp, tw) != (0, 0):
                            pv = (kp, tw - 1) if tw else (kp - 1, 3)
                            em.w(a, kb.sSHK, (P, "ksh", *pv))
                        elif hf == 1:
                            em.w(a, kb.sSHC, (("a0", 0), "csh", 15, 3))
                        a.activation(shtmp[64:128, 0:TW], ps[64:128, :], AF.Copy)
                        em.rec((P, "kodd", kp, tw), kb.sAC)
                        em.w(a, kb.sAC, (P, "kodd", kp, tw))
                        kb.sSHK.inc(a_raw.dma_start(
                            out=kbuf[2 * kp + 1][0:64, tw * TW:(tw + 1) * TW],
                            in_=shtmp[64:128, 0:TW]), 16)
                        em.rec((P, "ksh", kp, tw), kb.sSHK)
                        for par in range(2):
                            if (kp, tw, par) != (0, 0, 0):
                                pv = (kp, tw, 0) if par else (
                                    (kp, tw - 1, 1) if tw else (kp - 1, 3, 1))
                                em.w(a, kb.sMM, (P, "knmm", *pv))
                            lo, hi = (0, 64) if par == 0 else (64, 128)
                            a.activation(ktmp[lo:hi, 0:TW], ps[lo:hi, :],
                                         AF.Square)
                            em.rec((P, "ksq", kp, tw, par), kb.sAC)
                for tc in range(16):
                    ps = [kb.pA, kb.pB][tc % 2]
                    em.w(a, kb.sMM, (P, "vmm", tc))
                    a.activation(AR[:, AT_VG + tc * 256:AT_VG + (tc + 1) * 256],
                                 ps[:, 0:256], AF.Copy)
                    em.rec((P, "vev", tc), kb.sAC)
                for p in range(8):
                    for tw in range(4):
                        ps = [kb.pA, kb.pB][tw % 2]
                        em.w(a, kb.sMM, (P, "qmm", p, tw))
                        if p >= 2:
                            em.w(a, kb.sMM, (P, "smm", 2 * (p - 2) + 1, 39))
                            em.w(a, kb.sSHQ, (P, "qsh", p - 2, 3))
                        a.activation(qpair[p % 2][:, tw * TW:(tw + 1) * TW],
                                     ps[:], AF.Copy, scale=0.125)
                        em.rec((P, "qev", p, tw), kb.sAC)
                        if p >= 1 and tw == 0:
                            em.w(a, kb.sMM, (P, "smm", 2 * p - 1, 39))
                        em.w(a, kb.sAC, (P, "qev", p, tw))
                        kb.sSHQ.inc(a_raw.dma_start(
                            out=qodd[0:64, tw * TW:(tw + 1) * TW],
                            in_=qpair[p % 2][64:128, tw * TW:(tw + 1) * TW]), 16)
                        em.rec((P, "qsh", p, tw), kb.sSHQ)
                        for par in range(2):
                            pv = (P, "qnmm", p, tw - 1, par) if tw else \
                                (P, "qnmm", p - 1, 3, par) if p else \
                                (P, "knmm", 1, 3, par)
                            em.w(a, kb.sMM, pv)
                            lo, hi = (0, 64) if par == 0 else (64, 128)
                            a.activation(ktmp[lo:hi, 0:TW], ps[lo:hi, :],
                                         AF.Square, scale=0.125)
                            em.rec((P, "qsq", p, tw, par), kb.sAC)
                        kvl = p // 2
                        for par in range(2):
                            em.w(a, kb.sMM, (P, "qnmm", p, tw, par))
                            if tw == 0:
                                if p >= 1:
                                    em.w(a, kb.sMM, (P, "smm", 2 * p - 1, 39))
                                if p == 0:
                                    em.w(a, kb.sDV, (P, "mx", kvl))
                            a.activation(
                                cpos[par][0:1, tw * TW:(tw + 1) * TW],
                                [kb.pE, kb.pF][par][0:1, 0:TW], AF.Sqrt,
                                scale=st[0:1, 8 + kvl:9 + kvl])
                            em.rec((P, "cpos", p, tw, par), kb.sAC)
                    for h in (2 * p, 2 * p + 1):
                        hcfg = dict(cfg)
                        hcfg["first_hqt"] = (0, 0)
                        if h % 2 == 1:
                            hcfg["a_raw"] = a_raw
                            hcfg["shtmp"] = shtmp
                            hcfg["csh_dst"] = lambda hh, qt: AR[
                                64:128, AT_CTX + (hh // 2) * D + qt * TW:
                                AT_CTX + (hh // 2) * D + (qt + 1) * TW]
                        _head_act(kb, em, a, P, h, hcfg)
                for o_sl in range(4):
                    for t in range(NT128):
                        em.w(a, kb.sMM, (P, "womm", o_sl, t))
                        gi = o_sl * 16 + t
                        kslot = (adbase + gi) % 2
                        if gi >= 2:
                            em.w(a, kb.sAD[kslot], (P, "ad", *divmod(gi - 2, 16)))
                        else:
                            em.w(a, kb.sAD[0], (P, "adpre0"))
                            em.w(a, kb.sAD[1], (P, "adpre1"))
                        a.activation(ev_f[kslot], [kb.pA, kb.pB][t % 2][:],
                                     AF.Copy)
                        em.rec((P, "woev", o_sl, t), kb.sAC)
            em.on("scalar", acf)

            def ve(v_raw):
                v = E(v_raw, kb.sDV)
                for kp in range(2):
                    for tw in range(4):
                        for par in range(2):
                            kvl = 2 * kp + par
                            em.w(v, kb.sMM, (P, "knmm", kp, tw, par))
                            if (kp, tw, par) != (0, 0, 0):
                                pv = (kp, tw, 0) if par else (
                                    (kp, tw - 1, 1) if tw else (kp - 1, 3, 1))
                                em.w(v, kb.sDV, (P, "kred", *pv))
                            v.reduce_max(st[0:1, 12:13],
                                         [kb.pE, kb.pF][par][0:1, 0:TW], AX.X)
                            em.rec((P, "kredr", kp, tw, par), kb.sDV)
                            em.w(v, kb.sDV, (P, "kredr", kp, tw, par))
                            if tw == 0:
                                v.tensor_copy(st[0:1, 8 + kvl:9 + kvl],
                                              st[0:1, 12:13])
                            else:
                                v.tensor_max(st[0:1, 8 + kvl:9 + kvl],
                                             st[0:1, 8 + kvl:9 + kvl],
                                             st[0:1, 12:13])
                            em.rec((P, "kred", kp, tw, par), kb.sDV)
                            if tw == 3:
                                em.rec((P, "mx", kvl), kb.sDV)
                for p in range(8):
                    for h in (2 * p, 2 * p + 1):
                        hcfg = dict(cfg)
                        if h % 2 == 0:
                            hcfg["ctx_dst"] = lambda hh, qt: AR[
                                0:64, AT_CTX + (hh // 2) * D + qt * TW:
                                AT_CTX + (hh // 2) * D + (qt + 1) * TW]
                        else:
                            hcfg["ctx_dst"] = lambda hh, qt: None
                            hcfg["shtmp_prev"] = lambda hh, qt: (
                                (kb.sSHC, (P, "csh", hh, qt - 1)) if qt else
                                ((kb.sSHC, (P, "csh", hh - 2, 3)) if hh >= 3
                                 else (kb.sSHK, (P, "ksh", 1, 3))))
                        _head_dve(kb, em, v, P, h, hcfg)
            em.on("vector", ve)


def phase_attn1(kb, em):
    AR, ARf = kb.AR, kb.ARf
    kbuf = [AR[:, AT_KG + i * D:AT_KG + (i + 1) * D] for i in range(2)]
    vw = AR[:, AT_W:AT_W + 4096]
    wslot = [AR[:, AT_W + i * D:AT_W + (i + 1) * D] for i in range(2)]
    qslot = [AR[:, AT_Q + i * D:AT_Q + (i + 1) * D] for i in range(2)]
    ktmp = AR[:, AT_KT:AT_KT + TW]
    cpos = [AR[0:1, AT_NEGB:AT_NEGB + D], AR[0:1, AT_NEGB + D:AT_NEGB + 2 * D]]
    bias_f = ARf[0:1, AT_BIAF // 2:AT_BIAF // 2 + 768]
    bias_b = AR[0:1, AT_BIAB:AT_BIAB + 768]
    bo_b = AR[0:1, AT_BOB:AT_BOB + D]
    ev_f = [ARf[:, AT_EV // 2 + i * TW:AT_EV // 2 + (i + 1) * TW]
            for i in range(2)]
    st = kb.stats
    SC = float(1.0 / np.sqrt(128.0))
    BLOCKS = [(2, "k", 0), (3, "k", 1), (0, "q", 0), (1, "q", 1),
              (4, "v", 0), (5, "v", 1)]

    for hf in range(2):
        P = ("a1", hf)
        with em.block(f"attn1_{hf}"):
            em.rec((P, "adpre0"), kb.sAD[0])
            em.rec((P, "adpre1"), kb.sAD[1])
            adbase = kb._ad_idx

            def sy(e):
                def load(par, c0, c1, src, key):
                    kb.sWL[par].inc(e.dma_start(
                        out=kb.stage[:, par * D + c0:par * D + c1], in_=src), 16)
                    em.rec(key, kb.sWL[par])

                if hf == 0:
                    em.w(e, kb.sAD[0], (P, "adpre0"))
                    em.w(e, kb.sAD[1], (P, "adpre1"))
                    kb.sIN[1].inc(e.dma_start(
                        out=ARf[0:1, AT_EV // 2:AT_EV // 2 + D],
                        in_=kb.d["l1_bo"]), 16)
                    em.rec((P, "bol"), kb.sIN[1])
                for g in range(4):
                    grp = 4 * hf + g
                    if g > 0 or hf > 0:
                        pg = (P, "biasc", g - 1) if g > 0 else \
                            (("a1", 0), "biasc", 3)
                        em.w(e, kb.sPL, pg)
                    kb.sIN[0].inc(e.dma_start(
                        out=bias_f, in_=kb.d["l1_bqkv"][grp:grp + 1, :]), 16)
                    em.rec((P, "biasl", g), kb.sIN[0])
                    for bi, (b, kind, hl) in enumerate(BLOCKS):
                        par = bi % 2
                        if bi >= 2:
                            em.w(e, kb.sPL, (P, "wc", g, bi - 2))
                        elif g > 0:
                            em.w(e, kb.sPL, (P, "wc", g - 1, 4 + bi))
                        elif hf == 1:
                            em.w(e, kb.sPL, (("a1", 0), "wc_wo", 3, bi))
                        load(par, 0, D, kb.d["l1_wqkv"][grp, b],
                             (P, "wl", g, bi))
                for o_sl in range(4):
                    for sh in range(2):
                        em.w(e, kb.sPL, (P, "wc", 3, 4 + sh) if o_sl == 0
                             else (P, "wc_wo", o_sl - 1, sh))
                        for c in range(4):
                            load(sh, c * TW, (c + 1) * TW,
                                 kb.d["l1_wo"][o_sl, hf * 8 + sh * 4 + c],
                                 (P, "wl_wo", o_sl, sh, c))
            em.on("sync", sy)

            if hf == 0:
                def gp_bo(g_raw):
                    g_ = E(g_raw, kb.sPL)
                    em.w(g_, kb.sIN[1], (P, "bol"))
                    g_.tensor_copy(bo_b[:], ARf[0:1, AT_EV // 2:AT_EV // 2 + D])
                    em.rec((P, "boc"), kb.sPL)
                em.on("gpsimd", gp_bo)

            def gp(g_raw):
                g_ = E(g_raw, kb.sPL)
                for g in range(4):
                    em.w(g_, kb.sIN[0], (P, "biasl", g))
                    if g > 0 or hf > 0:
                        pg = (P, "vmm", g - 1, 15) if g > 0 else \
                            (("a1", 0), "vmm", 3, 15)
                        em.w(g_, kb.sMM, pg)
                    g_.tensor_copy(bias_b[:], bias_f)
                    em.rec((P, "biasc", g), kb.sPL)
                    for bi, (b, kind, hl) in enumerate(BLOCKS):
                        par = bi % 2
                        em.w(g_, kb.sWL[par], (P, "wl", g, bi))
                        if kind == "v":
                            em.w(g_, kb.sMM, (P, "qmm", g, 1, 3))
                            if g > 0:
                                em.w(g_, kb.sMM, (P, "vmm", g - 1, 15))
                            g_.tensor_copy(
                                bass.AP(kb.AR, AT_W + hl * 128,
                                        [[57344, 128], [256, NK], [1, 128]]),
                                kb.stage[:, par * D:(par + 1) * D])
                        else:
                            if bi >= 2:
                                pb, pkind, phl = BLOCKS[bi - 2]
                                em.w(g_, kb.sMM,
                                     (P, "kmm" if pkind == "k" else "qmm",
                                      g, phl, 3))
                            elif g > 0:
                                em.w(g_, kb.sMM, (P, "vmm", g - 1, 15))
                            elif hf == 1:
                                em.w(g_, kb.sMM, (("a1", 0), "womm", 3, 15))
                            g_.tensor_copy(wslot[par][:],
                                           kb.stage[:, par * D:(par + 1) * D])
                        em.rec((P, "wc", g, bi), kb.sPL)
                    for hl in range(2):
                        _head_pool(kb, em, g_, P, 2 * g + hl)
                for o_sl in range(4):
                    for sh in range(2):
                        em.w(g_, kb.sWL[sh], (P, "wl_wo", o_sl, sh, 3))
                        em.w(g_, kb.sMM, (P, "vmm", 3, 15) if o_sl == 0
                             else (P, "womm", o_sl - 1, 15))
                        g_.tensor_copy(wslot[sh][:],
                                       kb.stage[:, sh * D:(sh + 1) * D])
                        em.rec((P, "wc_wo", o_sl, sh), kb.sPL)
                    for t in range(NT128):
                        kslot = kb._ad_idx % 2
                        kb._ad_idx += 1
                        em.w(g_raw, kb.sAC, (P, "woev", o_sl, t))
                        if o_sl == 0 and t == 0:
                            em.w(g_raw, kb.sAD[0], (P, "adpre0"))
                            em.w(g_raw, kb.sAD[1], (P, "adpre1"))
                            g_raw.wait_ge(kb.sXS.h, 32)
                        kb.sAD[kslot].inc(g_raw.dma_start(
                            out=kb.d["out"][t * 128:(t + 1) * 128,
                                            o_sl * TW:(o_sl + 1) * TW],
                            in_=ev_f[kslot], accum_op=ALU.add), 16)
                        em.rec((P, "ad", o_sl, t), kb.sAD[kslot])
            em.on("gpsimd", gp)

            def te(t_eng):
                for g in range(4):
                    for bi, (b, kind, hl) in enumerate(BLOCKS):
                        em.w(t_eng, kb.sPL, (P, "wc", g, bi))
                        em.w(t_eng, kb.sPL, (P, "biasc", g))
                        if kind == "k":
                            for tw in range(4):
                                ps = [kb.pA, kb.pB][tw % 2]
                                if tw >= 2:
                                    em.w(t_eng, kb.sAC, (P, "ksq", g, hl, tw - 2))
                                elif hl == 1:
                                    em.w(t_eng, kb.sAC, (P, "ksq", g, 0, 2 + tw))
                                elif g > 0:
                                    em.w(t_eng, kb.sAC,
                                         (P, "exp", 2 * g - 1, 38 + tw))
                                elif hf == 1:
                                    em.w(t_eng, kb.sAC,
                                         (("a1", 0), "woev", 3, 14 + tw))
                                ins = None
                                for k in range(NK):
                                    ins = t_eng.matmul(
                                        ps[:], wslot[bi % 2][:, k * 128:(k + 1) * 128],
                                        kb.xn[:, k * D + tw * TW:
                                              k * D + (tw + 1) * TW],
                                        start=(k == 0), stop=False)
                                ins = t_eng.matmul(
                                    ps[:], bias_b[0:1, (256 + hl * 128):
                                                  (384 + hl * 128)],
                                    kb.ones_row[0:1, 0:TW],
                                    start=False, stop=True)
                                kb.sMM.inc(ins)
                                em.rec((P, "kmm", g, hl, tw), kb.sMM)
                                em.w(t_eng, kb.sAC, (P, "ksq", g, hl, tw))
                                pv = None
                                if tw > 0:
                                    pv = (P, "kred", g, hl, tw - 1)
                                elif hl > 0:
                                    pv = (P, "qred_none",)
                                if tw > 0:
                                    em.w(t_eng, kb.sDV, pv)
                                if tw == 0 and (g > 0 or hf > 0):
                                    pg = (P, "sumev_last", g - 1, 1) if g > 0 \
                                        else (("a1", 0), "sumev_last", 3, 1)
                                    em.w(t_eng, kb.sAC, pg)
                                kb.sMM.inc(t_eng.matmul(
                                    [kb.pE, kb.pF][hl][0:1, 0:TW],
                                    kb.ones_col[:, 0:1], ktmp[:, 0:TW],
                                    start=True, stop=True))
                                em.rec((P, "knmm", g, hl, tw), kb.sMM)
                        elif kind == "q":
                            for tw in range(4):
                                ps = [kb.pA, kb.pB][tw % 2]
                                if tw >= 2:
                                    em.w(t_eng, kb.sAC, (P, "qsq", g, hl, tw - 2))
                                else:
                                    em.w(t_eng, kb.sAC, (P, "ksq", g, 1, 2 + tw))
                                ins = None
                                for k in range(NK):
                                    ins = t_eng.matmul(
                                        ps[:], wslot[bi % 2][:, k * 128:(k + 1) * 128],
                                        kb.xn[:, k * D + tw * TW:
                                              k * D + (tw + 1) * TW],
                                        start=(k == 0), stop=False)
                                ins = t_eng.matmul(
                                    ps[:], bias_b[0:1, hl * 128:(hl + 1) * 128],
                                    kb.ones_row[0:1, 0:TW],
                                    start=False, stop=True)
                                kb.sMM.inc(ins)
                                em.rec((P, "qmm", g, hl, tw), kb.sMM)
                                em.w(t_eng, kb.sAC, (P, "qsq", g, hl, tw))
                                if tw > 0:
                                    em.w(t_eng, kb.sAC, (P, "cpos", g, hl, tw - 1))
                                else:
                                    em.w(t_eng, kb.sDV, (P, "kred", g, hl, 3))
                                kb.sMM.inc(t_eng.matmul(
                                    [kb.pE, kb.pF][hl][0:1, 0:TW],
                                    kb.ones_col[:, 0:1], ktmp[:, 0:TW],
                                    start=True, stop=True))
                                em.rec((P, "qnmm", g, hl, tw), kb.sMM)
                        else:
                            if hl == 0:
                                continue
                            em.w(t_eng, kb.sPL, (P, "wc", g, 5))
                            for tc in range(16):
                                ps = [kb.pA, kb.pB][tc % 2]
                                if tc >= 2:
                                    em.w(t_eng, kb.sAC, (P, "vev", g, tc - 2))
                                else:
                                    em.w(t_eng, kb.sAC, (P, "qsq", g, 1, 2 + tc))
                                ins = None
                                for k in range(NK):
                                    ins = t_eng.matmul(
                                        ps[:, 0:256],
                                        kb.xn[:, k * D + tc * 128:
                                              k * D + (tc + 1) * 128],
                                        vw[:, k * 256:(k + 1) * 256],
                                        start=(k == 0), stop=False)
                                ins = t_eng.matmul(
                                    ps[:, 0:256], kb.ones_row[0:1, 0:128],
                                    bias_b[0:1, 512:768],
                                    start=False, stop=True)
                                kb.sMM.inc(ins)
                                em.rec((P, "vmm", g, tc), kb.sMM)
                    for hl in range(2):
                        h = 2 * g + hl
                        hcfg = {
                            "dh": 128,
                            "first_h": 0,
                            "first_hqt": (0, 0),
                            "odd_shift": False,
                            "k_ap": (lambda hl_: lambda kc: kbuf[hl_][
                                :, kc * 128:(kc + 1) * 128])(hl),
                            "v_ap": (lambda hl_: lambda kc: AR[
                                :, AT_VG + kc * 256 + hl_ * 128:
                                AT_VG + kc * 256 + (hl_ + 1) * 128])(hl),
                            "q_ap": (lambda hl_: lambda qt: qslot[hl_][
                                :, qt * TW:(qt + 1) * TW])(hl),
                            "cpos": cpos[hl],
                            "ksh_dep": None,
                            "sbank_free": [
                                (P, "vev", g, 14), (P, "vev", g, 15)]
                            if hl == 0 else
                            [(P, "exp", h - 1, 38), (P, "exp", h - 1, 39)],
                            "sum_free": (lambda g_, hl_: lambda hh, qt: (
                                (P, "cpos", g_, qt % 2, 3)
                                if hl_ == 0 else
                                (P, "sumev", hh - 1, 2 + qt % 2)))(g, hl),
                        }
                        _head_pe(kb, em, t_eng, P, h, hcfg)
                for o_sl in range(4):
                    em.w(t_eng, kb.sPL, (P, "wc_wo", o_sl, 1))
                    if o_sl == 0:
                        em.w(t_eng, kb.sDV, (P, "ctxev", 7, 3))
                    for t in range(NT128):
                        ps = [kb.pA, kb.pB][t % 2]
                        if o_sl == 0 and t < 2:
                            em.w(t_eng, kb.sAC, (P, "exp", 7, 38 + t))
                        else:
                            pv = (o_sl, t - 2) if t >= 2 else (o_sl - 1, 14 + t)
                            em.w(t_eng, kb.sAC, (P, "woev", *pv))
                        ins = None
                        for c in range(8):
                            ins = t_eng.matmul(
                                ps[:],
                                AR[:, AT_CTX + c * D + t * 128:
                                   AT_CTX + c * D + (t + 1) * 128],
                                wslot[c // 4][:, (c % 4) * TW:(c % 4 + 1) * TW],
                                start=(c == 0), stop=(c == 7 and hf == 1))
                        if hf == 0:
                            if o_sl == 0 and t == 0:
                                em.w(t_eng, kb.sPL, (P, "boc"))
                            ins = t_eng.matmul(
                                ps[:], kb.ones_row[0:1, 0:128],
                                bo_b[0:1, o_sl * TW:(o_sl + 1) * TW],
                                start=False, stop=True)
                        kb.sMM.inc(ins)
                        em.rec((P, "womm", o_sl, t), kb.sMM)
            em.on("tensor", te)

            def acf(a_raw):
                a = E(a_raw, kb.sAC)
                for g in range(4):
                    for bi, (b, kind, hl) in enumerate(BLOCKS):
                        if kind == "k":
                            for tw in range(4):
                                ps = [kb.pA, kb.pB][tw % 2]
                                em.w(a, kb.sMM, (P, "kmm", g, hl, tw))
                                if tw == 0 and (g > 0 or hf > 0):
                                    pg = (P, "smm", 2 * (g - 1) + hl, 39) \
                                        if g > 0 else None
                                    if pg:
                                        em.w(a, kb.sMM, pg)
                                a.activation(kbuf[hl][:, tw * TW:(tw + 1) * TW],
                                             ps[:], AF.Copy)
                                em.rec((P, "kev", g, hl, tw), kb.sAC)
                                pv = (P, "knmm", g, hl, tw - 1) if tw else \
                                    ((P, "knmm", g, 0, 3) if hl else
                                     ((P, "qnmm", g - 1, 1, 3) if g else None))
                                if pv:
                                    em.w(a, kb.sMM, pv)
                                a.activation(ktmp[:, 0:TW], ps[:], AF.Square)
                                em.rec((P, "ksq", g, hl, tw), kb.sAC)
                        elif kind == "q":
                            for tw in range(4):
                                ps = [kb.pA, kb.pB][tw % 2]
                                em.w(a, kb.sMM, (P, "qmm", g, hl, tw))
                                if tw == 0 and g > 0:
                                    em.w(a, kb.sMM, (P, "smm", 2 * (g - 1) + hl, 39))
                                a.activation(qslot[hl][:, tw * TW:(tw + 1) * TW],
                                             ps[:], AF.Copy, scale=SC)
                                em.rec((P, "qev", g, hl, tw), kb.sAC)
                                pv = (P, "qnmm", g, hl, tw - 1) if tw else \
                                    (P, "knmm", g, 1, 3)
                                em.w(a, kb.sMM, pv)
                                a.activation(ktmp[:, 0:TW], ps[:], AF.Square,
                                             scale=SC)
                                em.rec((P, "qsq", g, hl, tw), kb.sAC)
                                em.w(a, kb.sMM, (P, "qnmm", g, hl, tw))
                                if tw == 0:
                                    em.w(a, kb.sDV, (P, "mx", g, hl))
                                    if g > 0:
                                        em.w(a, kb.sMM,
                                             (P, "smm", 2 * (g - 1) + hl, 39))
                                a.activation(
                                    cpos[hl][0:1, tw * TW:(tw + 1) * TW],
                                    [kb.pE, kb.pF][hl][0:1, 0:TW], AF.Sqrt,
                                    scale=st[0:1, 8 + hl:9 + hl])
                                em.rec((P, "cpos", g, hl, tw), kb.sAC)
                        else:
                            if hl == 0:
                                continue
                            for tc in range(16):
                                ps = [kb.pA, kb.pB][tc % 2]
                                em.w(a, kb.sMM, (P, "vmm", g, tc))
                                if tc < 2 and g > 0:
                                    em.w(a, kb.sMM, (P, "cs", 2 * g - 1, 39))
                                a.activation(
                                    AR[:, AT_VG + tc * 256:AT_VG + (tc + 1) * 256],
                                    ps[:, 0:256], AF.Copy)
                                em.rec((P, "vev", g, tc), kb.sAC)
                    for hl in range(2):
                        h = 2 * g + hl
                        hcfg = {"first_h": 0, "first_hqt": (0, 0)}
                        _head_act(kb, em, a, P, h, hcfg)
                        em.rec((P, "sumev_last", g, hl), kb.sAC)
                for o_sl in range(4):
                    for t in range(NT128):
                        em.w(a, kb.sMM, (P, "womm", o_sl, t))
                        gi = o_sl * 16 + t
                        kslot = (adbase + gi) % 2
                        if gi >= 2:
                            em.w(a, kb.sAD[kslot], (P, "ad", *divmod(gi - 2, 16)))
                        else:
                            em.w(a, kb.sAD[0], (P, "adpre0"))
                            em.w(a, kb.sAD[1], (P, "adpre1"))
                        a.activation(ev_f[kslot], [kb.pA, kb.pB][t % 2][:],
                                     AF.Copy)
                        em.rec((P, "woev", o_sl, t), kb.sAC)
            em.on("scalar", acf)

            def ve(v_raw):
                v = E(v_raw, kb.sDV)
                for g in range(4):
                    for hl in range(2):
                        for tw in range(4):
                            em.w(v, kb.sMM, (P, "knmm", g, hl, tw))
                            if (g, hl, tw) != (0, 0, 0):
                                pv = (P, "kred", g, hl, tw - 1) if tw else (
                                    (P, "kred", g, hl - 1, 3) if hl else
                                    (P, "kred", g - 1, 1, 3))
                                em.w(v, kb.sDV, pv)
                            if tw == 0 and (g > 0 or hf > 0):
                                pg = (P, "cpos", g - 1, hl, 3) if g > 0 else \
                                    (("a1", 0), "cpos", 3, hl, 3)
                                em.w(v, kb.sAC, pg)
                            v.reduce_max(st[0:1, 12:13],
                                         [kb.pE, kb.pF][hl][0:1, 0:TW], AX.X)
                            em.rec((P, "kredr", g, hl, tw), kb.sDV)
                            em.w(v, kb.sDV, (P, "kredr", g, hl, tw))
                            if tw == 0:
                                v.tensor_copy(st[0:1, 8 + hl:9 + hl],
                                              st[0:1, 12:13])
                            else:
                                v.tensor_max(st[0:1, 8 + hl:9 + hl],
                                             st[0:1, 8 + hl:9 + hl],
                                             st[0:1, 12:13])
                            em.rec((P, "kred", g, hl, tw), kb.sDV)
                            if tw == 3:
                                em.rec((P, "mx", g, hl), kb.sDV)
                    for hl in range(2):
                        h = 2 * g + hl
                        hcfg = {
                            "dh": 128,
                            "first_hqt": (0, 0),
                            "ctx_dst": lambda hh, qt: AR[
                                0:128, AT_CTX + hh * D + qt * TW:
                                AT_CTX + hh * D + (qt + 1) * TW],
                        }
                        _head_dve(kb, em, v, P, h, hcfg)
                        if hl == 1:
                            em.rec((P, "rs_last", g), kb.sDV)
            em.on("vector", ve)


def _yev_dep(kb, P, ex, o_sl, tloc):
    if ex == 0:
        return (kb.sAC, (P, "yevA", ex, o_sl, tloc))
    return (kb.sDV, (P, "yev", ex, o_sl, tloc))


def phase_moe(kb, em):
    AR, ARf = kb.AR, kb.ARf
    ms = kb.moe_s
    wfc = [AR[:, MO_WFC + i * D:MO_WFC + (i + 1) * D] for i in range(2)]
    wpj = [AR[:, MO_WPJ + i * 1024:MO_WPJ + (i + 1) * 1024] for i in range(2)]
    tmp_f = [ARf[:, MO_TMP // 2 + i * TW:MO_TMP // 2 + (i + 1) * TW]
             for i in range(2)]
    pss8 = None

    for hf in range(2):
        P = ("mo", hf)
        with em.block(f"moe_{hf}"):
            em.rec((P, "adpre0"), kb.sAD[0])
            em.rec((P, "adpre1"), kb.sAD[1])
            adbase = kb._ad_idx
            pss8 = [kb.pA, kb.pB, kb.pC, kb.pD, kb.pE, kb.pF, kb.pR, kb.pT]

            def sy(e):
                def load(par, c0, c1, src, key):
                    kb.sWL[par].inc(e.dma_start(
                        out=kb.stage[:, par * D + c0:par * D + c1], in_=src), 16)
                    em.rec(key, kb.sWL[par])

                u = 0
                for ex in range(8):
                    for ob in range(16):
                        par = u % 2
                        if u >= 2:
                            em.w(e, kb.sPL, (P, "wc", "fc", ex, ob - 2)
                                 if ob >= 2 else (P, "wc", "pj", ex - 1, 3, 6 + ob))
                        elif hf == 1:
                            em.w(e, kb.sPL, (("mo", 0), "wc", "pj", 7, 3, 6 + ob))
                        load(par, 0, D, kb.d["l1_fc"][ex, ob],
                             (P, "wl", "fc", ex, ob))
                        u += 1
                    for o_sl in range(4):
                        for sh in range(8):
                            par = u % 2
                            if sh >= 2 or o_sl > 0:
                                pv = (o_sl, sh - 2) if sh >= 2 else (o_sl - 1, 6 + sh)
                                em.w(e, kb.sPL, (P, "wc", "pj", ex, *pv))
                            else:
                                em.w(e, kb.sPL, (P, "wc", "fc", ex, 14 + sh))
                            for c in range(2):
                                load(par, c * TW, (c + 1) * TW,
                                     kb.d["l1_pj"][ex, o_sl, sh * 2 + c],
                                     (P, "wl", "pj", ex, o_sl, sh, c))
                            u += 1
            em.on("sync", sy)

            def gp(g_raw):
                g_ = E(g_raw, kb.sPL)
                u = 0
                for ex in range(8):
                    for ob in range(16):
                        par = u % 2
                        em.w(g_, kb.sWL[par], (P, "wl", "fc", ex, ob))
                        if ex > 0 or ob >= 2:
                            pv = (P, "hmm", ex, ob - 2, 1) if ob >= 2 else \
                                (P, "hmm", ex - 1, 14 + ob, 1)
                            em.w(g_, kb.sMM, pv)
                        elif hf == 1:
                            em.w(g_, kb.sMM, (("mo", 0), "hmm", 7, 14 + ob, 1))
                        g_.tensor_copy(wfc[par][:], kb.stage[:, par * D:(par + 1) * D])
                        em.rec((P, "wc", "fc", ex, ob), kb.sPL)
                        u += 1
                    for o_sl in range(4):
                        for sh in range(8):
                            par = u % 2
                            em.w(g_, kb.sWL[par], (P, "wl", "pj", ex, o_sl, sh, 1))
                            gi2 = o_sl * 8 + sh
                            if gi2 >= 2:
                                pv = divmod(gi2 - 2, 8)
                                em.w(g_, kb.sMM, (P, "ymm", ex, pv[0], pv[1], 7))
                            else:
                                em.w(g_, kb.sMM, (P, "hmm", ex, 14 + gi2, 1))
                            g_.tensor_copy(wpj[par][:],
                                           kb.stage[:, par * D:par * D + 1024])
                            em.rec((P, "wc", "pj", ex, o_sl, sh), kb.sPL)
                            u += 1
                    if ex == 7:
                        for tloc in range(8):
                            kslot = kb._ad_idx % 2
                            kb._ad_idx += 1
                            dep = _yev_dep(kb, P, 7, 3, tloc)
                            em.w(g_raw, dep[0], dep[1])
                            if tloc == 0:
                                em.w(g_raw, kb.sAD[0], (P, "adpre0"))
                                em.w(g_raw, kb.sAD[1], (P, "adpre1"))
                                g_raw.wait_ge(kb.sXS.h, 32)
                            t128 = hf * 8 + tloc
                            kb.sAD[kslot].inc(g_raw.dma_start(
                                out=kb.d["out"][t128 * 128:(t128 + 1) * 128, :],
                                in_=ARf[:, tloc * D:(tloc + 1) * D],
                                accum_op=ALU.add), 16)
                            em.rec((P, "ad", tloc), kb.sAD[kslot])
            em.on("gpsimd", gp)

            def te(t_eng):
                for tloc in range(8):
                    if tloc > 0:
                        em.w(t_eng, kb.sDV, (P, "lg", tloc - 1))
                    ins = None
                    for k in range(NK):
                        ins = t_eng.matmul(
                            kb.pR[:, 0:8],
                            kb.xn[:, k * D + hf * 1024 + tloc * 128:
                                  k * D + hf * 1024 + (tloc + 1) * 128],
                            kb.router_b[:, k * 8:(k + 1) * 8],
                            start=(k == 0), stop=(k == NK - 1))
                    kb.sMM.inc(ins)
                    em.rec((P, "rmm", tloc), kb.sMM)
                for ex in range(8):
                    for ob in range(16):
                        par = ob % 2 if False else (ex * 16 + ob) % 2
                        em.w(t_eng, kb.sPL, (P, "wc", "fc", ex, ob))
                        for tw in range(2):
                            ps = [kb.pA, kb.pB][tw]
                            if ex > 0 or ob > 0:
                                if ob:
                                    em.w(t_eng, kb.sDV, (P, "hrel", ex, ob - 1, tw))
                                else:
                                    dep = _yev_dep(kb, P, ex - 1, 3, 6 + tw)
                                    em.w(t_eng, dep[0], dep[1])
                            ins = None
                            for k in range(NK):
                                ins = t_eng.matmul(
                                    ps[:], wfc[par][:, k * 128:(k + 1) * 128],
                                    kb.xn[:, k * D + hf * 1024 + tw * TW:
                                          k * D + hf * 1024 + (tw + 1) * TW],
                                    start=(k == 0), stop=(k == NK - 1))
                            kb.sMM.inc(ins)
                            em.rec((P, "hmm", ex, ob, tw), kb.sMM)
                    for o_sl in range(4):
                        for sh in range(8):
                            par = (o_sl * 8 + sh) % 2
                            em.w(t_eng, kb.sPL, (P, "wc", "pj", ex, o_sl, sh))
                            if sh == 0 and o_sl == 0:
                                em.w(t_eng, kb.sAC, (P, "h2", ex, 15, 1))
                                if ex == 0:
                                    em.w(t_eng, kb.sDV, (P, "wtok", 7))
                            for tloc in range(8):
                                ps = pss8[tloc]
                                if sh == 0:
                                    if o_sl > 0:
                                        dep = _yev_dep(kb, P, ex, o_sl - 1, tloc)
                                        em.w(t_eng, dep[0], dep[1])
                                    elif ex > 0:
                                        dep = _yev_dep(kb, P, ex - 1, 3, tloc)
                                        em.w(t_eng, dep[0], dep[1])
                                    elif tloc == 6:
                                        em.w(t_eng, kb.sDV, (P, "lg", 7))
                                    if ex == 0 and o_sl == 0 and tloc < 2:
                                        em.w(t_eng, kb.sDV,
                                             (P, "hrel", ex, 15, tloc))
                                ins = None
                                for c in range(2):
                                    ic = sh * 2 + c
                                    ins = t_eng.matmul(
                                        ps[:],
                                        AR[:, MO_H2 + ic * 1024 + tloc * 128:
                                           MO_H2 + ic * 1024 + (tloc + 1) * 128],
                                        wpj[par][:, c * TW:(c + 1) * TW],
                                        start=(ic == 0), stop=(ic == 15))
                                kb.sMM.inc(ins)
                                em.rec((P, "ymm", ex, o_sl, sh, tloc), kb.sMM)
            em.on("tensor", te)

            def ve(v_raw):
                v = E(v_raw, kb.sDV)
                for tloc in range(8):
                    em.w(v, kb.sMM, (P, "rmm", tloc))
                    wv = lambda key: em.w(v, kb.sDV, key)
                    v.tensor_copy(ms[:, 0:8], kb.pR[:, 0:8])
                    em.rec((P, "lg", tloc), kb.sDV)
                    wv((P, "lg", tloc))
                    v.reduce_max(ms[:, 32:33], ms[:, 0:8], AX.X)
                    em.rec((P, "m1", tloc), kb.sDV)
                    wv((P, "m1", tloc))
                    v.tensor_scalar(ms[:, 8:16], ms[:, 0:8], ms[:, 32:33],
                                    None, ALU.is_equal)
                    em.rec((P, "eq1", tloc), kb.sDV)
                    wv((P, "eq1", tloc))
                    v.scalar_tensor_tensor(ms[:, 16:24], ms[:, 8:16], -1e30,
                                           ms[:, 0:8], ALU.mult, ALU.add)
                    em.rec((P, "lg2", tloc), kb.sDV)
                    wv((P, "lg2", tloc))
                    v.reduce_max(ms[:, 33:34], ms[:, 16:24], AX.X)
                    em.rec((P, "m2", tloc), kb.sDV)
                    wv((P, "m2", tloc))
                    v.tensor_sub(ms[:, 34:35], ms[:, 32:33], ms[:, 33:34])
                    em.rec((P, "dd", tloc), kb.sDV)
                    wv((P, "dd", tloc))
                    v.tensor_scalar(ms[:, 24:32], ms[:, 16:24], ms[:, 33:34],
                                    None, ALU.is_equal)
                    em.rec((P, "eq2", tloc), kb.sDV)
                    em.w(v, kb.sAC, (P, "p12", tloc))
                    v.tensor_scalar(ms[:, 40:48], ms[:, 8:16], ms[:, 35:36],
                                    None, ALU.mult)
                    em.rec((P, "t1", tloc), kb.sDV)
                    wv((P, "t1", tloc))
                    v.scalar_tensor_tensor(
                        ms[:, 64 + tloc * 8:72 + tloc * 8], ms[:, 24:32],
                        ms[:, 36:37], ms[:, 40:48], ALU.mult, ALU.add)
                    em.rec((P, "wtok", tloc), kb.sDV)
                for ex in range(8):
                    for ob in range(16):
                        for tw in range(2):
                            em.w(v, kb.sMM, (P, "hmm", ex, ob, tw))
                            if ex > 0 or ob > 0:
                                em.w(v, kb.sAC, (P, "h2", ex, ob - 1, tw)
                                     if ob else (P, "h2", ex - 1, 15, tw))
                            v.tensor_scalar_max(tmp_f[tw], [kb.pA, kb.pB][tw][:],
                                                0.0)
                            em.rec((P, "hrel", ex, ob, tw), kb.sDV)
                    for o_sl in range(4):
                        for tloc in range(8):
                            if ex == 0:
                                continue
                            em.w(v, kb.sMM, (P, "ymm", ex, o_sl, 7, tloc))
                            em.w(v, kb.sDV, (P, "wtok", tloc))
                            if ex == 1:
                                em.w(v, kb.sAC, (P, "yevA", 0, o_sl, tloc))
                            else:
                                em.w(v, kb.sDV, (P, "yev", ex - 1, o_sl, tloc))
                            ydst = ARf[:, tloc * D + o_sl * TW:
                                       tloc * D + (o_sl + 1) * TW]
                            v.scalar_tensor_tensor(
                                ydst, pss8[tloc][:],
                                ms[:, 64 + tloc * 8 + ex:65 + tloc * 8 + ex],
                                ydst, ALU.mult, ALU.add)
                            em.rec((P, "yev", ex, o_sl, tloc), kb.sDV)
            em.on("vector", ve)

            def acf(a_raw):
                a = E(a_raw, kb.sAC)
                for tloc in range(8):
                    em.w(a, kb.sDV, (P, "dd", tloc))
                    if tloc > 0:
                        em.w(a, kb.sDV, (P, "t1", tloc - 1))
                        em.w(a, kb.sDV, (P, "wtok", tloc - 1))
                    a.activation(ms[:, 35:36], ms[:, 34:35], AF.Sigmoid)
                    a.activation(ms[:, 36:37], ms[:, 34:35], AF.Sigmoid,
                                 scale=-1.0)
                    em.rec((P, "p12", tloc), kb.sAC)
                for ex in range(8):
                    for ob in range(16):
                        for tw in range(2):
                            em.w(a, kb.sDV, (P, "hrel", ex, ob, tw))
                            a.activation(
                                AR[:, MO_H2 + ob * 1024 + tw * TW:
                                   MO_H2 + ob * 1024 + (tw + 1) * TW],
                                tmp_f[tw], AF.Square)
                            em.rec((P, "h2", ex, ob, tw), kb.sAC)
                    if ex == 0:
                        for o_sl in range(4):
                            for tloc in range(8):
                                em.w(a, kb.sMM, (P, "ymm", 0, o_sl, 7, tloc))
                                em.w(a, kb.sDV, (P, "wtok", tloc))
                                if hf == 1 and o_sl == 0 and tloc == 0:
                                    em.w(a, kb.sAD[0], (P, "adpre0"))
                                    em.w(a, kb.sAD[1], (P, "adpre1"))
                                ydst = ARf[:, tloc * D + o_sl * TW:
                                           tloc * D + (o_sl + 1) * TW]
                                a.activation(ydst, pss8[tloc][:], AF.Copy,
                                             scale=ms[:, 64 + tloc * 8:
                                                      65 + tloc * 8])
                                em.rec((P, "yevA", 0, o_sl, tloc), kb.sAC)
            em.on("scalar", acf)


def phase_final(kb, em):
    with em.block("final"):
        def gp(g_raw):
            em.rec(("fin", "ad0"), kb.sAD[0])
            em.rec(("fin", "ad1"), kb.sAD[1])
            em.w(g_raw, kb.sAD[0], ("fin", "ad0"))
            em.w(g_raw, kb.sAD[1], ("fin", "ad1"))
        em.on("gpsimd", gp)




def _debug_dump(kb, em):
    with em.block("fin"):
        def sy(e):
            for c in range(NK):
                kb.sIN[0].inc(e.dma_start(
                    out=kb.d["out"][c * 128:(c + 1) * 128, 0:D // 2],
                    in_=kb.xn.bitcast(F32)[:, c * D // 2:(c + 1) * D // 2]), 16)
            if not em.dry:
                e.wait_ge(kb.sIN[0].h, kb.sIN[0].v)
        em.on("sync", sy)


def _phase_list(upto):
    phases = [phase_setup,
              lambda kb, em: phase_ln(kb, em, kb.d["x0"], "ln0")]
    if upto == "ln0":
        phases.append(_debug_dump)
        return phases
    phases.append(phase_ffn0)
    if upto == "ffn0":
        phases.append(_debug_dump)
        return phases
    phases.append(phase_attn0)
    if upto == "attn0":
        phases.append(_debug_dump)
        return phases
    phases.append(lambda kb, em: phase_ln(kb, em, kb.d["xs"], "ln1",
                                          wait_ad=True, copy_out=True))
    if upto != "nomoe":
        phases.append(phase_moe)
    phases.append(phase_attn1)
    phases.append(phase_final)
    return phases


def build_nc(upto=None):
    kb = KB()
    kb.nc = bass.Bass("TRN2", target_bir_lowering=False)
    declare_dram(kb)
    with ExitStack() as stack:
        _alloc(kb, stack)
        phases = _phase_list(upto)
        em = Em(kb)
        kb._ad_idx = 0
        kb._ad_base = 0
        for ph in phases:
            kb._ad_base = kb._ad_idx
            ph(kb, em)
        for s_ in kb.all_sems:
            s_.v = 0
        em.dry = False
        kb._ad_idx = 0
        kb._ad_base = 0
        for ph in phases:
            kb._ad_base = kb._ad_idx
            ph(kb, em)
    return kb


def kernel(**inputs):
    ins = host_prep({k: np.asarray(v) for k, v in inputs.items()})
    x_full = np.asarray(inputs["L0_attn_x"], dtype=np.float32)
    kb = build_nc()
    in_maps = []
    for core in range(8):
        m = dict(ins)
        m["x0"] = np.ascontiguousarray(x_full[core])
        in_maps.append(m)
    res = run_bass_kernel_spmd(kb.nc, in_maps, core_ids=list(range(8)))
    return np.stack([r["out"] for r in res.results]).astype(np.float32)


# version 02
# speedup vs baseline: 1.0529x; 1.0529x over previous
import numpy as np
from contextlib import ExitStack, contextmanager

import concourse.bass as bass
import concourse.mybir as mybir
from concourse.bass_utils import run_bass_kernel_spmd

F32 = mybir.dt.float32
BF16 = mybir.dt.bfloat16
AF = mybir.ActivationFunctionType
ALU = mybir.AluOpType
AX = mybir.AxisListType

ATTN0_HALVES = 2
D = 2048
S = 2048
NK = 16
TW = 512
NT128 = S // 128
I0 = 5632
IC0 = I0 // 128
EPS = 1e-5



class Sem:
    def __init__(self, h):
        self.h = h
        self.v = 0

    def inc(self, instr, n=1):
        instr.then_inc(self.h, n)
        self.v += n
        return self.v


class _FakeIns:
    def then_inc(self, h, n=1):
        return self


class _FakeEng:
    def __getattr__(self, name):
        return lambda *a, **k: _FakeIns()


class E:

    def __init__(self, eng, tick):
        self.e = eng
        self.t = tick

    def __getattr__(self, name):
        f = getattr(self.e, name)

        def g(*a, **k):
            return self.t.inc(f(*a, **k))

        return g


class Em:

    def __init__(self, kb):
        self.kb = kb
        self.T = {}
        self.dry = True

    @contextmanager
    def block(self, name):
        if self.dry:
            self._blk = None
            yield
        else:
            with self.kb.nc.Block(name) as b:
                self._blk = b
                yield

    def on(self, engname, f):
        if self.dry:
            f(_FakeEng())
        else:
            getattr(self._blk, engname)(f)

    def rec(self, key, sem):
        if self.dry:
            self.T[key] = sem.v
        else:
            assert self.T.get(key) == sem.v, (key, self.T.get(key), sem.v)
        return sem.v

    def w(self, eng, sem, key, delta=0):
        if self.dry:
            return
        e = eng.e if isinstance(eng, E) else eng
        v = self.T[key] + delta
        if v > 0:
            e.wait_ge(sem.h, v)

    def wv(self, eng, sem, val):
        if self.dry or val <= 0:
            return
        e = eng.e if isinstance(eng, E) else eng
        e.wait_ge(sem.h, val)


class KB:
    pass


def _alloc(kb, stack):
    nc = kb.nc
    sb = lambda name, shape, dt: stack.enter_context(nc.sbuf_tensor(name, shape, dt))
    ps = lambda name, shape, dt: stack.enter_context(nc.psum_tensor(name, shape, dt))
    sem = lambda name: Sem(stack.enter_context(nc.semaphore(name)))

    kb.xn = sb("xn", [128, NK * D], BF16)
    kb.AR = sb("AR", [128, 57344], BF16)
    kb.ARf = kb.AR.bitcast(F32)
    kb.stage = sb("stage", [128, 4096], F32)
    kb.ones_col = sb("ones_col", [128, 1], BF16)
    kb.ones_row = sb("ones_row", [1, 512], BF16)
    kb.mones_row = sb("mones_row", [1, 128], BF16)
    kb.onesf_row = sb("onesf_row", [1, 128], F32)
    kb.ident = sb("ident", [128, 128], BF16)
    kb.masks = sb("masks", [128, 4 * TW], BF16)
    kb.gs_us = sb("gs_us", [128, 2 * IC0], F32)
    kb.stats = sb("stats", [128, 16], F32)
    kb.consts = sb("consts", [128, 4], F32)
    kb.ds_row = sb("ds_row", [1, D], F32)
    kb.router_b = sb("router_b", [128, NK * 8], BF16)
    kb.moe_s = sb("moe_s", [128, 192], F32)
    kb.iota_p = sb("iota_p", [128, 4], F32)

    kb.pA = ps("pA", [128, TW], F32)
    kb.pB = ps("pB", [128, TW], F32)
    kb.pC = ps("pC", [128, TW], F32)
    kb.pD = ps("pD", [128, TW], F32)
    kb.pE = ps("pE", [128, TW], F32)
    kb.pF = ps("pF", [128, TW], F32)
    kb.pR = ps("pR", [128, TW], F32)
    kb.pT = ps("pT", [128, TW], F32)
    kb.pTb = kb.pT.bitcast(BF16)

    kb.sIN = [sem("sIN0"), sem("sIN1")]
    kb.sWL = [sem("sWL0"), sem("sWL1")]
    kb.sSHK = sem("sSHK")
    kb.sSHQ = sem("sSHQ")
    kb.sSHC = sem("sSHC")
    kb.sMM = sem("sMM")
    kb.sAC = sem("sAC")
    kb.sDV = sem("sDV")
    kb.sPL = sem("sPL")
    kb.sAD = [sem("sAD0"), sem("sAD1")]
    kb.sXS = sem("sXS")
    kb.all_sems = ([kb.sIN[0], kb.sIN[1], kb.sWL[0], kb.sWL[1], kb.sSHK,
                    kb.sSHQ, kb.sSHC, kb.sMM, kb.sAC, kb.sDV, kb.sPL,
                    kb.sAD[0], kb.sAD[1], kb.sXS])


LN_XNT = 0
LN_SQD = 2048
LN_IOT = 8192

AT_CTX = 0
AT_KG = 16384
AT_VG = 24576
AT_Q = 28672
AT_PT = 34816
AT_KT = 35840
AT_W = 36864
AT_NEGB = 40960
AT_SUM = 45056
AT_RS = 46080
AT_RSB = 47104
AT_BIAF = 48128
AT_BIAB = 49664
AT_BOB = 50432
AT_EV = 52480

MO_YACC = 0
MO_H2 = 32768
MO_WFC = 49152
MO_WPJ = 53248
MO_TMP = 55296

FF_HDN = 0
FF_W = 45056
FF_DS = 53248
FF_SIL = 54272
FF_TMP = 55296


def host_prep(inputs):
    asc = np.ascontiguousarray

    def lhsT_blocks(wt):
        Fdim, O = wt.shape
        a = wt.reshape(Fdim // 128, 128, O // 128, 128)
        return asc(a.transpose(2, 1, 0, 3).reshape(O // 128, 128, Fdim))

    def rhs_blocks(w):
        K, O = w.shape
        a = w.reshape(K // 128, 128, O // 512, 512)
        return asc(a.transpose(2, 0, 1, 3))

    ins = {}
    ins["l0_wq"] = lhsT_blocks(asc(inputs["L0_attn_Wq"].T))
    ins["l0_wk"] = lhsT_blocks(asc(inputs["L0_attn_Wk"].T))
    ins["l0_wv"] = asc(inputs["L0_attn_Wv"].T)
    ins["l0_wo"] = rhs_blocks(asc(inputs["L0_attn_Wo"].T))
    ins["l0_wg"] = lhsT_blocks(asc(inputs["L0_ffn_gate_w"].T))
    ins["l0_wu"] = lhsT_blocks(asc(inputs["L0_ffn_up_w"].T))
    ins["l0_wd"] = rhs_blocks(asc(inputs["L0_ffn_down_w"].T))
    ins["l0_gs"] = asc(inputs["L0_ffn_gate_s"].reshape(IC0, 128).T)
    ins["l0_us"] = asc(inputs["L0_ffn_up_s"].reshape(IC0, 128).T)
    ins["l0_ds"] = asc(inputs["L0_ffn_down_s"].reshape(1, D))

    wqkv = inputs["L1_attn_Wqkv"].reshape(D, 3, 16, 128)
    groups = []
    for g in range(8):
        h0, h1 = 2 * g, 2 * g + 1
        blk = np.concatenate(
            [wqkv[:, 0, h0], wqkv[:, 0, h1], wqkv[:, 1, h0], wqkv[:, 1, h1],
             wqkv[:, 2, h0], wqkv[:, 2, h1]], axis=1)
        groups.append(lhsT_blocks(asc(blk)))
    ins["l1_wqkv"] = asc(np.stack(groups))
    bq = inputs["L1_attn_bqkv"].reshape(3, 16, 128)
    ins["l1_bqkv"] = asc(np.stack(
        [np.concatenate([bq[0, 2 * g], bq[0, 2 * g + 1], bq[1, 2 * g],
                         bq[1, 2 * g + 1], bq[2, 2 * g], bq[2, 2 * g + 1]])
         for g in range(8)]))
    ins["l1_wo"] = rhs_blocks(asc(inputs["L1_attn_Wo"]))
    ins["l1_bo"] = asc(inputs["L1_attn_bo"].reshape(1, D))
    ins["l1_router"] = asc(inputs["L1_ffn_w_router"].reshape(NK, 128, 8)
                           .transpose(1, 0, 2).reshape(128, NK * 8))
    ins["l1_fc"] = asc(np.stack([lhsT_blocks(inputs["L1_ffn_w_fc"][e])
                                 for e in range(8)]))
    ins["l1_pj"] = asc(np.stack([rhs_blocks(inputs["L1_ffn_w_proj"][e])
                                 for e in range(8)]))
    return ins


def declare_dram(kb):
    nc = kb.nc
    dr = lambda name, shape, kind="ExternalInput": nc.dram_tensor(
        name, list(shape), F32, kind=kind).ap()
    kb.d = {
        "x0": dr("x0", (S, D)),
        "l0_wq": dr("l0_wq", (16, 128, D)),
        "l0_wk": dr("l0_wk", (4, 128, D)),
        "l0_wv": dr("l0_wv", (D, 512)),
        "l0_wo": dr("l0_wo", (4, NK, 128, 512)),
        "l0_wg": dr("l0_wg", (IC0, 128, D)),
        "l0_wu": dr("l0_wu", (IC0, 128, D)),
        "l0_wd": dr("l0_wd", (4, IC0, 128, 512)),
        "l0_gs": dr("l0_gs", (128, IC0)),
        "l0_us": dr("l0_us", (128, IC0)),
        "l0_ds": dr("l0_ds", (1, D)),
        "l1_wqkv": dr("l1_wqkv", (8, 6, 128, D)),
        "l1_bqkv": dr("l1_bqkv", (8, 768)),
        "l1_wo": dr("l1_wo", (4, NK, 128, 512)),
        "l1_bo": dr("l1_bo", (1, D)),
        "l1_router": dr("l1_router", (128, NK * 8)),
        "l1_fc": dr("l1_fc", (8, 16, 128, D)),
        "l1_pj": dr("l1_pj", (8, 4, 16, 128, 512)),
        "xs": dr("xs", (S, D), kind="Internal"),
        "out": dr("out", (S, D), kind="ExternalOutput"),
    }




def phase_setup(kb, em):
    iotaf = kb.ARf[:, LN_IOT // 2:LN_IOT // 2 + TW]
    with em.block("setup"):
        def gp(g_raw):
            g = E(g_raw, kb.sPL)
            kb.sXS.inc(g_raw.dma_start(out=kb.d["xs"], in_=kb.d["x0"]), 16)
            g.memset(kb.ones_col[:], 1.0)
            g.memset(kb.ones_row[:], 1.0)
            g.memset(kb.mones_row[:], -1.0)
            g.memset(kb.onesf_row[:], 1.0)
            g.memset(kb.consts[:, 0:1], EPS)
            g.iota(iotaf, [[1, TW]], channel_multiplier=0,
                   allow_small_or_imprecise_dtypes=True)
            g.iota(kb.iota_p[:, 0:1], [[0, 1]], channel_multiplier=1,
                   allow_small_or_imprecise_dtypes=True)
            em.rec("setup.pl", kb.sPL)
        em.on("gpsimd", gp)

        def ve(v_raw):
            v = E(v_raw, kb.sDV)
            em.w(v_raw, kb.sPL, "setup.pl")
            for m in range(1, 4):
                v.tensor_scalar_add(kb.iota_p[:, m:m + 1], kb.iota_p[:, 0:1],
                                    float(128 * m))
            em.rec("setup.iop", kb.sDV)
            em.w(v_raw, kb.sDV, "setup.iop")
            v.tensor_scalar(kb.ident[:], iotaf[:, 0:128], kb.iota_p[:, 0:1],
                            None, ALU.is_equal)
            for m in range(4):
                v.tensor_scalar(kb.masks[:, m * TW:(m + 1) * TW], iotaf,
                                kb.iota_p[:, m:m + 1], None, ALU.is_ge)
            em.rec("setup.dv", kb.sDV)
        em.on("vector", ve)

        def sy(e):
            kb.sIN[0].inc(e.dma_start(out=kb.gs_us[:, 0:IC0], in_=kb.d["l0_gs"]), 16)
            kb.sIN[0].inc(e.dma_start(out=kb.gs_us[:, IC0:2 * IC0],
                                      in_=kb.d["l0_us"]), 16)
            kb.sIN[0].inc(e.dma_start(out=kb.stage[:, 0:NK * 8],
                                      in_=kb.d["l1_router"]), 16)
            kb.sIN[0].inc(e.dma_start(out=kb.ds_row[:], in_=kb.d["l0_ds"]), 16)
            em.rec("setup.in", kb.sIN[0])
        em.on("sync", sy)

        def gp2(g_raw):
            g = E(g_raw, kb.sPL)
            em.w(g_raw, kb.sIN[0], "setup.in")
            g.tensor_copy(kb.router_b[:], kb.stage[:, 0:NK * 8])
        em.on("gpsimd", gp2)


def phase_ln(kb, em, src, tag, wait_ad=False, copy_out=False):
    st = kb.stats
    P = tag
    with em.block(f"ln_{tag}"):
        if wait_ad:
            em.rec((P, "adpre0"), kb.sAD[0])
            em.rec((P, "adpre1"), kb.sAD[1])

        def sy(e):
            for t in range(NT128):
                par = t % 2
                if t == 0 and wait_ad:
                    em.w(e, kb.sAD[0], (P, "adpre0"))
                    em.w(e, kb.sAD[1], (P, "adpre1"))
                if t >= 2:
                    em.w(e, kb.sDV, (P, "affine", t - 2))
                    em.w(e, kb.sAC, (P, "sq", t - 2))
                kb.sIN[par].inc(e.dma_start(
                    out=kb.stage[:, par * D:(par + 1) * D],
                    in_=src[t * 128:(t + 1) * 128, :]), 16)
                em.rec((P, "load", t), kb.sIN[par])
        em.on("sync", sy)

        if copy_out:
            def gpc(g_raw):
                em.w(g_raw, kb.sAD[0], (P, "adpre0"))
                em.w(g_raw, kb.sAD[1], (P, "adpre1"))
                kb.sXS.inc(g_raw.dma_start(out=kb.d["out"], in_=kb.d["xs"]), 16)
            em.on("gpsimd", gpc)

        def acf(a_raw):
            a = E(a_raw, kb.sAC)
            for t in range(NT128):
                par = t % 2
                em.w(a, kb.sIN[par], (P, "load", t))
                if t >= 1:
                    em.w(a, kb.sDV, (P, "var", t - 1))
                a.activation(kb.AR[:, LN_SQD:LN_SQD + D],
                             kb.stage[:, par * D:(par + 1) * D],
                             AF.Square, accum_out=st[:, 1:2])
                em.rec((P, "sq", t), kb.sAC)
                em.w(a, kb.sDV, (P, "var", t))
                a.activation(st[:, 4:5], st[:, 3:4], AF.Sqrt,
                             bias=kb.consts[:, 0:1])
                em.rec((P, "sqrt", t), kb.sAC)
        em.on("scalar", acf)

        def ve(v_raw):
            v = E(v_raw, kb.sDV)
            for t in range(NT128):
                par = t % 2
                em.w(v, kb.sIN[par], (P, "load", t))
                if t >= 1:
                    em.w(v, kb.sAC, (P, "sqrt", t - 1))
                v.reduce_sum(st[:, 0:1], kb.stage[:, par * D:(par + 1) * D], AX.X)
                em.rec((P, "rsum", t), kb.sDV)
                em.w(v, kb.sDV, (P, "rsum", t))
                v.tensor_scalar_mul(st[:, 2:3], st[:, 0:1], 1.0 / D)
                em.w(v, kb.sAC, (P, "sq", t))
                v.tensor_scalar_mul(st[:, 3:4], st[:, 1:2], 1.0 / D)
                em.rec((P, "m", t), kb.sDV)
                em.w(v, kb.sDV, (P, "m", t))
                v.tensor_mul(st[:, 5:6], st[:, 2:3], st[:, 2:3])
                em.rec((P, "m2", t), kb.sDV)
                em.w(v, kb.sDV, (P, "m2", t))
                v.tensor_sub(st[:, 3:4], st[:, 3:4], st[:, 5:6])
                em.rec((P, "var", t), kb.sDV)
                em.w(v, kb.sAC, (P, "sqrt", t))
                v.reciprocal(st[:, 6:7], st[:, 4:5])
                em.rec((P, "recip", t), kb.sDV)
                em.w(v, kb.sDV, (P, "recip", t))
                if t >= 1:
                    em.w(v, kb.sMM, (P, "tr", t - 1, 3))
                v.tensor_scalar(
                    kb.AR[:, LN_XNT:LN_XNT + D],
                    kb.stage[:, par * D:(par + 1) * D],
                    st[:, 2:3], st[:, 6:7], ALU.subtract, ALU.mult)
                em.rec((P, "affine", t), kb.sDV)
                for b in range(4):
                    em.w(v, kb.sMM, (P, "tr", t, b))
                    v.tensor_copy(
                        bass.AP(kb.xn, (4 * b) * D + t * 128,
                                [[NK * D, 128], [D, 4], [1, 128]]),
                        kb.pTb[:, 0:TW])
                    em.rec((P, "evac", t, b), kb.sDV)
        em.on("vector", ve)

        def te(t_eng):
            for t in range(NT128):
                em.w(t_eng, kb.sDV, (P, "affine", t))
                for b in range(4):
                    gb = 4 * t + b
                    if gb >= 1:
                        pt, pb = divmod(gb - 1, 4)
                        em.w(t_eng, kb.sDV, (P, "evac", pt, pb))
                    ins = None
                    for j in range(4):
                        ins = t_eng.transpose(
                            kb.pTb[:, j * 128:(j + 1) * 128],
                            kb.AR[:, LN_XNT + (4 * b + j) * 128:
                                  LN_XNT + (4 * b + j + 1) * 128],
                            kb.ident[:])
                    kb.sMM.inc(ins)
                    em.rec((P, "tr", t, b), kb.sMM)
        em.on("tensor", te)


def phase_ffn0(kb, em):
    ARf = kb.ARf
    sil_f = ARf[:, FF_SIL // 2:FF_SIL // 2 + TW]
    ds_f = ARf[:, FF_DS // 2:FF_DS // 2 + TW]
    tmp_f = [ARf[:, FF_TMP // 2 + i * TW:FF_TMP // 2 + (i + 1) * TW]
             for i in range(2)]
    wslot = [kb.AR[:, FF_W + i * D:FF_W + (i + 1) * D] for i in range(4)]
    NSL = 11

    P = "f0"
    with em.block("ffn0"):
        def sy(e):
            for hf in range(2):
                for ob in range(IC0):
                    j = hf * IC0 + ob
                    if ob >= 1:
                        em.w(e, kb.sPL, (P, "castA", hf, ob - 1, 1))
                    elif hf == 1:
                        em.w(e, kb.sPL, (P, "castB", 0, 3, 1, NSL - 1))
                    kb.sWL[0].inc(e.dma_start(
                        out=kb.stage[:, 0:D], in_=kb.d["l0_wg"][ob]), 16)
                    em.rec((P, "loadA", hf, ob, 0), kb.sWL[0])
                    kb.sWL[1].inc(e.dma_start(
                        out=kb.stage[:, D:2 * D], in_=kb.d["l0_wu"][ob]), 16)
                    em.rec((P, "loadA", hf, ob, 1), kb.sWL[1])
                for o_sl in range(4):
                    for tg in range(2):
                        for sl in range(NSL):
                            gi = (o_sl * 2 + tg) * NSL + sl
                            par = gi % 2
                            if gi == 0:
                                em.w(e, kb.sPL, (P, "castA", hf, IC0 - 1, 1))
                                em.w(e, kb.sPL, (P, "castA", hf, IC0 - 2, 1))
                            elif gi == 1:
                                pass
                            else:
                                po, pt_, psl = _b_unit(gi - 2, NSL)
                                em.w(e, kb.sPL, (P, "castB", hf, po, pt_, psl))
                            for c in range(4):
                                kb.sWL[par].inc(e.dma_start(
                                    out=kb.stage[:, par * D + c * TW:
                                                 par * D + (c + 1) * TW],
                                    in_=kb.d["l0_wd"][o_sl, sl * 4 + c]), 16)
                            em.rec((P, "loadB", hf, o_sl, tg, sl), kb.sWL[par])
        em.on("sync", sy)

        def gp(g_raw):
            g = E(g_raw, kb.sPL)
            for hf in range(2):
                for ob in range(IC0):
                    j = hf * IC0 + ob
                    par2 = 2 * (j % 2)
                    em.w(g, kb.sWL[0], (P, "loadA", hf, ob, 0))
                    if j >= 2:
                        phf, pob = divmod(j - 2, IC0)
                        em.w(g, kb.sMM, (P, "mmA", phf, pob, 3))
                    if hf == 1 and ob < 2:
                        em.w(g, kb.sMM, (P, "mmB", 0, 3, 1, NSL - 1, 3))
                    g.tensor_copy(wslot[par2][:], kb.stage[:, 0:D])
                    em.rec((P, "castA", hf, ob, 0), kb.sPL)
                    em.w(g, kb.sWL[1], (P, "loadA", hf, ob, 1))
                    g.tensor_copy(wslot[par2 + 1][:], kb.stage[:, D:2 * D])
                    em.rec((P, "castA", hf, ob, 1), kb.sPL)
                for o_sl in range(4):
                    for tg in range(2):
                        for sl in range(NSL):
                            gi = (o_sl * 2 + tg) * NSL + sl
                            par = gi % 2
                            em.w(g, kb.sWL[par], (P, "loadB", hf, o_sl, tg, sl))
                            if gi >= 2:
                                po, pt_, psl = _b_unit(gi - 2, NSL)
                                em.w(g, kb.sMM, (P, "mmB", hf, po, pt_, psl, 3))
                            else:
                                em.w(g, kb.sMM, (P, "mmA", hf, IC0 - 2 + gi, 3))
                            g.tensor_copy(wslot[par][:],
                                          kb.stage[:, par * D:(par + 1) * D])
                            em.rec((P, "castB", hf, o_sl, tg, sl), kb.sPL)
                        for tl in range(4):
                            kslot = kb._ad_idx % 2
                            kb._ad_idx += 1
                            em.w(g_raw, kb.sDV, (P, "ev", hf, o_sl, tg, tl))
                            if hf == 0 and o_sl == 0 and tg == 0 and tl == 0:
                                g_raw.wait_ge(kb.sXS.h, 16)
                            t128 = hf * 8 + tg * 4 + tl
                            kb.sAD[kslot].inc(g_raw.dma_start(
                                out=kb.d["xs"][t128 * 128:(t128 + 1) * 128,
                                               o_sl * TW:(o_sl + 1) * TW],
                                in_=tmp_f[kslot], accum_op=ALU.add), 16)
                            em.rec((P, "ad", hf, o_sl, tg, tl), kb.sAD[kslot])
        em.on("gpsimd", gp)

        def te(t_eng):
            pss = [kb.pA, kb.pB, kb.pC, kb.pD]
            for hf in range(2):
                for ob in range(IC0):
                    j = hf * IC0 + ob
                    par2 = 2 * (j % 2)
                    em.w(t_eng, kb.sPL, (P, "castA", hf, ob, 1))
                    for grp in range(4):
                        wt = wslot[par2 + (0 if grp < 2 else 1)]
                        tw = grp % 2
                        ps = pss[grp]
                        if j >= 1:
                            if grp < 2:
                                em.w(t_eng, kb.sAC, (P, "silu", hf, ob - 1, tw)
                                     if ob >= 1 else (P, "silu", hf - 1, IC0 - 1, tw))
                            else:
                                em.w(t_eng, kb.sDV, (P, "stt", hf, ob - 1, tw)
                                     if ob >= 1 else (P, "stt", hf - 1, IC0 - 1, tw))
                        if hf == 1 and ob < 2:
                            em.w(t_eng, kb.sDV, (P, "ev", 0, 3, 1, grp))
                        ins = None
                        for k in range(NK):
                            ins = t_eng.matmul(
                                ps[:], wt[:, k * 128:(k + 1) * 128],
                                kb.xn[:, k * D + hf * 1024 + tw * TW:
                                      k * D + hf * 1024 + (tw + 1) * TW],
                                start=(k == 0), stop=(k == NK - 1))
                        kb.sMM.inc(ins)
                        em.rec((P, "mmA", hf, ob, grp), kb.sMM)
                for o_sl in range(4):
                    if o_sl > 0 or hf > 0:
                        em.w(t_eng, kb.sDV,
                             (P, "dscp", hf, o_sl - 1) if o_sl > 0
                             else (P, "dscp", hf - 1, 3))
                    kb.sMM.inc(t_eng.matmul(
                        kb.pR[:], kb.onesf_row[:],
                        kb.ds_row[0:1, o_sl * TW:(o_sl + 1) * TW],
                        start=True, stop=True))
                    em.rec((P, "dsmm", hf, o_sl), kb.sMM)
                    for tg in range(2):
                        for sl in range(NSL):
                            gi = (o_sl * 2 + tg) * NSL + sl
                            par = gi % 2
                            em.w(t_eng, kb.sPL, (P, "castB", hf, o_sl, tg, sl))
                            for tl in range(4):
                                ps = pss[tl]
                                if sl == 0 and (tg > 0 or o_sl > 0):
                                    prev = (hf, o_sl, tg - 1, tl) if tg > 0 \
                                        else (hf, o_sl - 1, 1, tl)
                                    em.w(t_eng, kb.sDV, (P, "ev", *prev))
                                elif sl == 0 and o_sl == 0 and tg == 0:
                                    if tl >= 2:
                                        em.w(t_eng, kb.sDV,
                                             (P, "stt", hf, IC0 - 1, tl - 2))
                                    else:
                                        em.w(t_eng, kb.sAC,
                                             (P, "silu", hf, IC0 - 1, tl))
                                    if hf == 1:
                                        em.w(t_eng, kb.sDV, (P, "ev", 0, 3, 1, tl))
                                ins = None
                                for c in range(4):
                                    ic = sl * 4 + c
                                    ins = t_eng.matmul(
                                        ps[:],
                                        kb.AR[:, FF_HDN + ic * 1024 + (tg * 4 + tl) * 128:
                                              FF_HDN + ic * 1024 + (tg * 4 + tl + 1) * 128],
                                        wslot[par][:, c * TW:(c + 1) * TW],
                                        start=(ic == 0), stop=(ic == IC0 - 1))
                                kb.sMM.inc(ins)
                                em.rec((P, "mmB", hf, o_sl, tg, sl, tl), kb.sMM)
        em.on("tensor", te)

        def acf(a_raw):
            a = E(a_raw, kb.sAC)
            for hf in range(2):
                for ob in range(IC0):
                    for tw in range(2):
                        em.w(a, kb.sMM, (P, "mmA", hf, ob, tw))
                        if ob > 0 or tw > 0 or hf > 0:
                            pj = (hf, ob, tw - 1) if tw > 0 else (
                                (hf, ob - 1, 1) if ob > 0 else (hf - 1, IC0 - 1, 1))
                            em.w(a, kb.sDV, (P, "stt", *pj))
                        a.activation(sil_f, [kb.pA, kb.pB][tw][:], AF.Silu,
                                     scale=kb.gs_us[:, ob:ob + 1])
                        em.rec((P, "silu", hf, ob, tw), kb.sAC)
        em.on("scalar", acf)

        def ve(v_raw):
            v = E(v_raw, kb.sDV)
            for hf in range(2):
                for ob in range(IC0):
                    for tw in range(2):
                        em.w(v, kb.sMM, (P, "mmA", hf, ob, 2 + tw))
                        em.w(v, kb.sAC, (P, "silu", hf, ob, tw))
                        v.scalar_tensor_tensor(
                            kb.AR[:, FF_HDN + ob * 1024 + tw * TW:
                                  FF_HDN + ob * 1024 + (tw + 1) * TW],
                            [kb.pC, kb.pD][tw][:],
                            kb.gs_us[:, IC0 + ob:IC0 + ob + 1],
                            sil_f, ALU.mult, ALU.mult)
                        em.rec((P, "stt", hf, ob, tw), kb.sDV)
                for o_sl in range(4):
                    em.w(v, kb.sMM, (P, "dsmm", hf, o_sl))
                    if o_sl > 0 or hf > 0:
                        pk = (hf, o_sl - 1, 1, 3) if o_sl > 0 else (hf - 1, 3, 1, 3)
                        em.w(v, kb.sDV, (P, "ev", *pk))
                    v.tensor_copy(ds_f, kb.pR[:])
                    em.rec((P, "dscp", hf, o_sl), kb.sDV)
                    for tg in range(2):
                        for tl in range(4):
                            kidx = ((hf * 4 + o_sl) * 2 + tg) * 4 + tl
                            kslot = (kb._ad_base + kidx) % 2
                            em.w(v, kb.sDV, (P, "dscp", hf, o_sl))
                            em.w(v, kb.sMM, (P, "mmB", hf, o_sl, tg, NSL - 1, tl))
                            if kidx >= 2:
                                pk = kidx - 2
                                pi = _ev_unit(pk)
                                em.w(v, kb.sAD[kslot], (P, "ad", *pi))
                            v.tensor_tensor(tmp_f[kslot],
                                            [kb.pA, kb.pB, kb.pC, kb.pD][tl][:],
                                            ds_f, ALU.mult)
                            em.rec((P, "ev", hf, o_sl, tg, tl), kb.sDV)
        em.on("vector", ve)


def _b_unit(gi, NSL):
    o_sl, r = divmod(gi, 2 * NSL)
    tg, sl = divmod(r, NSL)
    return o_sl, tg, sl


def _ev_unit(k):
    hf, r = divmod(k, 32)
    o_sl, r2 = divmod(r, 8)
    tg, tl = divmod(r2, 4)
    return hf, o_sl, tg, tl




def _flush_rsmm(kb, em, t_eng, P):
    h, qt = kb._rs_pend.pop(0)
    em.w(t_eng, kb.sDV, (P, "rs", h, qt))
    pq = kb._rsmm_prev
    if pq is not None:
        em.w(t_eng, kb.sDV, (pq[0], "rsbcp", pq[1], pq[2]))
    kb.sMM.inc(t_eng.matmul(
        kb.pR[:], kb.onesf_row[0:1, 0:128],
        kb.ARf[0:1, AT_RS // 2:AT_RS // 2 + TW],
        start=True, stop=True))
    em.rec((P, "rsmm", h, qt), kb.sMM)
    kb._rsmm_prev = (P, h, qt)


def _head_pe(kb, em, t_eng, P, h, cfg):
    S3 = [kb.pA, kb.pB, kb.pC]
    pt = cfg["pt"]
    si = 0
    for qt in range(4):
        ctx_ps = [kb.pD, kb.pE][qt % 2]
        sum_ps = [kb.pF, kb.pT][qt % 2]
        nk = 4 * qt + 4
        for kc in range(nk):
            s_ps = S3[si % 3]
            if si >= 3:
                em.w(t_eng, kb.sAC, (P, "exp", h, si - 3))
            else:
                em.w(t_eng, kb.sAC, cfg["sbank_free"][si])
            if cfg["odd_shift"] and si == 0:
                em.w(t_eng, kb.sSHQ, (P, "qsh", h // 2, 3))
            if si == 0 and cfg.get("ksh_dep"):
                em.w(t_eng, kb.sSHK, cfg["ksh_dep"])
            ins = t_eng.matmul(s_ps[:], cfg["k_ap"](kc), cfg["q_ap"](qt),
                               start=True, stop=False)
            ins = t_eng.matmul(s_ps[:], kb.mones_row[0:1, 0:128],
                               cfg["cpos"][0:1, qt * TW:(qt + 1) * TW],
                               start=False, stop=True)
            kb.sMM.inc(ins)
            em.rec((P, "smm", h, si), kb.sMM)
            if kc == 2 and kb._rs_pend:
                _flush_rsmm(kb, em, t_eng, P)
            if kc >= 4 * qt:
                em.w(t_eng, kb.sPL, (P, "mask", h, si))
            else:
                em.w(t_eng, kb.sAC, (P, "exp", h, si))
            if kc == 0:
                if (h, qt) != cfg["first_hqt"]:
                    pq = (h, qt - 2) if qt >= 2 else (h - 1, qt + 2)
                    if pq[0] >= 0:
                        em.w(t_eng, kb.sDV, (P, "ctxev", *pq))
                em.w(t_eng, kb.sAC, cfg["sum_free"](h, qt))
            dh = cfg["dh"]
            ins = t_eng.matmul(ctx_ps[0:dh, 0:TW], cfg["v_ap"](kc),
                               pt[si % 3], start=(kc == 0), stop=(kc == nk - 1))
            ins = t_eng.matmul(sum_ps[0:1, 0:TW], kb.ones_col[:, 0:1],
                               pt[si % 3], start=(kc == 0), stop=(kc == nk - 1))
            kb.sMM.inc(ins)
            em.rec((P, "cs", h, si), kb.sMM)
            si += 1
        kb._rs_pend.append((h, qt))


def _flush_csh(kb, em, a, a_raw, shtmp):
    P, h, qt, dst = kb._csh_pend.pop(0)
    em.w(a, kb.sDV, (P, "ctxev", h, qt))
    kb.sSHC.inc(a_raw.dma_start(out=dst, in_=shtmp[0:64, 0:TW]), 16)
    em.rec((P, "csh", h, qt), kb.sSHC)


def _head_act(kb, em, a, P, h, cfg):
    si = 0
    for qt in range(4):
        sum_ps = [kb.pF, kb.pT][qt % 2]
        nk = 4 * qt + 4
        for kc in range(nk):
            if kc == 3 and kb._csh_pend:
                _flush_csh(kb, em, a, cfg["a_raw"], cfg["shtmp"])
            em.w(a, kb.sMM, (P, "smm", h, si))
            if si >= 3:
                em.w(a, kb.sMM, (P, "cs", h, si - 3))
            elif h > cfg["first_h"]:
                em.w(a, kb.sMM, (P, "cs", h - 1, 37 + si))
            a.activation(cfg["pt"][si % 3],
                         [kb.pA, kb.pB, kb.pC][si % 3][:], AF.Exp)
            em.rec((P, "exp", h, si), kb.sAC)
            si += 1
        em.w(a, kb.sMM, (P, "cs", h, si - 1))
        if (h, qt) != cfg["first_hqt"]:
            pq = (h, qt - 1) if qt >= 1 else (h - 1, 3)
            em.w(a, kb.sDV, (P, "rs", *pq))
        a.activation(kb.ARf[0:1, AT_SUM // 2:AT_SUM // 2 + TW],
                     sum_ps[0:1, 0:TW], AF.Copy)
        em.rec((P, "sumev", h, qt), kb.sAC)
        if cfg.get("csh_dst"):
            kb._csh_pend.append((P, h, qt, cfg["csh_dst"](h, qt)))


def _head_pool(kb, em, g, P, h, cfg):
    si = 0
    for qt in range(4):
        for kc in range(4 * qt + 4):
            if kc >= 4 * qt:
                m = kc - 4 * qt
                em.w(g, kb.sAC, (P, "exp", h, si))
                pt = cfg["pt"][si % 3]
                g.tensor_tensor(pt, pt,
                                kb.masks[:, m * TW:(m + 1) * TW], ALU.mult)
                em.rec((P, "mask", h, si), kb.sPL)
            si += 1


def _head_dve(kb, em, v, P, h, cfg):
    for qt in range(4):
        ctx_ps = [kb.pD, kb.pE][qt % 2]
        em.w(v, kb.sAC, (P, "sumev", h, qt))
        if (h, qt) != cfg["first_hqt"]:
            pq = (h, qt - 1) if qt >= 1 else (h - 1, 3)
            em.w(v, kb.sMM, (P, "rsmm", *pq))
        v.reciprocal(kb.ARf[0:1, AT_RS // 2:AT_RS // 2 + TW],
                     kb.ARf[0:1, AT_SUM // 2:AT_SUM // 2 + TW])
        em.rec((P, "rs", h, qt), kb.sDV)
        em.w(v, kb.sMM, (P, "rsmm", h, qt))
        if (h, qt) != cfg["first_hqt"]:
            pq = (h, qt - 1) if qt >= 1 else (h - 1, 3)
            em.w(v, kb.sDV, (P, "ctxev", *pq))
        v.tensor_copy(kb.ARf[:, AT_RSB // 2:AT_RSB // 2 + TW], kb.pR[:])
        em.rec((P, "rsbcp", h, qt), kb.sDV)
        em.w(v, kb.sDV, (P, "rsbcp", h, qt))
        em.w(v, kb.sMM, (P, "cs", h, 4 * qt + 3 + sum(4 * q + 4 for q in range(qt))))
        dh = cfg["dh"]
        dst = cfg["ctx_dst"](h, qt)
        if dst is None:
            pv = cfg["shtmp_prev"](h, qt)
            if pv is not None:
                em.w(v, pv[0], pv[1])
            v.tensor_tensor(
                kb.AR[0:dh, AT_KT + TW:AT_KT + 2 * TW],
                ctx_ps[0:dh, 0:TW],
                kb.ARf[0:dh, AT_RSB // 2:AT_RSB // 2 + TW], ALU.mult)
        else:
            v.tensor_tensor(dst, ctx_ps[0:dh, 0:TW],
                            kb.ARf[0:dh, AT_RSB // 2:AT_RSB // 2 + TW],
                            ALU.mult)
        em.rec((P, "ctxev", h, qt), kb.sDV)


def phase_attn0(kb, em):
    AR, ARf = kb.AR, kb.ARf
    kbuf = [AR[:, AT_KG + i * D:AT_KG + (i + 1) * D] for i in range(4)]
    vw = AR[:, AT_W:AT_W + 4096]
    wslot = [AR[:, AT_W + i * D:AT_W + (i + 1) * D] for i in range(2)]
    qpair = [AR[:, AT_Q + i * D:AT_Q + (i + 1) * D] for i in range(2)]
    qodd = AR[:, AT_Q + 2 * D:AT_Q + 3 * D]
    ktmp = AR[:, AT_KT:AT_KT + TW]
    shtmp = AR[:, AT_KT + TW:AT_KT + 2 * TW]
    pt3 = [AR[:, AT_PT:AT_PT + TW], AR[:, AT_PT + TW:AT_PT + 2 * TW],
           AR[:, AT_BIAF:AT_BIAF + TW]]
    cpos = [AR[0:1, AT_NEGB:AT_NEGB + D], AR[0:1, AT_NEGB + D:AT_NEGB + 2 * D]]
    ev_f = [ARf[:, AT_EV // 2 + i * TW:AT_EV // 2 + (i + 1) * TW]
            for i in range(2)]
    st = kb.stats

    for hf in range(ATTN0_HALVES):
        P = ("a0", hf)
        cfg = {
            "dh": 64,
            "first_h": 0,
            "first_hqt": (0, 0),
            "pt": pt3,
        }
        with em.block(f"attn0_{hf}"):
            em.rec((P, "adpre0"), kb.sAD[0])
            em.rec((P, "adpre1"), kb.sAD[1])
            adbase = kb._ad_idx

            def sy(e):
                def load(par, c0, c1, src, key):
                    kb.sWL[par].inc(e.dma_start(
                        out=kb.stage[:, par * D + c0:par * D + c1], in_=src), 16)
                    em.rec(key, kb.sWL[par])

                for kp in range(2):
                    load(kp, 0, D, kb.d["l0_wk"][2 * hf + kp], (P, "wl", "k", kp))
                for c in range(16):
                    em.w(e, kb.sPL, (P, "wc", "k", c % 2) if c < 2
                         else (P, "wc", "v", c - 2))
                    load(c % 2, 0, 256,
                         kb.d["l0_wv"][c * 128:(c + 1) * 128,
                                       hf * 256:(hf + 1) * 256],
                         (P, "wl", "v", c))
                for p in range(8):
                    em.w(e, kb.sPL, (P, "wc", "v", 14 + (p % 2)) if p < 2
                         else (P, "wc", "q", p - 2))
                    load(p % 2, 0, D, kb.d["l0_wq"][8 * hf + p], (P, "wl", "q", p))
                for o_sl in range(4):
                    for sh in range(2):
                        em.w(e, kb.sPL, (P, "wc", "q", 6 + sh) if o_sl == 0
                             else (P, "wc", "wo", o_sl - 1, sh))
                        for c in range(4):
                            load(sh, c * TW, (c + 1) * TW,
                                 kb.d["l0_wo"][o_sl, hf * 8 + sh * 4 + c],
                                 (P, "wl", "wo", o_sl, sh, c))
            em.on("sync", sy)

            def gp(g_raw):
                g = E(g_raw, kb.sPL)
                for i in range(4):
                    g.memset(kbuf[i][64:128, :], 0.0)
                if hf == 0:
                    g.memset(qodd[64:128, :], 0.0)
                em.rec((P, "kz"), kb.sPL)
                for kp in range(2):
                    em.w(g, kb.sWL[kp], (P, "wl", "k", kp))
                    g.tensor_copy(wslot[kp][:], kb.stage[:, kp * D:(kp + 1) * D])
                    em.rec((P, "wc", "k", kp), kb.sPL)
                for c in range(16):
                    em.w(g, kb.sWL[c % 2], (P, "wl", "v", c))
                    em.w(g, kb.sMM, (P, "kmm", c // 8, 3))
                    g.tensor_copy(vw[:, c * 256:(c + 1) * 256],
                                  kb.stage[:, (c % 2) * D:(c % 2) * D + 256])
                    em.rec((P, "wc", "v", c), kb.sPL)
                for p in range(8):
                    em.w(g, kb.sWL[p % 2], (P, "wl", "q", p))
                    em.w(g, kb.sMM, (P, "vmm", 15) if p < 2
                         else (P, "qmm", p - 2, 3))
                    g.tensor_copy(wslot[p % 2][:],
                                  kb.stage[:, (p % 2) * D:(p % 2 + 1) * D])
                    em.rec((P, "wc", "q", p), kb.sPL)
                    for h in (2 * p, 2 * p + 1):
                        _head_pool(kb, em, g, P, h, cfg)
                for o_sl in range(4):
                    for sh in range(2):
                        em.w(g, kb.sWL[sh], (P, "wl", "wo", o_sl, sh, 3))
                        em.w(g, kb.sMM, (P, "qmm", 6 + sh, 3) if o_sl == 0
                             else (P, "womm", o_sl - 1, 15))
                        g.tensor_copy(wslot[sh][:],
                                      kb.stage[:, sh * D:(sh + 1) * D])
                        em.rec((P, "wc", "wo", o_sl, sh), kb.sPL)
                    for t in range(NT128):
                        kslot = kb._ad_idx % 2
                        kb._ad_idx += 1
                        em.w(g_raw, kb.sAC, (P, "woev", o_sl, t))
                        if o_sl == 0 and t == 0:
                            em.w(g_raw, kb.sAD[0], (P, "adpre0"))
                            em.w(g_raw, kb.sAD[1], (P, "adpre1"))
                        kb.sAD[kslot].inc(g_raw.dma_start(
                            out=kb.d["xs"][t * 128:(t + 1) * 128,
                                           o_sl * TW:(o_sl + 1) * TW],
                            in_=ev_f[kslot], accum_op=ALU.add), 16)
                        em.rec((P, "ad", o_sl, t), kb.sAD[kslot])
            em.on("gpsimd", gp)

            def te(t_eng):
                for kp in range(2):
                    em.w(t_eng, kb.sPL, (P, "wc", "k", kp))
                    em.w(t_eng, kb.sPL, (P, "kz"))
                    for tw in range(4):
                        ps = [kb.pA, kb.pB][tw % 2]
                        if tw >= 2:
                            em.w(t_eng, kb.sAC, (P, "ksq", kp, tw - 2, 1))
                        elif kp == 1:
                            em.w(t_eng, kb.sAC, (P, "ksq", 0, 2 + tw, 1))
                        ins = None
                        for k in range(NK):
                            ins = t_eng.matmul(
                                ps[:], wslot[kp][:, k * 128:(k + 1) * 128],
                                kb.xn[:, k * D + tw * TW:k * D + (tw + 1) * TW],
                                start=(k == 0), stop=(k == NK - 1))
                        kb.sMM.inc(ins)
                        em.rec((P, "kmm", kp, tw), kb.sMM)
                        for par in range(2):
                            em.w(t_eng, kb.sAC, (P, "ksq", kp, tw, par))
                            if (kp, tw) != (0, 0):
                                pv = (kp, tw - 1, par) if tw else (kp - 1, 3, par)
                                em.w(t_eng, kb.sDV, (P, "kred", *pv))
                            lo, hi = (0, 64) if par == 0 else (64, 128)
                            kb.sMM.inc(t_eng.matmul(
                                [kb.pF, kb.pT][par][0:1, 0:TW],
                                kb.ones_col[lo:hi, 0:1], ktmp[lo:hi, 0:TW],
                                start=True, stop=True))
                            em.rec((P, "knmm", kp, tw, par), kb.sMM)
                for tc in range(16):
                    if tc == 0:
                        em.w(t_eng, kb.sPL, (P, "wc", "v", 15))
                    ps = [kb.pA, kb.pB][tc % 2]
                    if tc >= 2:
                        em.w(t_eng, kb.sAC, (P, "vev", tc - 2))
                    else:
                        em.w(t_eng, kb.sAC, (P, "ksq", 1, 2 + tc, 1))
                    ins = None
                    for k in range(NK):
                        ins = t_eng.matmul(
                            ps[:, 0:256],
                            kb.xn[:, k * D + tc * 128:k * D + (tc + 1) * 128],
                            vw[:, k * 256:(k + 1) * 256],
                            start=(k == 0), stop=(k == NK - 1))
                    kb.sMM.inc(ins)
                    em.rec((P, "vmm", tc), kb.sMM)
                for p in range(8):
                    em.w(t_eng, kb.sPL, (P, "wc", "q", p))
                    for tw in range(4):
                        ps = [kb.pA, kb.pB][tw % 2]
                        if tw >= 2:
                            em.w(t_eng, kb.sAC, (P, "qsq", p, tw - 2, 1))
                        elif p == 0:
                            em.w(t_eng, kb.sAC, (P, "vev", 14 + tw))
                        else:
                            em.w(t_eng, kb.sAC, (P, "exp", 2 * p - 1, 39))
                        ins = None
                        for k in range(NK):
                            ins = t_eng.matmul(
                                ps[:], wslot[p % 2][:, k * 128:(k + 1) * 128],
                                kb.xn[:, k * D + tw * TW:k * D + (tw + 1) * TW],
                                start=(k == 0), stop=(k == NK - 1))
                        kb.sMM.inc(ins)
                        em.rec((P, "qmm", p, tw), kb.sMM)
                        for par in range(2):
                            em.w(t_eng, kb.sAC, (P, "qsq", p, tw, par))
                            if tw == 0:
                                em.w(t_eng, kb.sAC, (P, "cpos", p - 1, 3, par)
                                     if p else (P, "xk",))
                                if p == 0:
                                    em.w(t_eng, kb.sDV, (P, "kred", 1, 3, par))
                            else:
                                em.w(t_eng, kb.sAC, (P, "cpos", p, tw - 1, par))
                            if p >= 1:
                                em.w(t_eng, kb.sAC, (P, "sumev", 2 * p - 1, 2 + par))
                            lo, hi = (0, 64) if par == 0 else (64, 128)
                            kb.sMM.inc(t_eng.matmul(
                                [kb.pF, kb.pT][par][0:1, 0:TW],
                                kb.ones_col[lo:hi, 0:1], ktmp[lo:hi, 0:TW],
                                start=True, stop=True))
                            em.rec((P, "qnmm", p, tw, par), kb.sMM)
                    for h in (2 * p, 2 * p + 1):
                        kvl = p // 2
                        hcfg = dict(cfg)
                        hcfg["k_ap"] = (lambda kvl_: lambda kc: kbuf[kvl_][
                            :, kc * 128:(kc + 1) * 128])(kvl)
                        hcfg["ksh_dep"] = ((P, "ksh", kvl // 2, 3)
                                           if kvl % 2 else None)
                        hcfg["v_ap"] = (lambda kvl_: lambda kc: AR[
                            :, AT_VG + kc * 256 + kvl_ * 64:
                            AT_VG + kc * 256 + (kvl_ + 1) * 64])(kvl)
                        if h % 2 == 0:
                            hcfg["q_ap"] = (lambda p_: lambda qt: qpair[p_ % 2][
                                :, qt * TW:(qt + 1) * TW])(p)
                            hcfg["odd_shift"] = False
                        else:
                            hcfg["q_ap"] = lambda qt: qodd[:, qt * TW:(qt + 1) * TW]
                            hcfg["odd_shift"] = True
                        hcfg["cpos"] = cpos[h % 2]
                        cl = (P, "cpos", p, 3, 1)
                        hcfg["sbank_free"] = [cl, cl, cl] if h % 2 == 0 \
                            else [(P, "exp", h - 1, 39), (P, "exp", h - 1, 37),
                                  (P, "exp", h - 1, 38)]
                        hcfg["sum_free"] = (lambda p_: lambda hh, qt: (
                            (P, "cpos", p_, 3, qt % 2) if hh % 2 == 0
                            else (P, "sumev", hh - 1, 2 + qt % 2)))(p)
                        _head_pe(kb, em, t_eng, P, h, hcfg)
                while kb._rs_pend:
                    _flush_rsmm(kb, em, t_eng, P)
                for o_sl in range(4):
                    em.w(t_eng, kb.sPL, (P, "wc", "wo", o_sl, 1))
                    if o_sl == 0:
                        em.w(t_eng, kb.sDV, (P, "ctxev", 15, 3))
                        em.w(t_eng, kb.sSHC, (P, "csh", 15, 3))
                    for t in range(NT128):
                        ps = [kb.pA, kb.pB][t % 2]
                        if o_sl == 0 and t < 2:
                            em.w(t_eng, kb.sAC, (P, "exp", 15, 39))
                        else:
                            pv = (o_sl, t - 2) if t >= 2 else (o_sl - 1, 14 + t)
                            em.w(t_eng, kb.sAC, (P, "woev", *pv))
                        ins = None
                        for c in range(8):
                            ins = t_eng.matmul(
                                ps[:],
                                AR[:, AT_CTX + c * D + t * 128:
                                   AT_CTX + c * D + (t + 1) * 128],
                                wslot[c // 4][:, (c % 4) * TW:(c % 4 + 1) * TW],
                                start=(c == 0), stop=(c == 7))
                        kb.sMM.inc(ins)
                        em.rec((P, "womm", o_sl, t), kb.sMM)
            em.on("tensor", te)

            def acf(a_raw):
                a = E(a_raw, kb.sAC)
                em.rec((P, "xk"), kb.sAC)
                for kp in range(2):
                    for tw in range(4):
                        ps = [kb.pA, kb.pB][tw % 2]
                        em.w(a, kb.sMM, (P, "kmm", kp, tw))
                        a.activation(kbuf[2 * kp][0:64, tw * TW:(tw + 1) * TW],
                                     ps[0:64, :], AF.Copy)
                        if (kp, tw) != (0, 0):
                            pv = (kp, tw - 1) if tw else (kp - 1, 3)
                            em.w(a, kb.sSHK, (P, "ksh", *pv))
                        elif hf == 1:
                            em.w(a, kb.sSHC, (("a0", 0), "csh", 15, 3))
                        a.activation(shtmp[64:128, 0:TW], ps[64:128, :], AF.Copy)
                        em.rec((P, "kodd", kp, tw), kb.sAC)
                        em.w(a, kb.sAC, (P, "kodd", kp, tw))
                        kb.sSHK.inc(a_raw.dma_start(
                            out=kbuf[2 * kp + 1][0:64, tw * TW:(tw + 1) * TW],
                            in_=shtmp[64:128, 0:TW]), 16)
                        em.rec((P, "ksh", kp, tw), kb.sSHK)
                        for par in range(2):
                            if (kp, tw, par) != (0, 0, 0):
                                pv = (kp, tw, 0) if par else (
                                    (kp, tw - 1, 1) if tw else (kp - 1, 3, 1))
                                em.w(a, kb.sMM, (P, "knmm", *pv))
                            lo, hi = (0, 64) if par == 0 else (64, 128)
                            a.activation(ktmp[lo:hi, 0:TW], ps[lo:hi, :],
                                         AF.Square)
                            em.rec((P, "ksq", kp, tw, par), kb.sAC)
                for tc in range(16):
                    ps = [kb.pA, kb.pB][tc % 2]
                    em.w(a, kb.sMM, (P, "vmm", tc))
                    a.activation(AR[:, AT_VG + tc * 256:AT_VG + (tc + 1) * 256],
                                 ps[:, 0:256], AF.Copy)
                    em.rec((P, "vev", tc), kb.sAC)
                for p in range(8):
                    for tw in range(4):
                        ps = [kb.pA, kb.pB][tw % 2]
                        em.w(a, kb.sMM, (P, "qmm", p, tw))
                        if p >= 2:
                            em.w(a, kb.sMM, (P, "smm", 2 * (p - 2) + 1, 39))
                            em.w(a, kb.sSHQ, (P, "qsh", p - 2, 3))
                        a.activation(qpair[p % 2][:, tw * TW:(tw + 1) * TW],
                                     ps[:], AF.Copy, scale=0.125)
                        em.rec((P, "qev", p, tw), kb.sAC)
                        if p >= 1 and tw == 0:
                            em.w(a, kb.sMM, (P, "smm", 2 * p - 1, 39))
                        em.w(a, kb.sAC, (P, "qev", p, tw))
                        kb.sSHQ.inc(a_raw.dma_start(
                            out=qodd[0:64, tw * TW:(tw + 1) * TW],
                            in_=qpair[p % 2][64:128, tw * TW:(tw + 1) * TW]), 16)
                        em.rec((P, "qsh", p, tw), kb.sSHQ)
                        for par in range(2):
                            pv = (P, "qnmm", p, tw - 1, par) if tw else \
                                (P, "qnmm", p - 1, 3, par) if p else \
                                (P, "knmm", 1, 3, par)
                            em.w(a, kb.sMM, pv)
                            lo, hi = (0, 64) if par == 0 else (64, 128)
                            a.activation(ktmp[lo:hi, 0:TW], ps[lo:hi, :],
                                         AF.Square, scale=0.125)
                            em.rec((P, "qsq", p, tw, par), kb.sAC)
                        kvl = p // 2
                        for par in range(2):
                            em.w(a, kb.sMM, (P, "qnmm", p, tw, par))
                            if tw == 0:
                                if p >= 1:
                                    em.w(a, kb.sMM, (P, "smm", 2 * p - 1, 39))
                                if p == 0:
                                    em.w(a, kb.sDV, (P, "mx", kvl))
                            a.activation(
                                cpos[par][0:1, tw * TW:(tw + 1) * TW],
                                [kb.pF, kb.pT][par][0:1, 0:TW], AF.Sqrt,
                                scale=st[0:1, 8 + kvl:9 + kvl])
                            em.rec((P, "cpos", p, tw, par), kb.sAC)
                    for h in (2 * p, 2 * p + 1):
                        hcfg = dict(cfg)
                        hcfg["first_hqt"] = (0, 0)
                        hcfg["a_raw"] = a_raw
                        hcfg["shtmp"] = shtmp
                        if h % 2 == 1:
                            hcfg["csh_dst"] = lambda hh, qt: AR[
                                64:128, AT_CTX + (hh // 2) * D + qt * TW:
                                AT_CTX + (hh // 2) * D + (qt + 1) * TW]
                        _head_act(kb, em, a, P, h, hcfg)
                while kb._csh_pend:
                    _flush_csh(kb, em, a, a_raw, shtmp)
                for o_sl in range(4):
                    for t in range(NT128):
                        em.w(a, kb.sMM, (P, "womm", o_sl, t))
                        gi = o_sl * 16 + t
                        kslot = (adbase + gi) % 2
                        if gi >= 2:
                            em.w(a, kb.sAD[kslot], (P, "ad", *divmod(gi - 2, 16)))
                        else:
                            em.w(a, kb.sAD[0], (P, "adpre0"))
                            em.w(a, kb.sAD[1], (P, "adpre1"))
                        a.activation(ev_f[kslot], [kb.pA, kb.pB][t % 2][:],
                                     AF.Copy)
                        em.rec((P, "woev", o_sl, t), kb.sAC)
            em.on("scalar", acf)

            def ve(v_raw):
                v = E(v_raw, kb.sDV)
                for kp in range(2):
                    for tw in range(4):
                        for par in range(2):
                            kvl = 2 * kp + par
                            em.w(v, kb.sMM, (P, "knmm", kp, tw, par))
                            if (kp, tw, par) != (0, 0, 0):
                                pv = (kp, tw, 0) if par else (
                                    (kp, tw - 1, 1) if tw else (kp - 1, 3, 1))
                                em.w(v, kb.sDV, (P, "kred", *pv))
                            v.reduce_max(st[0:1, 12:13],
                                         [kb.pF, kb.pT][par][0:1, 0:TW], AX.X)
                            em.rec((P, "kredr", kp, tw, par), kb.sDV)
                            em.w(v, kb.sDV, (P, "kredr", kp, tw, par))
                            if tw == 0:
                                v.tensor_copy(st[0:1, 8 + kvl:9 + kvl],
                                              st[0:1, 12:13])
                            else:
                                v.tensor_max(st[0:1, 8 + kvl:9 + kvl],
                                             st[0:1, 8 + kvl:9 + kvl],
                                             st[0:1, 12:13])
                            em.rec((P, "kred", kp, tw, par), kb.sDV)
                            if tw == 3:
                                em.rec((P, "mx", kvl), kb.sDV)
                for p in range(8):
                    for h in (2 * p, 2 * p + 1):
                        hcfg = dict(cfg)
                        if h % 2 == 0:
                            hcfg["ctx_dst"] = lambda hh, qt: AR[
                                0:64, AT_CTX + (hh // 2) * D + qt * TW:
                                AT_CTX + (hh // 2) * D + (qt + 1) * TW]
                        else:
                            hcfg["ctx_dst"] = lambda hh, qt: None
                            hcfg["shtmp_prev"] = lambda hh, qt: (
                                (kb.sSHC, (P, "csh", hh, qt - 1)) if qt else
                                ((kb.sSHC, (P, "csh", hh - 2, 3)) if hh >= 3
                                 else (kb.sSHK, (P, "ksh", 1, 3))))
                        _head_dve(kb, em, v, P, h, hcfg)
            em.on("vector", ve)


def phase_attn1(kb, em):
    AR, ARf = kb.AR, kb.ARf
    kbuf = [AR[:, AT_KG + i * D:AT_KG + (i + 1) * D] for i in range(2)]
    vw = AR[:, AT_W:AT_W + 4096]
    wslot = [AR[:, AT_W + i * D:AT_W + (i + 1) * D] for i in range(2)]
    qslot = [AR[:, AT_Q + i * D:AT_Q + (i + 1) * D] for i in range(2)]
    ktmp = AR[:, AT_KT:AT_KT + TW]
    cpos = [AR[0:1, AT_NEGB:AT_NEGB + D], AR[0:1, AT_NEGB + D:AT_NEGB + 2 * D]]
    pt3 = [AR[:, AT_PT:AT_PT + TW], AR[:, AT_PT + TW:AT_PT + 2 * TW],
           AR[:, 54528:54528 + TW]]
    bias_f = ARf[0:1, AT_BIAF // 2:AT_BIAF // 2 + 768]
    bias_b = AR[0:1, AT_BIAB:AT_BIAB + 768]
    bo_b = AR[0:1, AT_BOB:AT_BOB + D]
    ev_f = [ARf[:, AT_EV // 2 + i * TW:AT_EV // 2 + (i + 1) * TW]
            for i in range(2)]
    st = kb.stats
    SC = float(1.0 / np.sqrt(128.0))
    BLOCKS = [(2, "k", 0), (3, "k", 1), (0, "q", 0), (1, "q", 1),
              (4, "v", 0), (5, "v", 1)]

    for hf in range(2):
        P = ("a1", hf)
        with em.block(f"attn1_{hf}"):
            em.rec((P, "adpre0"), kb.sAD[0])
            em.rec((P, "adpre1"), kb.sAD[1])
            adbase = kb._ad_idx

            def sy(e):
                def load(par, c0, c1, src, key):
                    kb.sWL[par].inc(e.dma_start(
                        out=kb.stage[:, par * D + c0:par * D + c1], in_=src), 16)
                    em.rec(key, kb.sWL[par])

                if hf == 0:
                    em.w(e, kb.sAD[0], (P, "adpre0"))
                    em.w(e, kb.sAD[1], (P, "adpre1"))
                    kb.sIN[1].inc(e.dma_start(
                        out=ARf[0:1, AT_EV // 2:AT_EV // 2 + D],
                        in_=kb.d["l1_bo"]), 16)
                    em.rec((P, "bol"), kb.sIN[1])
                for g in range(4):
                    grp = 4 * hf + g
                    if g > 0 or hf > 0:
                        pg = (P, "biasc", g - 1) if g > 0 else \
                            (("a1", 0), "biasc", 3)
                        em.w(e, kb.sPL, pg)
                    kb.sIN[0].inc(e.dma_start(
                        out=bias_f, in_=kb.d["l1_bqkv"][grp:grp + 1, :]), 16)
                    em.rec((P, "biasl", g), kb.sIN[0])
                    for bi, (b, kind, hl) in enumerate(BLOCKS):
                        par = bi % 2
                        if bi >= 2:
                            em.w(e, kb.sPL, (P, "wc", g, bi - 2))
                        elif g > 0:
                            em.w(e, kb.sPL, (P, "wc", g - 1, 4 + bi))
                        elif hf == 1:
                            em.w(e, kb.sPL, (("a1", 0), "wc_wo", 3, bi))
                        load(par, 0, D, kb.d["l1_wqkv"][grp, b],
                             (P, "wl", g, bi))
                for o_sl in range(4):
                    for sh in range(2):
                        em.w(e, kb.sPL, (P, "wc", 3, 4 + sh) if o_sl == 0
                             else (P, "wc_wo", o_sl - 1, sh))
                        for c in range(4):
                            load(sh, c * TW, (c + 1) * TW,
                                 kb.d["l1_wo"][o_sl, hf * 8 + sh * 4 + c],
                                 (P, "wl_wo", o_sl, sh, c))
            em.on("sync", sy)

            if hf == 0:
                def gp_bo(g_raw):
                    g_ = E(g_raw, kb.sPL)
                    em.w(g_, kb.sIN[1], (P, "bol"))
                    g_.tensor_copy(bo_b[:], ARf[0:1, AT_EV // 2:AT_EV // 2 + D])
                    em.rec((P, "boc"), kb.sPL)
                em.on("gpsimd", gp_bo)

            def gp(g_raw):
                g_ = E(g_raw, kb.sPL)
                for g in range(4):
                    em.w(g_, kb.sIN[0], (P, "biasl", g))
                    if g > 0 or hf > 0:
                        pg = (P, "vmm", g - 1, 15) if g > 0 else \
                            (("a1", 0), "vmm", 3, 15)
                        em.w(g_, kb.sMM, pg)
                    g_.tensor_copy(bias_b[:], bias_f)
                    em.rec((P, "biasc", g), kb.sPL)
                    for bi, (b, kind, hl) in enumerate(BLOCKS):
                        par = bi % 2
                        em.w(g_, kb.sWL[par], (P, "wl", g, bi))
                        if kind == "v":
                            em.w(g_, kb.sMM, (P, "qmm", g, 1, 3))
                            if g > 0:
                                em.w(g_, kb.sMM, (P, "vmm", g - 1, 15))
                            g_.tensor_copy(
                                bass.AP(kb.AR, AT_W + hl * 128,
                                        [[57344, 128], [256, NK], [1, 128]]),
                                kb.stage[:, par * D:(par + 1) * D])
                        else:
                            if bi >= 2:
                                pb, pkind, phl = BLOCKS[bi - 2]
                                em.w(g_, kb.sMM,
                                     (P, "kmm" if pkind == "k" else "qmm",
                                      g, phl, 3))
                            elif g > 0:
                                em.w(g_, kb.sMM, (P, "vmm", g - 1, 15))
                            elif hf == 1:
                                em.w(g_, kb.sMM, (("a1", 0), "womm", 3, 15))
                            g_.tensor_copy(wslot[par][:],
                                           kb.stage[:, par * D:(par + 1) * D])
                        em.rec((P, "wc", g, bi), kb.sPL)
                    for hl in range(2):
                        _head_pool(kb, em, g_, P, 2 * g + hl, {"pt": pt3})
                for o_sl in range(4):
                    for sh in range(2):
                        em.w(g_, kb.sWL[sh], (P, "wl_wo", o_sl, sh, 3))
                        em.w(g_, kb.sMM, (P, "vmm", 3, 15) if o_sl == 0
                             else (P, "womm", o_sl - 1, 15))
                        g_.tensor_copy(wslot[sh][:],
                                       kb.stage[:, sh * D:(sh + 1) * D])
                        em.rec((P, "wc_wo", o_sl, sh), kb.sPL)
                    for t in range(NT128):
                        kslot = kb._ad_idx % 2
                        kb._ad_idx += 1
                        em.w(g_raw, kb.sAC, (P, "woev", o_sl, t))
                        if o_sl == 0 and t == 0:
                            em.w(g_raw, kb.sAD[0], (P, "adpre0"))
                            em.w(g_raw, kb.sAD[1], (P, "adpre1"))
                            g_raw.wait_ge(kb.sXS.h, 32)
                        kb.sAD[kslot].inc(g_raw.dma_start(
                            out=kb.d["out"][t * 128:(t + 1) * 128,
                                            o_sl * TW:(o_sl + 1) * TW],
                            in_=ev_f[kslot], accum_op=ALU.add), 16)
                        em.rec((P, "ad", o_sl, t), kb.sAD[kslot])
            em.on("gpsimd", gp)

            def te(t_eng):
                for g in range(4):
                    while kb._rs_pend:
                        _flush_rsmm(kb, em, t_eng, P)
                    for bi, (b, kind, hl) in enumerate(BLOCKS):
                        em.w(t_eng, kb.sPL, (P, "wc", g, bi))
                        em.w(t_eng, kb.sPL, (P, "biasc", g))
                        if kind == "k":
                            for tw in range(4):
                                ps = [kb.pA, kb.pB][tw % 2]
                                if tw >= 2:
                                    em.w(t_eng, kb.sAC, (P, "ksq", g, hl, tw - 2))
                                elif hl == 1:
                                    em.w(t_eng, kb.sAC, (P, "ksq", g, 0, 2 + tw))
                                elif g > 0:
                                    em.w(t_eng, kb.sAC,
                                         (P, "exp", 2 * g - 1, 39))
                                elif hf == 1:
                                    em.w(t_eng, kb.sAC,
                                         (("a1", 0), "woev", 3, 14 + tw))
                                ins = None
                                for k in range(NK):
                                    ins = t_eng.matmul(
                                        ps[:], wslot[bi % 2][:, k * 128:(k + 1) * 128],
                                        kb.xn[:, k * D + tw * TW:
                                              k * D + (tw + 1) * TW],
                                        start=(k == 0), stop=False)
                                ins = t_eng.matmul(
                                    ps[:], bias_b[0:1, (256 + hl * 128):
                                                  (384 + hl * 128)],
                                    kb.ones_row[0:1, 0:TW],
                                    start=False, stop=True)
                                kb.sMM.inc(ins)
                                em.rec((P, "kmm", g, hl, tw), kb.sMM)
                                em.w(t_eng, kb.sAC, (P, "ksq", g, hl, tw))
                                pv = None
                                if tw > 0:
                                    pv = (P, "kred", g, hl, tw - 1)
                                elif hl > 0:
                                    pv = (P, "qred_none",)
                                if tw > 0:
                                    em.w(t_eng, kb.sDV, pv)
                                if tw == 0 and (g > 0 or hf > 0):
                                    pg = (P, "sumev_last", g - 1, 1) if g > 0 \
                                        else (("a1", 0), "sumev_last", 3, 1)
                                    em.w(t_eng, kb.sAC, pg)
                                kb.sMM.inc(t_eng.matmul(
                                    [kb.pF, kb.pT][hl][0:1, 0:TW],
                                    kb.ones_col[:, 0:1], ktmp[:, 0:TW],
                                    start=True, stop=True))
                                em.rec((P, "knmm", g, hl, tw), kb.sMM)
                        elif kind == "q":
                            for tw in range(4):
                                ps = [kb.pA, kb.pB][tw % 2]
                                if tw >= 2:
                                    em.w(t_eng, kb.sAC, (P, "qsq", g, hl, tw - 2))
                                else:
                                    em.w(t_eng, kb.sAC, (P, "ksq", g, 1, 2 + tw))
                                ins = None
                                for k in range(NK):
                                    ins = t_eng.matmul(
                                        ps[:], wslot[bi % 2][:, k * 128:(k + 1) * 128],
                                        kb.xn[:, k * D + tw * TW:
                                              k * D + (tw + 1) * TW],
                                        start=(k == 0), stop=False)
                                ins = t_eng.matmul(
                                    ps[:], bias_b[0:1, hl * 128:(hl + 1) * 128],
                                    kb.ones_row[0:1, 0:TW],
                                    start=False, stop=True)
                                kb.sMM.inc(ins)
                                em.rec((P, "qmm", g, hl, tw), kb.sMM)
                                em.w(t_eng, kb.sAC, (P, "qsq", g, hl, tw))
                                if tw > 0:
                                    em.w(t_eng, kb.sAC, (P, "cpos", g, hl, tw - 1))
                                else:
                                    em.w(t_eng, kb.sDV, (P, "kred", g, hl, 3))
                                kb.sMM.inc(t_eng.matmul(
                                    [kb.pF, kb.pT][hl][0:1, 0:TW],
                                    kb.ones_col[:, 0:1], ktmp[:, 0:TW],
                                    start=True, stop=True))
                                em.rec((P, "qnmm", g, hl, tw), kb.sMM)
                        else:
                            if hl == 0:
                                continue
                            em.w(t_eng, kb.sPL, (P, "wc", g, 5))
                            for tc in range(16):
                                ps = [kb.pA, kb.pB][tc % 2]
                                if tc >= 2:
                                    em.w(t_eng, kb.sAC, (P, "vev", g, tc - 2))
                                else:
                                    em.w(t_eng, kb.sAC, (P, "qsq", g, 1, 2 + tc))
                                ins = None
                                for k in range(NK):
                                    ins = t_eng.matmul(
                                        ps[:, 0:256],
                                        kb.xn[:, k * D + tc * 128:
                                              k * D + (tc + 1) * 128],
                                        vw[:, k * 256:(k + 1) * 256],
                                        start=(k == 0), stop=False)
                                ins = t_eng.matmul(
                                    ps[:, 0:256], kb.ones_row[0:1, 0:128],
                                    bias_b[0:1, 512:768],
                                    start=False, stop=True)
                                kb.sMM.inc(ins)
                                em.rec((P, "vmm", g, tc), kb.sMM)
                    for hl in range(2):
                        h = 2 * g + hl
                        hcfg = {
                            "dh": 128,
                            "first_h": 0,
                            "first_hqt": (0, 0),
                            "odd_shift": False,
                            "k_ap": (lambda hl_: lambda kc: kbuf[hl_][
                                :, kc * 128:(kc + 1) * 128])(hl),
                            "v_ap": (lambda hl_: lambda kc: AR[
                                :, AT_VG + kc * 256 + hl_ * 128:
                                AT_VG + kc * 256 + (hl_ + 1) * 128])(hl),
                            "q_ap": (lambda hl_: lambda qt: qslot[hl_][
                                :, qt * TW:(qt + 1) * TW])(hl),
                            "cpos": cpos[hl],
                            "pt": pt3,
                            "ksh_dep": None,
                            "sbank_free": [
                                (P, "vev", g, 15), (P, "vev", g, 15),
                                (P, "vev", g, 15)]
                            if hl == 0 else
                            [(P, "exp", h - 1, 39), (P, "exp", h - 1, 37),
                             (P, "exp", h - 1, 38)],
                            "sum_free": (lambda g_, hl_: lambda hh, qt: (
                                (P, "cpos", g_, qt % 2, 3)
                                if hl_ == 0 else
                                (P, "sumev", hh - 1, 2 + qt % 2)))(g, hl),
                        }
                        _head_pe(kb, em, t_eng, P, h, hcfg)
                while kb._rs_pend:
                    _flush_rsmm(kb, em, t_eng, P)
                for o_sl in range(4):
                    em.w(t_eng, kb.sPL, (P, "wc_wo", o_sl, 1))
                    if o_sl == 0:
                        em.w(t_eng, kb.sDV, (P, "ctxev", 7, 3))
                    for t in range(NT128):
                        ps = [kb.pA, kb.pB][t % 2]
                        if o_sl == 0 and t < 2:
                            em.w(t_eng, kb.sAC, (P, "exp", 7, 39))
                        else:
                            pv = (o_sl, t - 2) if t >= 2 else (o_sl - 1, 14 + t)
                            em.w(t_eng, kb.sAC, (P, "woev", *pv))
                        ins = None
                        for c in range(8):
                            ins = t_eng.matmul(
                                ps[:],
                                AR[:, AT_CTX + c * D + t * 128:
                                   AT_CTX + c * D + (t + 1) * 128],
                                wslot[c // 4][:, (c % 4) * TW:(c % 4 + 1) * TW],
                                start=(c == 0), stop=(c == 7 and hf == 1))
                        if hf == 0:
                            if o_sl == 0 and t == 0:
                                em.w(t_eng, kb.sPL, (P, "boc"))
                            ins = t_eng.matmul(
                                ps[:], kb.ones_row[0:1, 0:128],
                                bo_b[0:1, o_sl * TW:(o_sl + 1) * TW],
                                start=False, stop=True)
                        kb.sMM.inc(ins)
                        em.rec((P, "womm", o_sl, t), kb.sMM)
            em.on("tensor", te)

            def acf(a_raw):
                a = E(a_raw, kb.sAC)
                for g in range(4):
                    for bi, (b, kind, hl) in enumerate(BLOCKS):
                        if kind == "k":
                            for tw in range(4):
                                ps = [kb.pA, kb.pB][tw % 2]
                                em.w(a, kb.sMM, (P, "kmm", g, hl, tw))
                                if tw == 0 and (g > 0 or hf > 0):
                                    pg = (P, "smm", 2 * (g - 1) + hl, 39) \
                                        if g > 0 else None
                                    if pg:
                                        em.w(a, kb.sMM, pg)
                                a.activation(kbuf[hl][:, tw * TW:(tw + 1) * TW],
                                             ps[:], AF.Copy)
                                em.rec((P, "kev", g, hl, tw), kb.sAC)
                                pv = (P, "knmm", g, hl, tw - 1) if tw else \
                                    ((P, "knmm", g, 0, 3) if hl else
                                     ((P, "qnmm", g - 1, 1, 3) if g else None))
                                if pv:
                                    em.w(a, kb.sMM, pv)
                                a.activation(ktmp[:, 0:TW], ps[:], AF.Square)
                                em.rec((P, "ksq", g, hl, tw), kb.sAC)
                        elif kind == "q":
                            for tw in range(4):
                                ps = [kb.pA, kb.pB][tw % 2]
                                em.w(a, kb.sMM, (P, "qmm", g, hl, tw))
                                if tw == 0 and g > 0:
                                    em.w(a, kb.sMM, (P, "smm", 2 * (g - 1) + hl, 39))
                                a.activation(qslot[hl][:, tw * TW:(tw + 1) * TW],
                                             ps[:], AF.Copy, scale=SC)
                                em.rec((P, "qev", g, hl, tw), kb.sAC)
                                pv = (P, "qnmm", g, hl, tw - 1) if tw else \
                                    (P, "knmm", g, 1, 3)
                                em.w(a, kb.sMM, pv)
                                a.activation(ktmp[:, 0:TW], ps[:], AF.Square,
                                             scale=SC)
                                em.rec((P, "qsq", g, hl, tw), kb.sAC)
                                em.w(a, kb.sMM, (P, "qnmm", g, hl, tw))
                                if tw == 0:
                                    em.w(a, kb.sDV, (P, "mx", g, hl))
                                    if g > 0:
                                        em.w(a, kb.sMM,
                                             (P, "smm", 2 * (g - 1) + hl, 39))
                                a.activation(
                                    cpos[hl][0:1, tw * TW:(tw + 1) * TW],
                                    [kb.pF, kb.pT][hl][0:1, 0:TW], AF.Sqrt,
                                    scale=st[0:1, 8 + hl:9 + hl])
                                em.rec((P, "cpos", g, hl, tw), kb.sAC)
                        else:
                            if hl == 0:
                                continue
                            for tc in range(16):
                                ps = [kb.pA, kb.pB][tc % 2]
                                em.w(a, kb.sMM, (P, "vmm", g, tc))
                                if tc < 2 and g > 0:
                                    em.w(a, kb.sMM, (P, "cs", 2 * g - 1, 39))
                                a.activation(
                                    AR[:, AT_VG + tc * 256:AT_VG + (tc + 1) * 256],
                                    ps[:, 0:256], AF.Copy)
                                em.rec((P, "vev", g, tc), kb.sAC)
                    for hl in range(2):
                        h = 2 * g + hl
                        hcfg = {"first_h": 0, "first_hqt": (0, 0), "pt": pt3,
                                "a_raw": a_raw, "shtmp": None}
                        _head_act(kb, em, a, P, h, hcfg)
                        em.rec((P, "sumev_last", g, hl), kb.sAC)
                for o_sl in range(4):
                    for t in range(NT128):
                        em.w(a, kb.sMM, (P, "womm", o_sl, t))
                        gi = o_sl * 16 + t
                        kslot = (adbase + gi) % 2
                        if gi >= 2:
                            em.w(a, kb.sAD[kslot], (P, "ad", *divmod(gi - 2, 16)))
                        else:
                            em.w(a, kb.sAD[0], (P, "adpre0"))
                            em.w(a, kb.sAD[1], (P, "adpre1"))
                        a.activation(ev_f[kslot], [kb.pA, kb.pB][t % 2][:],
                                     AF.Copy)
                        em.rec((P, "woev", o_sl, t), kb.sAC)
            em.on("scalar", acf)

            def ve(v_raw):
                v = E(v_raw, kb.sDV)
                for g in range(4):
                    for hl in range(2):
                        for tw in range(4):
                            em.w(v, kb.sMM, (P, "knmm", g, hl, tw))
                            if (g, hl, tw) != (0, 0, 0):
                                pv = (P, "kred", g, hl, tw - 1) if tw else (
                                    (P, "kred", g, hl - 1, 3) if hl else
                                    (P, "kred", g - 1, 1, 3))
                                em.w(v, kb.sDV, pv)
                            if tw == 0 and (g > 0 or hf > 0):
                                pg = (P, "cpos", g - 1, hl, 3) if g > 0 else \
                                    (("a1", 0), "cpos", 3, hl, 3)
                                em.w(v, kb.sAC, pg)
                            v.reduce_max(st[0:1, 12:13],
                                         [kb.pF, kb.pT][hl][0:1, 0:TW], AX.X)
                            em.rec((P, "kredr", g, hl, tw), kb.sDV)
                            em.w(v, kb.sDV, (P, "kredr", g, hl, tw))
                            if tw == 0:
                                v.tensor_copy(st[0:1, 8 + hl:9 + hl],
                                              st[0:1, 12:13])
                            else:
                                v.tensor_max(st[0:1, 8 + hl:9 + hl],
                                             st[0:1, 8 + hl:9 + hl],
                                             st[0:1, 12:13])
                            em.rec((P, "kred", g, hl, tw), kb.sDV)
                            if tw == 3:
                                em.rec((P, "mx", g, hl), kb.sDV)
                    for hl in range(2):
                        h = 2 * g + hl
                        hcfg = {
                            "dh": 128,
                            "first_hqt": (0, 0),
                            "ctx_dst": lambda hh, qt: AR[
                                0:128, AT_CTX + hh * D + qt * TW:
                                AT_CTX + hh * D + (qt + 1) * TW],
                        }
                        _head_dve(kb, em, v, P, h, hcfg)
                        if hl == 1:
                            em.rec((P, "rs_last", g), kb.sDV)
            em.on("vector", ve)


def _yev_dep(kb, P, ex, o_sl, tloc):
    if ex == 0:
        return (kb.sAC, (P, "yevA", ex, o_sl, tloc))
    return (kb.sDV, (P, "yev", ex, o_sl, tloc))


def phase_moe(kb, em):
    AR, ARf = kb.AR, kb.ARf
    ms = kb.moe_s
    wfc = [AR[:, MO_WFC + i * D:MO_WFC + (i + 1) * D] for i in range(2)]
    wpj = [AR[:, MO_WPJ + i * 1024:MO_WPJ + (i + 1) * 1024] for i in range(2)]
    tmp_f = [ARf[:, MO_TMP // 2 + i * TW:MO_TMP // 2 + (i + 1) * TW]
             for i in range(2)]
    pss8 = None

    for hf in range(2):
        P = ("mo", hf)
        with em.block(f"moe_{hf}"):
            em.rec((P, "adpre0"), kb.sAD[0])
            em.rec((P, "adpre1"), kb.sAD[1])
            adbase = kb._ad_idx
            pss8 = [kb.pA, kb.pB, kb.pC, kb.pD, kb.pE, kb.pF, kb.pR, kb.pT]

            def sy(e):
                def load(par, c0, c1, src, key):
                    kb.sWL[par].inc(e.dma_start(
                        out=kb.stage[:, par * D + c0:par * D + c1], in_=src), 16)
                    em.rec(key, kb.sWL[par])

                u = 0
                for ex in range(8):
                    for ob in range(16):
                        par = u % 2
                        if u >= 2:
                            em.w(e, kb.sPL, (P, "wc", "fc", ex, ob - 2)
                                 if ob >= 2 else (P, "wc", "pj", ex - 1, 3, 6 + ob))
                        elif hf == 1:
                            em.w(e, kb.sPL, (("mo", 0), "wc", "pj", 7, 3, 6 + ob))
                        load(par, 0, D, kb.d["l1_fc"][ex, ob],
                             (P, "wl", "fc", ex, ob))
                        u += 1
                    for o_sl in range(4):
                        for sh in range(8):
                            par = u % 2
                            if sh >= 2 or o_sl > 0:
                                pv = (o_sl, sh - 2) if sh >= 2 else (o_sl - 1, 6 + sh)
                                em.w(e, kb.sPL, (P, "wc", "pj", ex, *pv))
                            else:
                                em.w(e, kb.sPL, (P, "wc", "fc", ex, 14 + sh))
                            for c in range(2):
                                load(par, c * TW, (c + 1) * TW,
                                     kb.d["l1_pj"][ex, o_sl, sh * 2 + c],
                                     (P, "wl", "pj", ex, o_sl, sh, c))
                            u += 1
            em.on("sync", sy)

            def gp(g_raw):
                g_ = E(g_raw, kb.sPL)
                u = 0
                for ex in range(8):
                    for ob in range(16):
                        par = u % 2
                        em.w(g_, kb.sWL[par], (P, "wl", "fc", ex, ob))
                        if ex > 0 or ob >= 2:
                            pv = (P, "hmm", ex, ob - 2, 1) if ob >= 2 else \
                                (P, "hmm", ex - 1, 14 + ob, 1)
                            em.w(g_, kb.sMM, pv)
                        elif hf == 1:
                            em.w(g_, kb.sMM, (("mo", 0), "hmm", 7, 14 + ob, 1))
                        g_.tensor_copy(wfc[par][:], kb.stage[:, par * D:(par + 1) * D])
                        em.rec((P, "wc", "fc", ex, ob), kb.sPL)
                        u += 1
                    for o_sl in range(4):
                        for sh in range(8):
                            par = u % 2
                            em.w(g_, kb.sWL[par], (P, "wl", "pj", ex, o_sl, sh, 1))
                            gi2 = o_sl * 8 + sh
                            if gi2 >= 2:
                                pv = divmod(gi2 - 2, 8)
                                em.w(g_, kb.sMM, (P, "ymm", ex, pv[0], pv[1], 7))
                            else:
                                em.w(g_, kb.sMM, (P, "hmm", ex, 14 + gi2, 1))
                            g_.tensor_copy(wpj[par][:],
                                           kb.stage[:, par * D:par * D + 1024])
                            em.rec((P, "wc", "pj", ex, o_sl, sh), kb.sPL)
                            u += 1
                    if ex == 7:
                        for tloc in range(8):
                            kslot = kb._ad_idx % 2
                            kb._ad_idx += 1
                            dep = _yev_dep(kb, P, 7, 3, tloc)
                            em.w(g_raw, dep[0], dep[1])
                            if tloc == 0:
                                em.w(g_raw, kb.sAD[0], (P, "adpre0"))
                                em.w(g_raw, kb.sAD[1], (P, "adpre1"))
                                g_raw.wait_ge(kb.sXS.h, 32)
                            t128 = hf * 8 + tloc
                            kb.sAD[kslot].inc(g_raw.dma_start(
                                out=kb.d["out"][t128 * 128:(t128 + 1) * 128, :],
                                in_=ARf[:, tloc * D:(tloc + 1) * D],
                                accum_op=ALU.add), 16)
                            em.rec((P, "ad", tloc), kb.sAD[kslot])
            em.on("gpsimd", gp)

            def te(t_eng):
                for tloc in range(8):
                    if tloc > 0:
                        em.w(t_eng, kb.sDV, (P, "lg", tloc - 1))
                    ins = None
                    for k in range(NK):
                        ins = t_eng.matmul(
                            kb.pR[:, 0:8],
                            kb.xn[:, k * D + hf * 1024 + tloc * 128:
                                  k * D + hf * 1024 + (tloc + 1) * 128],
                            kb.router_b[:, k * 8:(k + 1) * 8],
                            start=(k == 0), stop=(k == NK - 1))
                    kb.sMM.inc(ins)
                    em.rec((P, "rmm", tloc), kb.sMM)
                for ex in range(8):
                    for ob in range(16):
                        par = ob % 2 if False else (ex * 16 + ob) % 2
                        em.w(t_eng, kb.sPL, (P, "wc", "fc", ex, ob))
                        for tw in range(2):
                            ps = [kb.pA, kb.pB][tw]
                            if ex > 0 or ob > 0:
                                if ob:
                                    em.w(t_eng, kb.sDV, (P, "hrel", ex, ob - 1, tw))
                                else:
                                    dep = _yev_dep(kb, P, ex - 1, 3, 6 + tw)
                                    em.w(t_eng, dep[0], dep[1])
                            ins = None
                            for k in range(NK):
                                ins = t_eng.matmul(
                                    ps[:], wfc[par][:, k * 128:(k + 1) * 128],
                                    kb.xn[:, k * D + hf * 1024 + tw * TW:
                                          k * D + hf * 1024 + (tw + 1) * TW],
                                    start=(k == 0), stop=(k == NK - 1))
                            kb.sMM.inc(ins)
                            em.rec((P, "hmm", ex, ob, tw), kb.sMM)
                    for o_sl in range(4):
                        for sh in range(8):
                            par = (o_sl * 8 + sh) % 2
                            em.w(t_eng, kb.sPL, (P, "wc", "pj", ex, o_sl, sh))
                            if sh == 0 and o_sl == 0:
                                em.w(t_eng, kb.sAC, (P, "h2", ex, 15, 1))
                                if ex == 0:
                                    em.w(t_eng, kb.sDV, (P, "wtok", 7))
                            for tloc in range(8):
                                ps = pss8[tloc]
                                if sh == 0:
                                    if o_sl > 0:
                                        dep = _yev_dep(kb, P, ex, o_sl - 1, tloc)
                                        em.w(t_eng, dep[0], dep[1])
                                    elif ex > 0:
                                        dep = _yev_dep(kb, P, ex - 1, 3, tloc)
                                        em.w(t_eng, dep[0], dep[1])
                                    elif tloc == 6:
                                        em.w(t_eng, kb.sDV, (P, "lg", 7))
                                    if ex == 0 and o_sl == 0 and tloc < 2:
                                        em.w(t_eng, kb.sDV,
                                             (P, "hrel", ex, 15, tloc))
                                ins = None
                                for c in range(2):
                                    ic = sh * 2 + c
                                    ins = t_eng.matmul(
                                        ps[:],
                                        AR[:, MO_H2 + ic * 1024 + tloc * 128:
                                           MO_H2 + ic * 1024 + (tloc + 1) * 128],
                                        wpj[par][:, c * TW:(c + 1) * TW],
                                        start=(ic == 0), stop=(ic == 15))
                                kb.sMM.inc(ins)
                                em.rec((P, "ymm", ex, o_sl, sh, tloc), kb.sMM)
            em.on("tensor", te)

            def ve(v_raw):
                v = E(v_raw, kb.sDV)
                for tloc in range(8):
                    em.w(v, kb.sMM, (P, "rmm", tloc))
                    wv = lambda key: em.w(v, kb.sDV, key)
                    v.tensor_copy(ms[:, 0:8], kb.pR[:, 0:8])
                    em.rec((P, "lg", tloc), kb.sDV)
                    wv((P, "lg", tloc))
                    v.reduce_max(ms[:, 32:33], ms[:, 0:8], AX.X)
                    em.rec((P, "m1", tloc), kb.sDV)
                    wv((P, "m1", tloc))
                    v.tensor_scalar(ms[:, 8:16], ms[:, 0:8], ms[:, 32:33],
                                    None, ALU.is_equal)
                    em.rec((P, "eq1", tloc), kb.sDV)
                    wv((P, "eq1", tloc))
                    v.scalar_tensor_tensor(ms[:, 16:24], ms[:, 8:16], -1e30,
                                           ms[:, 0:8], ALU.mult, ALU.add)
                    em.rec((P, "lg2", tloc), kb.sDV)
                    wv((P, "lg2", tloc))
                    v.reduce_max(ms[:, 33:34], ms[:, 16:24], AX.X)
                    em.rec((P, "m2", tloc), kb.sDV)
                    wv((P, "m2", tloc))
                    v.tensor_sub(ms[:, 34:35], ms[:, 32:33], ms[:, 33:34])
                    em.rec((P, "dd", tloc), kb.sDV)
                    wv((P, "dd", tloc))
                    v.tensor_scalar(ms[:, 24:32], ms[:, 16:24], ms[:, 33:34],
                                    None, ALU.is_equal)
                    em.rec((P, "eq2", tloc), kb.sDV)
                    em.w(v, kb.sAC, (P, "p12", tloc))
                    v.tensor_scalar(ms[:, 40:48], ms[:, 8:16], ms[:, 35:36],
                                    None, ALU.mult)
                    em.rec((P, "t1", tloc), kb.sDV)
                    wv((P, "t1", tloc))
                    v.scalar_tensor_tensor(
                        ms[:, 64 + tloc * 8:72 + tloc * 8], ms[:, 24:32],
                        ms[:, 36:37], ms[:, 40:48], ALU.mult, ALU.add)
                    em.rec((P, "wtok", tloc), kb.sDV)
                for ex in range(8):
                    for ob in range(16):
                        for tw in range(2):
                            em.w(v, kb.sMM, (P, "hmm", ex, ob, tw))
                            if ex > 0 or ob > 0:
                                em.w(v, kb.sAC, (P, "h2", ex, ob - 1, tw)
                                     if ob else (P, "h2", ex - 1, 15, tw))
                            v.tensor_scalar_max(tmp_f[tw], [kb.pA, kb.pB][tw][:],
                                                0.0)
                            em.rec((P, "hrel", ex, ob, tw), kb.sDV)
                    for o_sl in range(4):
                        for tloc in range(8):
                            if ex == 0:
                                continue
                            em.w(v, kb.sMM, (P, "ymm", ex, o_sl, 7, tloc))
                            em.w(v, kb.sDV, (P, "wtok", tloc))
                            if ex == 1:
                                em.w(v, kb.sAC, (P, "yevA", 0, o_sl, tloc))
                            else:
                                em.w(v, kb.sDV, (P, "yev", ex - 1, o_sl, tloc))
                            ydst = ARf[:, tloc * D + o_sl * TW:
                                       tloc * D + (o_sl + 1) * TW]
                            v.scalar_tensor_tensor(
                                ydst, pss8[tloc][:],
                                ms[:, 64 + tloc * 8 + ex:65 + tloc * 8 + ex],
                                ydst, ALU.mult, ALU.add)
                            em.rec((P, "yev", ex, o_sl, tloc), kb.sDV)
            em.on("vector", ve)

            def acf(a_raw):
                a = E(a_raw, kb.sAC)
                for tloc in range(8):
                    em.w(a, kb.sDV, (P, "dd", tloc))
                    if tloc > 0:
                        em.w(a, kb.sDV, (P, "t1", tloc - 1))
                        em.w(a, kb.sDV, (P, "wtok", tloc - 1))
                    a.activation(ms[:, 35:36], ms[:, 34:35], AF.Sigmoid)
                    a.activation(ms[:, 36:37], ms[:, 34:35], AF.Sigmoid,
                                 scale=-1.0)
                    em.rec((P, "p12", tloc), kb.sAC)
                for ex in range(8):
                    for ob in range(16):
                        for tw in range(2):
                            em.w(a, kb.sDV, (P, "hrel", ex, ob, tw))
                            a.activation(
                                AR[:, MO_H2 + ob * 1024 + tw * TW:
                                   MO_H2 + ob * 1024 + (tw + 1) * TW],
                                tmp_f[tw], AF.Square)
                            em.rec((P, "h2", ex, ob, tw), kb.sAC)
                    if ex == 0:
                        for o_sl in range(4):
                            for tloc in range(8):
                                em.w(a, kb.sMM, (P, "ymm", 0, o_sl, 7, tloc))
                                em.w(a, kb.sDV, (P, "wtok", tloc))
                                if hf == 1 and o_sl == 0 and tloc == 0:
                                    em.w(a, kb.sAD[0], (P, "adpre0"))
                                    em.w(a, kb.sAD[1], (P, "adpre1"))
                                ydst = ARf[:, tloc * D + o_sl * TW:
                                           tloc * D + (o_sl + 1) * TW]
                                a.activation(ydst, pss8[tloc][:], AF.Copy,
                                             scale=ms[:, 64 + tloc * 8:
                                                      65 + tloc * 8])
                                em.rec((P, "yevA", 0, o_sl, tloc), kb.sAC)
            em.on("scalar", acf)


def phase_final(kb, em):
    with em.block("final"):
        def gp(g_raw):
            em.rec(("fin", "ad0"), kb.sAD[0])
            em.rec(("fin", "ad1"), kb.sAD[1])
            em.w(g_raw, kb.sAD[0], ("fin", "ad0"))
            em.w(g_raw, kb.sAD[1], ("fin", "ad1"))
        em.on("gpsimd", gp)




def _debug_dump(kb, em):
    with em.block("fin"):
        def sy(e):
            for c in range(NK):
                kb.sIN[0].inc(e.dma_start(
                    out=kb.d["out"][c * 128:(c + 1) * 128, 0:D // 2],
                    in_=kb.xn.bitcast(F32)[:, c * D // 2:(c + 1) * D // 2]), 16)
            if not em.dry:
                e.wait_ge(kb.sIN[0].h, kb.sIN[0].v)
        em.on("sync", sy)


def _phase_list(upto):
    phases = [phase_setup,
              lambda kb, em: phase_ln(kb, em, kb.d["x0"], "ln0")]
    if upto == "ln0":
        phases.append(_debug_dump)
        return phases
    phases.append(phase_ffn0)
    if upto == "ffn0":
        phases.append(_debug_dump)
        return phases
    phases.append(phase_attn0)
    if upto == "attn0":
        phases.append(_debug_dump)
        return phases
    phases.append(lambda kb, em: phase_ln(kb, em, kb.d["xs"], "ln1",
                                          wait_ad=True, copy_out=True))
    if upto != "nomoe":
        phases.append(phase_moe)
    phases.append(phase_attn1)
    phases.append(phase_final)
    return phases


def build_nc(upto=None):
    kb = KB()
    kb.nc = bass.Bass("TRN2", target_bir_lowering=False)
    declare_dram(kb)
    with ExitStack() as stack:
        _alloc(kb, stack)
        phases = _phase_list(upto)
        em = Em(kb)
        kb._ad_idx = 0
        kb._ad_base = 0
        kb._rs_pend = []
        kb._rsmm_prev = None
        kb._csh_pend = []
        for ph in phases:
            kb._ad_base = kb._ad_idx
            ph(kb, em)
        assert not kb._rs_pend and not kb._csh_pend
        for s_ in kb.all_sems:
            s_.v = 0
        em.dry = False
        kb._ad_idx = 0
        kb._ad_base = 0
        kb._rs_pend = []
        kb._rsmm_prev = None
        kb._csh_pend = []
        for ph in phases:
            kb._ad_base = kb._ad_idx
            ph(kb, em)
    return kb


def kernel(**inputs):
    ins = host_prep({k: np.asarray(v) for k, v in inputs.items()})
    x_full = np.asarray(inputs["L0_attn_x"], dtype=np.float32)
    kb = build_nc()
    in_maps = []
    for core in range(8):
        m = dict(ins)
        m["x0"] = np.ascontiguousarray(x_full[core])
        in_maps.append(m)
    res = run_bass_kernel_spmd(kb.nc, in_maps, core_ids=list(range(8)))
    return np.stack([r["out"] for r in res.results]).astype(np.float32)


# version 03
# speedup vs baseline: 1.3449x; 1.2773x over previous
import numpy as np
from contextlib import ExitStack, contextmanager

import concourse.bass as bass
import concourse.mybir as mybir
from concourse.bass_utils import run_bass_kernel_spmd

F32 = mybir.dt.float32
BF16 = mybir.dt.bfloat16
AF = mybir.ActivationFunctionType
ALU = mybir.AluOpType
AX = mybir.AxisListType

ATTN0_HALVES = 2
D = 2048
S = 2048
NK = 16
TW = 512
NT128 = S // 128
I0 = 5632
IC0 = I0 // 128
EPS = 1e-5



class Sem:
    def __init__(self, h):
        self.h = h
        self.v = 0

    def inc(self, instr, n=1):
        instr.then_inc(self.h, n)
        self.v += n
        return self.v


class _FakeIns:
    def then_inc(self, h, n=1):
        return self


class _FakeEng:
    def __getattr__(self, name):
        return lambda *a, **k: _FakeIns()


class E:

    def __init__(self, eng, tick):
        self.e = eng
        self.t = tick

    def __getattr__(self, name):
        f = getattr(self.e, name)

        def g(*a, **k):
            return self.t.inc(f(*a, **k))

        return g


class Em:

    def __init__(self, kb):
        self.kb = kb
        self.T = {}
        self.dry = True

    @contextmanager
    def block(self, name):
        if self.dry:
            self._blk = None
            yield
        else:
            with self.kb.nc.Block(name) as b:
                self._blk = b
                yield

    def on(self, engname, f):
        if self.dry:
            f(_FakeEng())
        else:
            getattr(self._blk, engname)(f)

    def rec(self, key, sem):
        if self.dry:
            self.T[key] = sem.v
        else:
            assert self.T.get(key) == sem.v, (key, self.T.get(key), sem.v)
        return sem.v

    def w(self, eng, sem, key, delta=0):
        if self.dry:
            return
        e = eng.e if isinstance(eng, E) else eng
        v = self.T[key] + delta
        if v > 0:
            e.wait_ge(sem.h, v)

    def wv(self, eng, sem, val):
        if self.dry or val <= 0:
            return
        e = eng.e if isinstance(eng, E) else eng
        e.wait_ge(sem.h, val)


class KB:
    pass


def _alloc(kb, stack):
    nc = kb.nc
    sb = lambda name, shape, dt: stack.enter_context(nc.sbuf_tensor(name, shape, dt))
    ps = lambda name, shape, dt: stack.enter_context(nc.psum_tensor(name, shape, dt))
    sem = lambda name: Sem(stack.enter_context(nc.semaphore(name)))

    kb.xn = sb("xn", [128, NK * D], BF16)
    kb.AR = sb("AR", [128, 57344], BF16)
    kb.ARf = kb.AR.bitcast(F32)
    kb.stage = sb("stage", [128, 4096], F32)
    kb.ones_col = sb("ones_col", [128, 1], BF16)
    kb.ones_row = sb("ones_row", [1, 512], BF16)
    kb.mones_row = sb("mones_row", [1, 128], BF16)
    kb.onesf_row = sb("onesf_row", [1, 128], F32)
    kb.ident = sb("ident", [128, 128], BF16)
    kb.masks = sb("masks", [128, 4 * TW], BF16)
    kb.gs_us = sb("gs_us", [128, 2 * IC0], F32)
    kb.stats = sb("stats", [128, 16], F32)
    kb.consts = sb("consts", [128, 4], F32)
    kb.ds_row = sb("ds_row", [1, D], F32)
    kb.router_b = sb("router_b", [128, NK * 8], BF16)
    kb.moe_s = sb("moe_s", [128, 192], F32)
    kb.iota_p = sb("iota_p", [128, 4], F32)

    kb.pA = ps("pA", [128, TW], F32)
    kb.pB = ps("pB", [128, TW], F32)
    kb.pC = ps("pC", [128, TW], F32)
    kb.pD = ps("pD", [128, TW], F32)
    kb.pE = ps("pE", [128, TW], F32)
    kb.pF = ps("pF", [128, TW], F32)
    kb.pR = ps("pR", [128, TW], F32)
    kb.pT = ps("pT", [128, TW], F32)
    kb.pTb = kb.pT.bitcast(BF16)

    kb.sIN = [sem("sIN0"), sem("sIN1")]
    kb.sWL = [sem("sWL0"), sem("sWL1")]
    kb.sSHK = sem("sSHK")
    kb.sSHQ = sem("sSHQ")
    kb.sSHC = sem("sSHC")
    kb.sMM = sem("sMM")
    kb.sAC = sem("sAC")
    kb.sDV = sem("sDV")
    kb.sPL = sem("sPL")
    kb.sAD = [sem("sAD0"), sem("sAD1")]
    kb.sXS = sem("sXS")
    kb.all_sems = ([kb.sIN[0], kb.sIN[1], kb.sWL[0], kb.sWL[1], kb.sSHK,
                    kb.sSHQ, kb.sSHC, kb.sMM, kb.sAC, kb.sDV, kb.sPL,
                    kb.sAD[0], kb.sAD[1], kb.sXS])


LN_XNT = 0
LN_SQD = 2048
LN_IOT = 8192

AT_CTX = 0
AT_KG = 16384
AT_VG = 24576
AT_Q = 28736
AT_PT = 34880
AT_KT = 35904
AT_W = 36928
AT_SUM = 41024
AT_RS = 42048
AT_RSB = 43072
AT_BIAF = 44096
AT_BIAB = 45632
AT_BOB = 46400
AT_EV = 48448
AT_PT3 = 50496

MO_YACC = 0
MO_H2 = 32768
MO_WFC = 49152
MO_WPJ = 53248
MO_TMP = 55296

FF_HDN = 0
FF_W = 45056
FF_DS = 53248
FF_SIL = 54272
FF_TMP = 55296


def host_prep(inputs):
    asc = np.ascontiguousarray

    def lhsT_blocks(wt):
        Fdim, O = wt.shape
        a = wt.reshape(Fdim // 128, 128, O // 128, 128)
        return asc(a.transpose(2, 1, 0, 3).reshape(O // 128, 128, Fdim))

    def rhs_blocks(w):
        K, O = w.shape
        a = w.reshape(K // 128, 128, O // 512, 512)
        return asc(a.transpose(2, 0, 1, 3))

    ins = {}
    ins["l0_wq"] = lhsT_blocks(asc(inputs["L0_attn_Wq"].T))
    ins["l0_wk"] = lhsT_blocks(asc(inputs["L0_attn_Wk"].T))
    ins["l0_wv"] = asc(inputs["L0_attn_Wv"].T)
    ins["l0_wo"] = rhs_blocks(asc(inputs["L0_attn_Wo"].T))
    ins["l0_wg"] = lhsT_blocks(asc(inputs["L0_ffn_gate_w"].T))
    ins["l0_wu"] = lhsT_blocks(asc(inputs["L0_ffn_up_w"].T))
    ins["l0_wd"] = rhs_blocks(asc(inputs["L0_ffn_down_w"].T))
    ins["l0_gs"] = asc(inputs["L0_ffn_gate_s"].reshape(IC0, 128).T)
    ins["l0_us"] = asc(inputs["L0_ffn_up_s"].reshape(IC0, 128).T)
    ins["l0_ds"] = asc(inputs["L0_ffn_down_s"].reshape(1, D))

    wqkv = inputs["L1_attn_Wqkv"].reshape(D, 3, 16, 128)
    groups = []
    for g in range(8):
        h0, h1 = 2 * g, 2 * g + 1
        blk = np.concatenate(
            [wqkv[:, 0, h0], wqkv[:, 0, h1], wqkv[:, 1, h0], wqkv[:, 1, h1],
             wqkv[:, 2, h0], wqkv[:, 2, h1]], axis=1)
        groups.append(lhsT_blocks(asc(blk)))
    ins["l1_wqkv"] = asc(np.stack(groups))
    bq = inputs["L1_attn_bqkv"].reshape(3, 16, 128)
    ins["l1_bqkv"] = asc(np.stack(
        [np.concatenate([bq[0, 2 * g], bq[0, 2 * g + 1], bq[1, 2 * g],
                         bq[1, 2 * g + 1], bq[2, 2 * g], bq[2, 2 * g + 1]])
         for g in range(8)]))
    ins["l1_wo"] = rhs_blocks(asc(inputs["L1_attn_Wo"]))
    ins["l1_bo"] = asc(inputs["L1_attn_bo"].reshape(1, D))
    ins["l1_router"] = asc(inputs["L1_ffn_w_router"].reshape(NK, 128, 8)
                           .transpose(1, 0, 2).reshape(128, NK * 8))
    ins["l1_fc"] = asc(np.stack([lhsT_blocks(inputs["L1_ffn_w_fc"][e])
                                 for e in range(8)]))
    ins["l1_pj"] = asc(np.stack([rhs_blocks(inputs["L1_ffn_w_proj"][e])
                                 for e in range(8)]))
    return ins


def declare_dram(kb):
    nc = kb.nc
    dr = lambda name, shape, kind="ExternalInput": nc.dram_tensor(
        name, list(shape), F32, kind=kind).ap()
    kb.d = {
        "x0": dr("x0", (S, D)),
        "l0_wq": dr("l0_wq", (16, 128, D)),
        "l0_wk": dr("l0_wk", (4, 128, D)),
        "l0_wv": dr("l0_wv", (D, 512)),
        "l0_wo": dr("l0_wo", (4, NK, 128, 512)),
        "l0_wg": dr("l0_wg", (IC0, 128, D)),
        "l0_wu": dr("l0_wu", (IC0, 128, D)),
        "l0_wd": dr("l0_wd", (4, IC0, 128, 512)),
        "l0_gs": dr("l0_gs", (128, IC0)),
        "l0_us": dr("l0_us", (128, IC0)),
        "l0_ds": dr("l0_ds", (1, D)),
        "l1_wqkv": dr("l1_wqkv", (8, 6, 128, D)),
        "l1_bqkv": dr("l1_bqkv", (8, 768)),
        "l1_wo": dr("l1_wo", (4, NK, 128, 512)),
        "l1_bo": dr("l1_bo", (1, D)),
        "l1_router": dr("l1_router", (128, NK * 8)),
        "l1_fc": dr("l1_fc", (8, 16, 128, D)),
        "l1_pj": dr("l1_pj", (8, 4, 16, 128, 512)),
        "xs": dr("xs", (S, D), kind="Internal"),
        "out": dr("out", (S, D), kind="ExternalOutput"),
    }




def phase_setup(kb, em):
    iotaf = kb.ARf[:, LN_IOT // 2:LN_IOT // 2 + TW]
    with em.block("setup"):
        def gp(g_raw):
            g = E(g_raw, kb.sPL)
            kb.sXS.inc(g_raw.dma_start(out=kb.d["xs"], in_=kb.d["x0"]), 16)
            g.memset(kb.ones_col[:], 1.0)
            g.memset(kb.ones_row[:], 1.0)
            g.memset(kb.mones_row[:], -1.0)
            g.memset(kb.onesf_row[:], 1.0)
            g.memset(kb.consts[:, 0:1], EPS)
            g.iota(iotaf, [[1, TW]], channel_multiplier=0,
                   allow_small_or_imprecise_dtypes=True)
            g.iota(kb.iota_p[:, 0:1], [[0, 1]], channel_multiplier=1,
                   allow_small_or_imprecise_dtypes=True)
            em.rec("setup.pl", kb.sPL)
        em.on("gpsimd", gp)

        def ve(v_raw):
            v = E(v_raw, kb.sDV)
            em.w(v_raw, kb.sPL, "setup.pl")
            for m in range(1, 4):
                v.tensor_scalar_add(kb.iota_p[:, m:m + 1], kb.iota_p[:, 0:1],
                                    float(128 * m))
            em.rec("setup.iop", kb.sDV)
            em.w(v_raw, kb.sDV, "setup.iop")
            v.tensor_scalar(kb.ident[:], iotaf[:, 0:128], kb.iota_p[:, 0:1],
                            None, ALU.is_equal)
            for m in range(4):
                v.tensor_scalar(kb.masks[:, m * TW:(m + 1) * TW], iotaf,
                                kb.iota_p[:, m:m + 1], None, ALU.is_ge)
            em.rec("setup.dv", kb.sDV)
        em.on("vector", ve)

        def sy(e):
            kb.sIN[0].inc(e.dma_start(out=kb.gs_us[:, 0:IC0], in_=kb.d["l0_gs"]), 16)
            kb.sIN[0].inc(e.dma_start(out=kb.gs_us[:, IC0:2 * IC0],
                                      in_=kb.d["l0_us"]), 16)
            kb.sIN[0].inc(e.dma_start(out=kb.stage[:, 0:NK * 8],
                                      in_=kb.d["l1_router"]), 16)
            kb.sIN[0].inc(e.dma_start(out=kb.ds_row[:], in_=kb.d["l0_ds"]), 16)
            em.rec("setup.in", kb.sIN[0])
        em.on("sync", sy)

        def gp2(g_raw):
            g = E(g_raw, kb.sPL)
            em.w(g_raw, kb.sIN[0], "setup.in")
            g.tensor_copy(kb.router_b[:], kb.stage[:, 0:NK * 8])
        em.on("gpsimd", gp2)


def phase_ln(kb, em, src, tag, wait_ad=False, copy_out=False):
    st = kb.stats
    P = tag
    with em.block(f"ln_{tag}"):
        if wait_ad:
            em.rec((P, "adpre0"), kb.sAD[0])
            em.rec((P, "adpre1"), kb.sAD[1])

        def sy(e):
            for t in range(NT128):
                par = t % 2
                if t == 0 and wait_ad:
                    em.w(e, kb.sAD[0], (P, "adpre0"))
                    em.w(e, kb.sAD[1], (P, "adpre1"))
                if t >= 2:
                    em.w(e, kb.sDV, (P, "affine", t - 2))
                    em.w(e, kb.sAC, (P, "sq", t - 2))
                kb.sIN[par].inc(e.dma_start(
                    out=kb.stage[:, par * D:(par + 1) * D],
                    in_=src[t * 128:(t + 1) * 128, :]), 16)
                em.rec((P, "load", t), kb.sIN[par])
        em.on("sync", sy)

        if copy_out:
            def gpc(g_raw):
                em.w(g_raw, kb.sAD[0], (P, "adpre0"))
                em.w(g_raw, kb.sAD[1], (P, "adpre1"))
                kb.sXS.inc(g_raw.dma_start(out=kb.d["out"], in_=kb.d["xs"]), 16)
            em.on("gpsimd", gpc)

        def acf(a_raw):
            a = E(a_raw, kb.sAC)
            for t in range(NT128):
                par = t % 2
                em.w(a, kb.sIN[par], (P, "load", t))
                if t >= 1:
                    em.w(a, kb.sDV, (P, "var", t - 1))
                a.activation(kb.AR[:, LN_SQD:LN_SQD + D],
                             kb.stage[:, par * D:(par + 1) * D],
                             AF.Square, accum_out=st[:, 1:2])
                em.rec((P, "sq", t), kb.sAC)
                em.w(a, kb.sDV, (P, "var", t))
                a.activation(st[:, 4:5], st[:, 3:4], AF.Sqrt,
                             bias=kb.consts[:, 0:1])
                em.rec((P, "sqrt", t), kb.sAC)
        em.on("scalar", acf)

        def ve(v_raw):
            v = E(v_raw, kb.sDV)
            for t in range(NT128):
                par = t % 2
                em.w(v, kb.sIN[par], (P, "load", t))
                if t >= 1:
                    em.w(v, kb.sAC, (P, "sqrt", t - 1))
                v.reduce_sum(st[:, 0:1], kb.stage[:, par * D:(par + 1) * D], AX.X)
                em.rec((P, "rsum", t), kb.sDV)
                em.w(v, kb.sDV, (P, "rsum", t))
                v.tensor_scalar_mul(st[:, 2:3], st[:, 0:1], 1.0 / D)
                em.w(v, kb.sAC, (P, "sq", t))
                v.tensor_scalar_mul(st[:, 3:4], st[:, 1:2], 1.0 / D)
                em.rec((P, "m", t), kb.sDV)
                em.w(v, kb.sDV, (P, "m", t))
                v.tensor_mul(st[:, 5:6], st[:, 2:3], st[:, 2:3])
                em.rec((P, "m2", t), kb.sDV)
                em.w(v, kb.sDV, (P, "m2", t))
                v.tensor_sub(st[:, 3:4], st[:, 3:4], st[:, 5:6])
                em.rec((P, "var", t), kb.sDV)
                em.w(v, kb.sAC, (P, "sqrt", t))
                v.reciprocal(st[:, 6:7], st[:, 4:5])
                em.rec((P, "recip", t), kb.sDV)
                em.w(v, kb.sDV, (P, "recip", t))
                if t >= 1:
                    em.w(v, kb.sMM, (P, "tr", t - 1, 3))
                v.tensor_scalar(
                    kb.AR[:, LN_XNT:LN_XNT + D],
                    kb.stage[:, par * D:(par + 1) * D],
                    st[:, 2:3], st[:, 6:7], ALU.subtract, ALU.mult)
                em.rec((P, "affine", t), kb.sDV)
                for b in range(4):
                    em.w(v, kb.sMM, (P, "tr", t, b))
                    v.tensor_copy(
                        bass.AP(kb.xn, (4 * b) * D + t * 128,
                                [[NK * D, 128], [D, 4], [1, 128]]),
                        kb.pTb[:, 0:TW])
                    em.rec((P, "evac", t, b), kb.sDV)
        em.on("vector", ve)

        def te(t_eng):
            for t in range(NT128):
                em.w(t_eng, kb.sDV, (P, "affine", t))
                for b in range(4):
                    gb = 4 * t + b
                    if gb >= 1:
                        pt, pb = divmod(gb - 1, 4)
                        em.w(t_eng, kb.sDV, (P, "evac", pt, pb))
                    ins = None
                    for j in range(4):
                        ins = t_eng.transpose(
                            kb.pTb[:, j * 128:(j + 1) * 128],
                            kb.AR[:, LN_XNT + (4 * b + j) * 128:
                                  LN_XNT + (4 * b + j + 1) * 128],
                            kb.ident[:])
                    kb.sMM.inc(ins)
                    em.rec((P, "tr", t, b), kb.sMM)
        em.on("tensor", te)


def phase_ffn0(kb, em):
    ARf = kb.ARf
    sil_f = ARf[:, FF_SIL // 2:FF_SIL // 2 + TW]
    ds_f = ARf[:, FF_DS // 2:FF_DS // 2 + TW]
    tmp_f = [ARf[:, FF_TMP // 2 + i * TW:FF_TMP // 2 + (i + 1) * TW]
             for i in range(2)]
    wslot = [kb.AR[:, FF_W + i * D:FF_W + (i + 1) * D] for i in range(4)]
    NSL = 11

    P = "f0"
    with em.block("ffn0"):
        def sy(e):
            for hf in range(2):
                for ob in range(IC0):
                    j = hf * IC0 + ob
                    if ob >= 1:
                        em.w(e, kb.sPL, (P, "castA", hf, ob - 1, 1))
                    elif hf == 1:
                        em.w(e, kb.sPL, (P, "castB", 0, 3, 1, NSL - 1))
                    kb.sWL[0].inc(e.dma_start(
                        out=kb.stage[:, 0:D], in_=kb.d["l0_wg"][ob]), 16)
                    em.rec((P, "loadA", hf, ob, 0), kb.sWL[0])
                    kb.sWL[1].inc(e.dma_start(
                        out=kb.stage[:, D:2 * D], in_=kb.d["l0_wu"][ob]), 16)
                    em.rec((P, "loadA", hf, ob, 1), kb.sWL[1])
                for o_sl in range(4):
                    for tg in range(2):
                        for sl in range(NSL):
                            gi = (o_sl * 2 + tg) * NSL + sl
                            par = gi % 2
                            if gi == 0:
                                em.w(e, kb.sPL, (P, "castA", hf, IC0 - 1, 1))
                                em.w(e, kb.sPL, (P, "castA", hf, IC0 - 2, 1))
                            elif gi == 1:
                                pass
                            else:
                                po, pt_, psl = _b_unit(gi - 2, NSL)
                                em.w(e, kb.sPL, (P, "castB", hf, po, pt_, psl))
                            for c in range(4):
                                kb.sWL[par].inc(e.dma_start(
                                    out=kb.stage[:, par * D + c * TW:
                                                 par * D + (c + 1) * TW],
                                    in_=kb.d["l0_wd"][o_sl, sl * 4 + c]), 16)
                            em.rec((P, "loadB", hf, o_sl, tg, sl), kb.sWL[par])
        em.on("sync", sy)

        def gp(g_raw):
            g = E(g_raw, kb.sPL)
            for hf in range(2):
                for ob in range(IC0):
                    j = hf * IC0 + ob
                    par2 = 2 * (j % 2)
                    em.w(g, kb.sWL[0], (P, "loadA", hf, ob, 0))
                    if j >= 2:
                        phf, pob = divmod(j - 2, IC0)
                        em.w(g, kb.sMM, (P, "mmA", phf, pob, 3))
                    if hf == 1 and ob < 2:
                        em.w(g, kb.sMM, (P, "mmB", 0, 3, 1, NSL - 1, 3))
                    g.tensor_copy(wslot[par2][:], kb.stage[:, 0:D])
                    em.rec((P, "castA", hf, ob, 0), kb.sPL)
                    em.w(g, kb.sWL[1], (P, "loadA", hf, ob, 1))
                    g.tensor_copy(wslot[par2 + 1][:], kb.stage[:, D:2 * D])
                    em.rec((P, "castA", hf, ob, 1), kb.sPL)
                for o_sl in range(4):
                    for tg in range(2):
                        for sl in range(NSL):
                            gi = (o_sl * 2 + tg) * NSL + sl
                            par = gi % 2
                            em.w(g, kb.sWL[par], (P, "loadB", hf, o_sl, tg, sl))
                            if gi >= 2:
                                po, pt_, psl = _b_unit(gi - 2, NSL)
                                em.w(g, kb.sMM, (P, "mmB", hf, po, pt_, psl, 3))
                            else:
                                em.w(g, kb.sMM, (P, "mmA", hf, IC0 - 2 + gi, 3))
                            g.tensor_copy(wslot[par][:],
                                          kb.stage[:, par * D:(par + 1) * D])
                            em.rec((P, "castB", hf, o_sl, tg, sl), kb.sPL)
                        for tl in range(4):
                            kslot = kb._ad_idx % 2
                            kb._ad_idx += 1
                            em.w(g_raw, kb.sDV, (P, "ev", hf, o_sl, tg, tl))
                            if hf == 0 and o_sl == 0 and tg == 0 and tl == 0:
                                g_raw.wait_ge(kb.sXS.h, 16)
                            t128 = hf * 8 + tg * 4 + tl
                            kb.sAD[kslot].inc(g_raw.dma_start(
                                out=kb.d["xs"][t128 * 128:(t128 + 1) * 128,
                                               o_sl * TW:(o_sl + 1) * TW],
                                in_=tmp_f[kslot], accum_op=ALU.add), 16)
                            em.rec((P, "ad", hf, o_sl, tg, tl), kb.sAD[kslot])
        em.on("gpsimd", gp)

        def te(t_eng):
            pss = [kb.pA, kb.pB, kb.pC, kb.pD]
            for hf in range(2):
                for ob in range(IC0):
                    j = hf * IC0 + ob
                    par2 = 2 * (j % 2)
                    em.w(t_eng, kb.sPL, (P, "castA", hf, ob, 1))
                    for grp in range(4):
                        wt = wslot[par2 + (0 if grp < 2 else 1)]
                        tw = grp % 2
                        ps = pss[grp]
                        if j >= 1:
                            if grp < 2:
                                em.w(t_eng, kb.sAC, (P, "silu", hf, ob - 1, tw)
                                     if ob >= 1 else (P, "silu", hf - 1, IC0 - 1, tw))
                            else:
                                em.w(t_eng, kb.sDV, (P, "stt", hf, ob - 1, tw)
                                     if ob >= 1 else (P, "stt", hf - 1, IC0 - 1, tw))
                        if hf == 1 and ob < 2:
                            em.w(t_eng, kb.sDV, (P, "ev", 0, 3, 1, grp))
                        ins = None
                        for k in range(NK):
                            ins = t_eng.matmul(
                                ps[:], wt[:, k * 128:(k + 1) * 128],
                                kb.xn[:, k * D + hf * 1024 + tw * TW:
                                      k * D + hf * 1024 + (tw + 1) * TW],
                                start=(k == 0), stop=(k == NK - 1))
                        kb.sMM.inc(ins)
                        em.rec((P, "mmA", hf, ob, grp), kb.sMM)
                for o_sl in range(4):
                    if o_sl > 0 or hf > 0:
                        em.w(t_eng, kb.sDV,
                             (P, "dscp", hf, o_sl - 1) if o_sl > 0
                             else (P, "dscp", hf - 1, 3))
                    kb.sMM.inc(t_eng.matmul(
                        kb.pR[:], kb.onesf_row[:],
                        kb.ds_row[0:1, o_sl * TW:(o_sl + 1) * TW],
                        start=True, stop=True))
                    em.rec((P, "dsmm", hf, o_sl), kb.sMM)
                    for tg in range(2):
                        for sl in range(NSL):
                            gi = (o_sl * 2 + tg) * NSL + sl
                            par = gi % 2
                            em.w(t_eng, kb.sPL, (P, "castB", hf, o_sl, tg, sl))
                            for tl in range(4):
                                ps = pss[tl]
                                if sl == 0 and (tg > 0 or o_sl > 0):
                                    prev = (hf, o_sl, tg - 1, tl) if tg > 0 \
                                        else (hf, o_sl - 1, 1, tl)
                                    em.w(t_eng, kb.sDV, (P, "ev", *prev))
                                elif sl == 0 and o_sl == 0 and tg == 0:
                                    if tl >= 2:
                                        em.w(t_eng, kb.sDV,
                                             (P, "stt", hf, IC0 - 1, tl - 2))
                                    else:
                                        em.w(t_eng, kb.sAC,
                                             (P, "silu", hf, IC0 - 1, tl))
                                    if hf == 1:
                                        em.w(t_eng, kb.sDV, (P, "ev", 0, 3, 1, tl))
                                ins = None
                                for c in range(4):
                                    ic = sl * 4 + c
                                    ins = t_eng.matmul(
                                        ps[:],
                                        kb.AR[:, FF_HDN + ic * 1024 + (tg * 4 + tl) * 128:
                                              FF_HDN + ic * 1024 + (tg * 4 + tl + 1) * 128],
                                        wslot[par][:, c * TW:(c + 1) * TW],
                                        start=(ic == 0), stop=(ic == IC0 - 1))
                                kb.sMM.inc(ins)
                                em.rec((P, "mmB", hf, o_sl, tg, sl, tl), kb.sMM)
        em.on("tensor", te)

        def acf(a_raw):
            a = E(a_raw, kb.sAC)
            for hf in range(2):
                for ob in range(IC0):
                    for tw in range(2):
                        em.w(a, kb.sMM, (P, "mmA", hf, ob, tw))
                        if ob > 0 or tw > 0 or hf > 0:
                            pj = (hf, ob, tw - 1) if tw > 0 else (
                                (hf, ob - 1, 1) if ob > 0 else (hf - 1, IC0 - 1, 1))
                            em.w(a, kb.sDV, (P, "stt", *pj))
                        a.activation(sil_f, [kb.pA, kb.pB][tw][:], AF.Silu,
                                     scale=kb.gs_us[:, ob:ob + 1])
                        em.rec((P, "silu", hf, ob, tw), kb.sAC)
        em.on("scalar", acf)

        def ve(v_raw):
            v = E(v_raw, kb.sDV)
            for hf in range(2):
                for ob in range(IC0):
                    for tw in range(2):
                        em.w(v, kb.sMM, (P, "mmA", hf, ob, 2 + tw))
                        em.w(v, kb.sAC, (P, "silu", hf, ob, tw))
                        v.scalar_tensor_tensor(
                            kb.AR[:, FF_HDN + ob * 1024 + tw * TW:
                                  FF_HDN + ob * 1024 + (tw + 1) * TW],
                            [kb.pC, kb.pD][tw][:],
                            kb.gs_us[:, IC0 + ob:IC0 + ob + 1],
                            sil_f, ALU.mult, ALU.mult)
                        em.rec((P, "stt", hf, ob, tw), kb.sDV)
                for o_sl in range(4):
                    em.w(v, kb.sMM, (P, "dsmm", hf, o_sl))
                    if o_sl > 0 or hf > 0:
                        pk = (hf, o_sl - 1, 1, 3) if o_sl > 0 else (hf - 1, 3, 1, 3)
                        em.w(v, kb.sDV, (P, "ev", *pk))
                    v.tensor_copy(ds_f, kb.pR[:])
                    em.rec((P, "dscp", hf, o_sl), kb.sDV)
                    for tg in range(2):
                        for tl in range(4):
                            kidx = ((hf * 4 + o_sl) * 2 + tg) * 4 + tl
                            kslot = (kb._ad_base + kidx) % 2
                            em.w(v, kb.sDV, (P, "dscp", hf, o_sl))
                            em.w(v, kb.sMM, (P, "mmB", hf, o_sl, tg, NSL - 1, tl))
                            if kidx >= 2:
                                pk = kidx - 2
                                pi = _ev_unit(pk)
                                em.w(v, kb.sAD[kslot], (P, "ad", *pi))
                            v.tensor_tensor(tmp_f[kslot],
                                            [kb.pA, kb.pB, kb.pC, kb.pD][tl][:],
                                            ds_f, ALU.mult)
                            em.rec((P, "ev", hf, o_sl, tg, tl), kb.sDV)
        em.on("vector", ve)


def _b_unit(gi, NSL):
    o_sl, r = divmod(gi, 2 * NSL)
    tg, sl = divmod(r, NSL)
    return o_sl, tg, sl


def _ev_unit(k):
    hf, r = divmod(k, 32)
    o_sl, r2 = divmod(r, 8)
    tg, tl = divmod(r2, 4)
    return hf, o_sl, tg, tl




def _flush_rsmm(kb, em, t_eng, P):
    h, qt = kb._rs_pend.pop(0)
    em.w(t_eng, kb.sDV, (P, "rs", h, qt))
    pq = kb._rsmm_prev
    if pq is not None:
        em.w(t_eng, kb.sDV, (pq[0], "rsbcp", pq[1], pq[2]))
    kb.sMM.inc(t_eng.matmul(
        kb.pR[:], kb.onesf_row[0:1, 0:128],
        kb.ARf[0:1, AT_RS // 2:AT_RS // 2 + TW],
        start=True, stop=True))
    em.rec((P, "rsmm", h, qt), kb.sMM)
    kb._rsmm_prev = (P, h, qt)


_CHUNKS = [(qt, kc) for qt in range(4) for kc in range(4 * qt + 4)]


def _head_pe(kb, em, t_eng, P, h, cfg):
    S3 = [kb.pA, kb.pB, kb.pC]
    pt = cfg["pt"]
    N = len(_CHUNKS)
    LAG = 2
    for j in range(N + LAG):
        if j < N:
            qt, kc = _CHUNKS[j]
            s_ps = S3[j % 3]
            if j >= 3:
                em.w(t_eng, kb.sAC, (P, "exp", h, j - 3))
            else:
                dep = cfg["sbank_free"][j]
                em.w(t_eng, dep[0], dep[1])
            if cfg["odd_shift"] and j == 0:
                em.w(t_eng, kb.sSHQ, (P, "qsh", h // 2, 3))
            if j == 0 and cfg.get("ksh_dep"):
                em.w(t_eng, kb.sSHK, cfg["ksh_dep"])
            kb.sMM.inc(t_eng.matmul(s_ps[:], cfg["k_ap"](kc), cfg["q_ap"](qt),
                                    start=True, stop=True))
            em.rec((P, "smm", h, j), kb.sMM)
        i = j - LAG
        if 0 <= i < N:
            qt, kc = _CHUNKS[i]
            nk = 4 * qt + 4
            ctx_ps = [kb.pD, kb.pE][qt % 2]
            if kb._rs_pend and (kc == 2 or (qt == 0 and kc == 0)):
                _flush_rsmm(kb, em, t_eng, P)
            if kc >= 4 * qt:
                em.w(t_eng, kb.sDV, (P, "mask", h, i))
            else:
                em.w(t_eng, kb.sAC, (P, "exp", h, i))
            if kc == 0:
                if (h, qt) != cfg["first_hqt"]:
                    pq = (h, qt - 2) if qt >= 2 else (h - 1, qt + 2)
                    if pq[0] >= 0:
                        em.w(t_eng, kb.sDV, (P, "ctxev", *pq))
                if cfg["sum_free"] is not None:
                    dep = cfg["sum_free"](h, qt)
                    em.w(t_eng, dep[0], dep[1])
            da = cfg["dh_acc"]
            ins = t_eng.matmul(ctx_ps[0:da, 0:TW], cfg["v_ap"](kc),
                               pt[i % 3], start=(kc == 0), stop=(kc == nk - 1))
            if cfg["sum_free"] is not None:
                sum_ps = [kb.pF, kb.pT][qt % 2]
                ins = t_eng.matmul(sum_ps[0:1, 0:TW], kb.ones_col[:, 0:1],
                                   pt[i % 3], start=(kc == 0),
                                   stop=(kc == nk - 1))
            kb.sMM.inc(ins)
            em.rec((P, "cs", h, i), kb.sMM)
            if kc == nk - 1:
                kb._rs_pend.append((h, qt))


def _flush_csh(kb, em, a, a_raw, shtmp):
    P, h, qt, dst = kb._csh_pend.pop(0)
    em.w(a, kb.sDV, (P, "ctxev", h, qt))
    kb.sSHC.inc(a_raw.dma_start(out=dst, in_=shtmp[0:64, 0:TW]), 16)
    em.rec((P, "csh", h, qt), kb.sSHC)


def _head_act(kb, em, a, P, h, cfg):
    si = 0
    for qt in range(4):
        nk = 4 * qt + 4
        for kc in range(nk):
            if kc == 3 and kb._csh_pend:
                _flush_csh(kb, em, a, cfg["a_raw"], cfg["shtmp"])
            em.w(a, kb.sMM, (P, "smm", h, si))
            if si >= 3:
                em.w(a, kb.sMM, (P, "cs", h, si - 3))
            elif h > cfg["first_h"]:
                em.w(a, kb.sMM, (P, "cs", h - 1, {0: 39, 1: 37, 2: 38}[si]))
            a.activation(cfg["pt"][si % 3],
                         [kb.pA, kb.pB, kb.pC][si % 3][:], AF.Exp,
                         bias=cfg["negb_ap"])
            em.rec((P, "exp", h, si), kb.sAC)
            si += 1
        em.w(a, kb.sMM, (P, "cs", h, si - 1))
        if (h, qt) != cfg["first_hqt"]:
            pq = (h, qt - 1) if qt >= 1 else (h - 1, 3)
            em.w(a, kb.sDV, (P, "rs", *pq))
        if cfg["sum_free"] is None:
            a.activation(kb.ARf[0:1, AT_SUM // 2:AT_SUM // 2 + TW],
                         [kb.pD, kb.pE][qt % 2][64:65, 0:TW], AF.Copy)
        else:
            a.activation(kb.ARf[0:1, AT_SUM // 2:AT_SUM // 2 + TW],
                         [kb.pF, kb.pT][qt % 2][0:1, 0:TW], AF.Copy)
        em.rec((P, "sumev", h, qt), kb.sAC)
        if cfg.get("csh_dst"):
            kb._csh_pend.append((P, h, qt, cfg["csh_dst"](h, qt)))


def _head_dve(kb, em, v, P, h, cfg):
    si = 0
    for qt in range(4):
        ctx_ps = [kb.pD, kb.pE][qt % 2]
        nk = 4 * qt + 4
        for kc in range(nk):
            if kc >= 4 * qt:
                m = kc - 4 * qt
                em.w(v, kb.sAC, (P, "exp", h, si))
                pt = cfg["pt"][si % 3]
                v.tensor_tensor(pt, pt,
                                kb.masks[:, m * TW:(m + 1) * TW], ALU.mult)
                em.rec((P, "mask", h, si), kb.sDV)
            si += 1
        em.w(v, kb.sAC, (P, "sumev", h, qt))
        if (h, qt) != cfg["first_hqt"]:
            pq = (h, qt - 1) if qt >= 1 else (h - 1, 3)
            em.w(v, kb.sMM, (P, "rsmm", *pq))
        v.reciprocal(kb.ARf[0:1, AT_RS // 2:AT_RS // 2 + TW],
                     kb.ARf[0:1, AT_SUM // 2:AT_SUM // 2 + TW])
        em.rec((P, "rs", h, qt), kb.sDV)
        em.w(v, kb.sMM, (P, "rsmm", h, qt))
        if (h, qt) != cfg["first_hqt"]:
            pq = (h, qt - 1) if qt >= 1 else (h - 1, 3)
            em.w(v, kb.sDV, (P, "ctxev", *pq))
        v.tensor_copy(kb.ARf[:, AT_RSB // 2:AT_RSB // 2 + TW], kb.pR[:])
        em.rec((P, "rsbcp", h, qt), kb.sDV)
        em.w(v, kb.sDV, (P, "rsbcp", h, qt))
        em.w(v, kb.sMM, (P, "cs", h, si - 1))
        dh = cfg["dh"]
        dst = cfg["ctx_dst"](h, qt)
        if dst is None:
            pv = cfg["shtmp_prev"](h, qt)
            if pv is not None:
                em.w(v, pv[0], pv[1])
            v.tensor_tensor(
                kb.AR[0:dh, AT_KT + TW:AT_KT + 2 * TW],
                ctx_ps[0:dh, 0:TW],
                kb.ARf[0:dh, AT_RSB // 2:AT_RSB // 2 + TW], ALU.mult)
        else:
            v.tensor_tensor(dst, ctx_ps[0:dh, 0:TW],
                            kb.ARf[0:dh, AT_RSB // 2:AT_RSB // 2 + TW],
                            ALU.mult)
        em.rec((P, "ctxev", h, qt), kb.sDV)


def phase_attn0(kb, em):
    AR, ARf = kb.AR, kb.ARf
    kbuf = [AR[:, AT_KG + i * D:AT_KG + (i + 1) * D] for i in range(4)]
    vw = AR[:, AT_W:AT_W + 4096]
    wslot = [AR[:, AT_W + i * D:AT_W + (i + 1) * D] for i in range(2)]
    qpair = [AR[:, AT_Q + i * D:AT_Q + (i + 1) * D] for i in range(2)]
    qodd = AR[:, AT_Q + 2 * D:AT_Q + 3 * D]
    ktmp = AR[:, AT_KT:AT_KT + TW]
    shtmp = AR[:, AT_KT + TW:AT_KT + 2 * TW]
    pt3 = [AR[:, AT_PT:AT_PT + TW], AR[:, AT_PT + TW:AT_PT + 2 * TW],
           AR[:, AT_PT3:AT_PT3 + TW]]
    ev_f = [ARf[:, AT_EV // 2 + i * TW:AT_EV // 2 + (i + 1) * TW]
            for i in range(2)]
    st = kb.stats

    for hf in range(2):
        P = ("a0", hf)
        with em.block(f"attn0_{hf}"):
            em.rec((P, "adpre0"), kb.sAD[0])
            em.rec((P, "adpre1"), kb.sAD[1])
            adbase = kb._ad_idx

            def sy(e):
                def load(par, c0, c1, src, key):
                    kb.sWL[par].inc(e.dma_start(
                        out=kb.stage[:, par * D + c0:par * D + c1], in_=src), 16)
                    em.rec(key, kb.sWL[par])

                for kp in range(2):
                    load(kp, 0, D, kb.d["l0_wk"][2 * hf + kp], (P, "wl", "k", kp))
                for c in range(16):
                    em.w(e, kb.sPL, (P, "wc", "k", c % 2) if c < 2
                         else (P, "wc", "v", c - 2))
                    load(c % 2, 0, 256,
                         kb.d["l0_wv"][c * 128:(c + 1) * 128,
                                       hf * 256:(hf + 1) * 256],
                         (P, "wl", "v", c))
                for p in range(8):
                    em.w(e, kb.sPL, (P, "wc", "v", 14 + (p % 2)) if p < 2
                         else (P, "wc", "q", p - 2))
                    load(p % 2, 0, D, kb.d["l0_wq"][8 * hf + p], (P, "wl", "q", p))
                for o_sl in range(4):
                    for sh in range(2):
                        em.w(e, kb.sPL, (P, "wc", "q", 6 + sh) if o_sl == 0
                             else (P, "wc", "wo", o_sl - 1, sh))
                        for c in range(4):
                            load(sh, c * TW, (c + 1) * TW,
                                 kb.d["l0_wo"][o_sl, hf * 8 + sh * 4 + c],
                                 (P, "wl", "wo", o_sl, sh, c))
            em.on("sync", sy)

            def gp(g_raw):
                g = E(g_raw, kb.sPL)
                for i in range(4):
                    g.memset(kbuf[i][64:128, :], 0.0)
                if hf == 0:
                    g.memset(qodd[64:128, :], 0.0)
                g.memset(bass.AP(kb.AR, AT_VG + 64,
                                 [[57344, 128], [260, 16], [65, 4]]), 1.0)
                em.rec((P, "kz"), kb.sPL)
                for kp in range(2):
                    em.w(g, kb.sWL[kp], (P, "wl", "k", kp))
                    g.tensor_copy(wslot[kp][:], kb.stage[:, kp * D:(kp + 1) * D])
                    em.rec((P, "wc", "k", kp), kb.sPL)
                for c in range(16):
                    em.w(g, kb.sWL[c % 2], (P, "wl", "v", c))
                    em.w(g, kb.sMM, (P, "kmm", c // 8, 3))
                    g.tensor_copy(vw[:, c * 256:(c + 1) * 256],
                                  kb.stage[:, (c % 2) * D:(c % 2) * D + 256])
                    em.rec((P, "wc", "v", c), kb.sPL)
                for p in range(8):
                    em.w(g, kb.sWL[p % 2], (P, "wl", "q", p))
                    em.w(g, kb.sMM, (P, "vmm", 15) if p < 2
                         else (P, "qmm", p - 2, 3))
                    g.tensor_copy(wslot[p % 2][:],
                                  kb.stage[:, (p % 2) * D:(p % 2 + 1) * D])
                    em.rec((P, "wc", "q", p), kb.sPL)
                for o_sl in range(4):
                    for sh in range(2):
                        em.w(g, kb.sWL[sh], (P, "wl", "wo", o_sl, sh, 3))
                        em.w(g, kb.sMM, (P, "qmm", 6 + sh, 3) if o_sl == 0
                             else (P, "womm", o_sl - 1, 15))
                        g.tensor_copy(wslot[sh][:],
                                      kb.stage[:, sh * D:(sh + 1) * D])
                        em.rec((P, "wc", "wo", o_sl, sh), kb.sPL)
                    for t in range(NT128):
                        kslot = kb._ad_idx % 2
                        kb._ad_idx += 1
                        em.w(g_raw, kb.sAC, (P, "woev", o_sl, t))
                        if o_sl == 0 and t == 0:
                            em.w(g_raw, kb.sAD[0], (P, "adpre0"))
                            em.w(g_raw, kb.sAD[1], (P, "adpre1"))
                        kb.sAD[kslot].inc(g_raw.dma_start(
                            out=kb.d["xs"][t * 128:(t + 1) * 128,
                                           o_sl * TW:(o_sl + 1) * TW],
                            in_=ev_f[kslot], accum_op=ALU.add), 16)
                        em.rec((P, "ad", o_sl, t), kb.sAD[kslot])
            em.on("gpsimd", gp)

            def te(t_eng):
                for kp in range(2):
                    em.w(t_eng, kb.sPL, (P, "wc", "k", kp))
                    em.w(t_eng, kb.sPL, (P, "kz"))
                    for tw in range(4):
                        ps = [kb.pA, kb.pB][tw % 2]
                        if tw >= 2:
                            em.w(t_eng, kb.sAC, (P, "ksq", kp, tw - 2, 1))
                        elif kp == 1:
                            em.w(t_eng, kb.sAC, (P, "ksq", 0, 2 + tw, 1))
                        ins = None
                        for k in range(NK):
                            ins = t_eng.matmul(
                                ps[:], wslot[kp][:, k * 128:(k + 1) * 128],
                                kb.xn[:, k * D + tw * TW:k * D + (tw + 1) * TW],
                                start=(k == 0), stop=(k == NK - 1))
                        kb.sMM.inc(ins)
                        em.rec((P, "kmm", kp, tw), kb.sMM)
                        for par in range(2):
                            em.w(t_eng, kb.sAC, (P, "ksq", kp, tw, par))
                            if (kp, tw) != (0, 0):
                                pv = (kp, tw - 1, par) if tw else (kp - 1, 3, par)
                                em.w(t_eng, kb.sDV, (P, "kred", *pv))
                            lo, hi = (0, 64) if par == 0 else (64, 128)
                            kb.sMM.inc(t_eng.matmul(
                                [kb.pF, kb.pT][par][0:1, 0:TW],
                                kb.ones_col[lo:hi, 0:1], ktmp[lo:hi, 0:TW],
                                start=True, stop=True))
                            em.rec((P, "knmm", kp, tw, par), kb.sMM)
                for tc in range(16):
                    if tc == 0:
                        em.w(t_eng, kb.sPL, (P, "wc", "v", 15))
                    ps = [kb.pA, kb.pB][tc % 2]
                    if tc >= 2:
                        em.w(t_eng, kb.sAC, (P, "vev", tc - 2))
                    else:
                        em.w(t_eng, kb.sAC, (P, "ksq", 1, 2 + tc, 1))
                    ins = None
                    for k in range(NK):
                        ins = t_eng.matmul(
                            ps[:, 0:256],
                            kb.xn[:, k * D + tc * 128:k * D + (tc + 1) * 128],
                            vw[:, k * 256:(k + 1) * 256],
                            start=(k == 0), stop=(k == NK - 1))
                    kb.sMM.inc(ins)
                    em.rec((P, "vmm", tc), kb.sMM)
                for p in range(8):
                    kvl = p // 2
                    while kb._rs_pend:
                        _flush_rsmm(kb, em, t_eng, P)
                    em.w(t_eng, kb.sPL, (P, "wc", "q", p))
                    for tw in range(4):
                        ps = [kb.pA, kb.pB][tw % 2]
                        if tw >= 2:
                            em.w(t_eng, kb.sAC, (P, "qsq", p, tw - 2, 1))
                        elif p == 0:
                            em.w(t_eng, kb.sAC, (P, "vev", 14 + tw))
                        else:
                            em.w(t_eng, kb.sAC, (P, "exp", 2 * p - 1, 39))
                        ins = None
                        for k in range(NK):
                            ins = t_eng.matmul(
                                ps[:], wslot[p % 2][:, k * 128:(k + 1) * 128],
                                kb.xn[:, k * D + tw * TW:k * D + (tw + 1) * TW],
                                start=(k == 0), stop=(k == NK - 1))
                        kb.sMM.inc(ins)
                        em.rec((P, "qmm", p, tw), kb.sMM)
                        for par in range(2):
                            em.w(t_eng, kb.sAC, (P, "qsq", p, tw, par))
                            if tw > 0:
                                em.w(t_eng, kb.sDV, (P, "qred", p, tw - 1, par))
                            elif p > 0:
                                em.w(t_eng, kb.sDV, (P, "qred", p - 1, 3, par))
                            else:
                                em.w(t_eng, kb.sDV, (P, "kred", 1, 3, par))
                            lo, hi = (0, 64) if par == 0 else (64, 128)
                            kb.sMM.inc(t_eng.matmul(
                                [kb.pF, kb.pT][par][0:1, 0:TW],
                                kb.ones_col[lo:hi, 0:1], ktmp[lo:hi, 0:TW],
                                start=True, stop=True))
                            em.rec((P, "qnmm", p, tw, par), kb.sMM)
                    for par in range(2):
                        em.w(t_eng, kb.sAC, (P, "csqrt", p, par))
                        if p > 0 and par == 0:
                            em.w(t_eng, kb.sAC, (P, "exp", 2 * p - 1, 38))
                        kb.sMM.inc(t_eng.matmul(
                            kb.pC[0:128, 0:1], kb.onesf_row[0:1, 0:128],
                            st[0:1, 2 + par:3 + par], start=True, stop=True))
                        em.rec((P, "negbmm", p, par), kb.sMM)
                    for h in (2 * p, 2 * p + 1):
                        hcfg = {
                            "dh": 64,
                            "dh_acc": 65,
                            "first_h": 0,
                            "first_hqt": (0, 0),
                            "pt": pt3,
                            "sum_free": None,
                            "k_ap": (lambda kvl_: lambda kc: kbuf[kvl_][
                                :, kc * 128:(kc + 1) * 128])(kvl),
                            "v_ap": (lambda kvl_: lambda kc: AR[
                                :, AT_VG + kc * 260 + kvl_ * 65:
                                AT_VG + kc * 260 + kvl_ * 65 + 65])(kvl),
                            "ksh_dep": ((P, "ksh", kvl // 2, 3)
                                        if kvl % 2 else None),
                        }
                        if h % 2 == 0:
                            hcfg["q_ap"] = (lambda p_: lambda qt: qpair[p_ % 2][
                                :, qt * TW:(qt + 1) * TW])(p)
                            hcfg["odd_shift"] = False
                        else:
                            hcfg["q_ap"] = lambda qt: qodd[:, qt * TW:(qt + 1) * TW]
                            hcfg["odd_shift"] = True
                        nb = (kb.sAC, (P, "negb", p, 1))
                        hcfg["sbank_free"] = [nb, nb, nb] if h % 2 == 0 \
                            else [(kb.sAC, (P, "exp", h - 1, 39)),
                                  (kb.sAC, (P, "exp", h - 1, 37)),
                                  (kb.sAC, (P, "exp", h - 1, 38))]
                        _head_pe(kb, em, t_eng, P, h, hcfg)
                while kb._rs_pend:
                    _flush_rsmm(kb, em, t_eng, P)
                for o_sl in range(4):
                    em.w(t_eng, kb.sPL, (P, "wc", "wo", o_sl, 1))
                    if o_sl == 0:
                        em.w(t_eng, kb.sDV, (P, "ctxev", 15, 3))
                        em.w(t_eng, kb.sSHC, (P, "csh", 15, 3))
                    for t in range(NT128):
                        ps = [kb.pA, kb.pB][t % 2]
                        if o_sl == 0 and t < 2:
                            em.w(t_eng, kb.sAC, (P, "exp", 15, 39))
                        else:
                            pv = (o_sl, t - 2) if t >= 2 else (o_sl - 1, 14 + t)
                            em.w(t_eng, kb.sAC, (P, "woev", *pv))
                        ins = None
                        for c in range(8):
                            ins = t_eng.matmul(
                                ps[:],
                                AR[:, AT_CTX + c * D + t * 128:
                                   AT_CTX + c * D + (t + 1) * 128],
                                wslot[c // 4][:, (c % 4) * TW:(c % 4 + 1) * TW],
                                start=(c == 0), stop=(c == 7))
                        kb.sMM.inc(ins)
                        em.rec((P, "womm", o_sl, t), kb.sMM)
            em.on("tensor", te)

            def acf(a_raw):
                a = E(a_raw, kb.sAC)
                em.rec((P, "xk"), kb.sAC)
                for kp in range(2):
                    for tw in range(4):
                        ps = [kb.pA, kb.pB][tw % 2]
                        em.w(a, kb.sMM, (P, "kmm", kp, tw))
                        a.activation(kbuf[2 * kp][0:64, tw * TW:(tw + 1) * TW],
                                     ps[0:64, :], AF.Copy)
                        if (kp, tw) != (0, 0):
                            pv = (kp, tw - 1) if tw else (kp - 1, 3)
                            em.w(a, kb.sSHK, (P, "ksh", *pv))
                        elif hf == 1:
                            em.w(a, kb.sSHC, (("a0", 0), "csh", 15, 3))
                        a.activation(shtmp[64:128, 0:TW], ps[64:128, :], AF.Copy)
                        em.rec((P, "kodd", kp, tw), kb.sAC)
                        em.w(a, kb.sAC, (P, "kodd", kp, tw))
                        kb.sSHK.inc(a_raw.dma_start(
                            out=kbuf[2 * kp + 1][0:64, tw * TW:(tw + 1) * TW],
                            in_=shtmp[64:128, 0:TW]), 16)
                        em.rec((P, "ksh", kp, tw), kb.sSHK)
                        for par in range(2):
                            if (kp, tw, par) != (0, 0, 0):
                                pv = (kp, tw, 0) if par else (
                                    (kp, tw - 1, 1) if tw else (kp - 1, 3, 1))
                                em.w(a, kb.sMM, (P, "knmm", *pv))
                            lo, hi = (0, 64) if par == 0 else (64, 128)
                            a.activation(ktmp[lo:hi, 0:TW], ps[lo:hi, :],
                                         AF.Square)
                            em.rec((P, "ksq", kp, tw, par), kb.sAC)
                for tc in range(16):
                    ps = [kb.pA, kb.pB][tc % 2]
                    em.w(a, kb.sMM, (P, "vmm", tc))
                    a.activation(
                        bass.AP(kb.AR, AT_VG + tc * 260,
                                [[57344, 128], [65, 4], [1, 64]]),
                        bass.AP(kb.pA if tc % 2 == 0 else kb.pB, 0,
                                [[512, 128], [64, 4], [1, 64]]),
                        AF.Copy)
                    em.rec((P, "vev", tc), kb.sAC)
                for p in range(8):
                    kvl = p // 2
                    for tw in range(4):
                        ps = [kb.pA, kb.pB][tw % 2]
                        em.w(a, kb.sMM, (P, "qmm", p, tw))
                        if p >= 2:
                            em.w(a, kb.sMM, (P, "smm", 2 * (p - 2) + 1, 39))
                            em.w(a, kb.sSHQ, (P, "qsh", p - 2, 3))
                        a.activation(qpair[p % 2][:, tw * TW:(tw + 1) * TW],
                                     ps[:], AF.Copy, scale=0.125)
                        em.rec((P, "qev", p, tw), kb.sAC)
                        if p >= 1 and tw == 0:
                            em.w(a, kb.sMM, (P, "smm", 2 * p - 1, 39))
                        em.w(a, kb.sAC, (P, "qev", p, tw))
                        kb.sSHQ.inc(a_raw.dma_start(
                            out=qodd[0:64, tw * TW:(tw + 1) * TW],
                            in_=qpair[p % 2][64:128, tw * TW:(tw + 1) * TW]), 16)
                        em.rec((P, "qsh", p, tw), kb.sSHQ)
                        for par in range(2):
                            pv = (P, "qnmm", p, tw - 1, par) if tw else \
                                (P, "qnmm", p - 1, 3, par) if p else \
                                (P, "knmm", 1, 3, par)
                            em.w(a, kb.sMM, pv)
                            lo, hi = (0, 64) if par == 0 else (64, 128)
                            a.activation(ktmp[lo:hi, 0:TW], ps[lo:hi, :],
                                         AF.Square, scale=0.125)
                            em.rec((P, "qsq", p, tw, par), kb.sAC)
                    for par in range(2):
                        em.w(a, kb.sDV, (P, "qred", p, 3, par))
                        if p > 0:
                            em.w(a, kb.sMM, (P, "negbmm", p - 1, par))
                        a.activation(st[0:1, 2 + par:3 + par],
                                     st[0:1, par:par + 1], AF.Sqrt,
                                     scale=st[0:1, 8 + kvl:9 + kvl])
                        em.rec((P, "csqrt", p, par), kb.sAC)
                        em.w(a, kb.sMM, (P, "negbmm", p, par))
                        a.activation(st[:, 13 + par:14 + par], kb.pC[:, 0:1],
                                     AF.Copy, scale=-1.0)
                        em.rec((P, "negb", p, par), kb.sAC)
                    for h in (2 * p, 2 * p + 1):
                        hcfg = {
                            "first_h": 0, "first_hqt": (0, 0), "pt": pt3,
                            "a_raw": a_raw, "shtmp": shtmp,
                            "sum_free": None,
                            "negb_ap": st[:, 13 + h % 2:14 + h % 2],
                        }
                        if h % 2 == 1:
                            hcfg["csh_dst"] = lambda hh, qt: AR[
                                64:128, AT_CTX + (hh // 2) * D + qt * TW:
                                AT_CTX + (hh // 2) * D + (qt + 1) * TW]
                        _head_act(kb, em, a, P, h, hcfg)
                while kb._csh_pend:
                    _flush_csh(kb, em, a, a_raw, shtmp)
                for o_sl in range(4):
                    for t in range(NT128):
                        em.w(a, kb.sMM, (P, "womm", o_sl, t))
                        gi = o_sl * 16 + t
                        kslot = (adbase + gi) % 2
                        if gi >= 2:
                            em.w(a, kb.sAD[kslot], (P, "ad", *divmod(gi - 2, 16)))
                        else:
                            em.w(a, kb.sAD[0], (P, "adpre0"))
                            em.w(a, kb.sAD[1], (P, "adpre1"))
                        a.activation(ev_f[kslot], [kb.pA, kb.pB][t % 2][:],
                                     AF.Copy)
                        em.rec((P, "woev", o_sl, t), kb.sAC)
            em.on("scalar", acf)

            def ve(v_raw):
                v = E(v_raw, kb.sDV)
                for kp in range(2):
                    for tw in range(4):
                        for par in range(2):
                            kvl = 2 * kp + par
                            em.w(v, kb.sMM, (P, "knmm", kp, tw, par))
                            if (kp, tw, par) != (0, 0, 0):
                                pv = (kp, tw, 0) if par else (
                                    (kp, tw - 1, 1) if tw else (kp - 1, 3, 1))
                                em.w(v, kb.sDV, (P, "kred", *pv))
                            v.reduce_max(st[0:1, 12:13],
                                         [kb.pF, kb.pT][par][0:1, 0:TW], AX.X)
                            em.rec((P, "kredr", kp, tw, par), kb.sDV)
                            em.w(v, kb.sDV, (P, "kredr", kp, tw, par))
                            if tw == 0:
                                v.tensor_copy(st[0:1, 8 + kvl:9 + kvl],
                                              st[0:1, 12:13])
                            else:
                                v.tensor_max(st[0:1, 8 + kvl:9 + kvl],
                                             st[0:1, 8 + kvl:9 + kvl],
                                             st[0:1, 12:13])
                            em.rec((P, "kred", kp, tw, par), kb.sDV)
                for p in range(8):
                    for tw in range(4):
                        for par in range(2):
                            em.w(v, kb.sMM, (P, "qnmm", p, tw, par))
                            pv = (P, "qred", p, tw, 0) if par else (
                                (P, "qred", p, tw - 1, 1) if tw else (
                                    (P, "qred", p - 1, 3, 1) if p else
                                    (P, "kred", 1, 3, 1)))
                            em.w(v, kb.sDV, pv)
                            if tw == 0 and p > 0:
                                em.w(v, kb.sAC, (P, "csqrt", p - 1, par))
                            v.reduce_max(st[0:1, 12:13],
                                         [kb.pF, kb.pT][par][0:1, 0:TW], AX.X)
                            em.rec((P, "qredr", p, tw, par), kb.sDV)
                            em.w(v, kb.sDV, (P, "qredr", p, tw, par))
                            if tw == 0:
                                v.tensor_copy(st[0:1, par:par + 1],
                                              st[0:1, 12:13])
                            else:
                                v.tensor_max(st[0:1, par:par + 1],
                                             st[0:1, par:par + 1],
                                             st[0:1, 12:13])
                            em.rec((P, "qred", p, tw, par), kb.sDV)
                    for h in (2 * p, 2 * p + 1):
                        hcfg = {"dh": 64, "first_hqt": (0, 0), "pt": pt3}
                        if h % 2 == 0:
                            hcfg["ctx_dst"] = lambda hh, qt: AR[
                                0:64, AT_CTX + (hh // 2) * D + qt * TW:
                                AT_CTX + (hh // 2) * D + (qt + 1) * TW]
                        else:
                            hcfg["ctx_dst"] = lambda hh, qt: None
                            hcfg["shtmp_prev"] = lambda hh, qt: (
                                (kb.sSHC, (P, "csh", hh, qt - 1)) if qt else
                                ((kb.sSHC, (P, "csh", hh - 2, 3)) if hh >= 3
                                 else (kb.sSHK, (P, "ksh", 1, 3))))
                        _head_dve(kb, em, v, P, h, hcfg)
            em.on("vector", ve)


def phase_attn1(kb, em):
    AR, ARf = kb.AR, kb.ARf
    kbuf = [AR[:, AT_KG + i * D:AT_KG + (i + 1) * D] for i in range(2)]
    vw = AR[:, AT_W:AT_W + 4096]
    wslot = [AR[:, AT_W + i * D:AT_W + (i + 1) * D] for i in range(2)]
    qslot = [AR[:, AT_Q + i * D:AT_Q + (i + 1) * D] for i in range(2)]
    ktmp = AR[:, AT_KT:AT_KT + TW]
    pt3 = [AR[:, AT_PT:AT_PT + TW], AR[:, AT_PT + TW:AT_PT + 2 * TW],
           AR[:, AT_PT3:AT_PT3 + TW]]
    bias_f = ARf[0:1, AT_BIAF // 2:AT_BIAF // 2 + 768]
    bias_b = AR[0:1, AT_BIAB:AT_BIAB + 768]
    bo_b = AR[0:1, AT_BOB:AT_BOB + D]
    ev_f = [ARf[:, AT_EV // 2 + i * TW:AT_EV // 2 + (i + 1) * TW]
            for i in range(2)]
    st = kb.stats
    SC = float(1.0 / np.sqrt(128.0))
    BLOCKS = [(2, "k", 0), (3, "k", 1), (0, "q", 0), (1, "q", 1),
              (4, "v", 0), (5, "v", 1)]

    for hf in range(2):
        P = ("a1", hf)
        with em.block(f"attn1_{hf}"):
            em.rec((P, "adpre0"), kb.sAD[0])
            em.rec((P, "adpre1"), kb.sAD[1])
            adbase = kb._ad_idx

            def sy(e):
                def load(par, c0, c1, src, key):
                    kb.sWL[par].inc(e.dma_start(
                        out=kb.stage[:, par * D + c0:par * D + c1], in_=src), 16)
                    em.rec(key, kb.sWL[par])

                if hf == 0:
                    em.w(e, kb.sAD[0], (P, "adpre0"))
                    em.w(e, kb.sAD[1], (P, "adpre1"))
                    kb.sIN[1].inc(e.dma_start(
                        out=ARf[0:1, AT_EV // 2:AT_EV // 2 + D],
                        in_=kb.d["l1_bo"]), 16)
                    em.rec((P, "bol"), kb.sIN[1])
                for g in range(4):
                    grp = 4 * hf + g
                    if g > 0 or hf > 0:
                        pg = (P, "biasc", g - 1) if g > 0 else \
                            (("a1", 0), "biasc", 3)
                        em.w(e, kb.sPL, pg)
                    kb.sIN[0].inc(e.dma_start(
                        out=bias_f, in_=kb.d["l1_bqkv"][grp:grp + 1, :]), 16)
                    em.rec((P, "biasl", g), kb.sIN[0])
                    for bi, (b, kind, hl) in enumerate(BLOCKS):
                        par = bi % 2
                        if bi >= 2:
                            em.w(e, kb.sPL, (P, "wc", g, bi - 2))
                        elif g > 0:
                            em.w(e, kb.sPL, (P, "wc", g - 1, 4 + bi))
                        elif hf == 1:
                            em.w(e, kb.sPL, (("a1", 0), "wc_wo", 3, bi))
                        load(par, 0, D, kb.d["l1_wqkv"][grp, b],
                             (P, "wl", g, bi))
                for o_sl in range(4):
                    for sh in range(2):
                        em.w(e, kb.sPL, (P, "wc", 3, 4 + sh) if o_sl == 0
                             else (P, "wc_wo", o_sl - 1, sh))
                        for c in range(4):
                            load(sh, c * TW, (c + 1) * TW,
                                 kb.d["l1_wo"][o_sl, hf * 8 + sh * 4 + c],
                                 (P, "wl_wo", o_sl, sh, c))
            em.on("sync", sy)

            if hf == 0:
                def gp_bo(g_raw):
                    g_ = E(g_raw, kb.sPL)
                    em.w(g_, kb.sIN[1], (P, "bol"))
                    g_.tensor_copy(bo_b[:], ARf[0:1, AT_EV // 2:AT_EV // 2 + D])
                    em.rec((P, "boc"), kb.sPL)
                em.on("gpsimd", gp_bo)

            def gp(g_raw):
                g_ = E(g_raw, kb.sPL)
                for g in range(4):
                    em.w(g_, kb.sIN[0], (P, "biasl", g))
                    if g > 0 or hf > 0:
                        pg = (P, "vmm", g - 1, 15) if g > 0 else \
                            (("a1", 0), "vmm", 3, 15)
                        em.w(g_, kb.sMM, pg)
                    g_.tensor_copy(bias_b[:], bias_f)
                    em.rec((P, "biasc", g), kb.sPL)
                    for bi, (b, kind, hl) in enumerate(BLOCKS):
                        par = bi % 2
                        em.w(g_, kb.sWL[par], (P, "wl", g, bi))
                        if kind == "v":
                            em.w(g_, kb.sMM, (P, "qmm", g, 1, 3))
                            if g > 0:
                                em.w(g_, kb.sMM, (P, "vmm", g - 1, 15))
                            g_.tensor_copy(
                                bass.AP(kb.AR, AT_W + hl * 128,
                                        [[57344, 128], [256, NK], [1, 128]]),
                                kb.stage[:, par * D:(par + 1) * D])
                        else:
                            if bi >= 2:
                                pb, pkind, phl = BLOCKS[bi - 2]
                                em.w(g_, kb.sMM,
                                     (P, "kmm" if pkind == "k" else "qmm",
                                      g, phl, 3))
                            elif g > 0:
                                em.w(g_, kb.sMM, (P, "vmm", g - 1, 15))
                            elif hf == 1:
                                em.w(g_, kb.sMM, (("a1", 0), "womm", 3, 15))
                            g_.tensor_copy(wslot[par][:],
                                           kb.stage[:, par * D:(par + 1) * D])
                        em.rec((P, "wc", g, bi), kb.sPL)
                for o_sl in range(4):
                    for sh in range(2):
                        em.w(g_, kb.sWL[sh], (P, "wl_wo", o_sl, sh, 3))
                        em.w(g_, kb.sMM, (P, "vmm", 3, 15) if o_sl == 0
                             else (P, "womm", o_sl - 1, 15))
                        g_.tensor_copy(wslot[sh][:],
                                       kb.stage[:, sh * D:(sh + 1) * D])
                        em.rec((P, "wc_wo", o_sl, sh), kb.sPL)
                    for t in range(NT128):
                        kslot = kb._ad_idx % 2
                        kb._ad_idx += 1
                        em.w(g_raw, kb.sAC, (P, "woev", o_sl, t))
                        if o_sl == 0 and t == 0:
                            em.w(g_raw, kb.sAD[0], (P, "adpre0"))
                            em.w(g_raw, kb.sAD[1], (P, "adpre1"))
                            g_raw.wait_ge(kb.sXS.h, 32)
                        kb.sAD[kslot].inc(g_raw.dma_start(
                            out=kb.d["out"][t * 128:(t + 1) * 128,
                                            o_sl * TW:(o_sl + 1) * TW],
                            in_=ev_f[kslot], accum_op=ALU.add), 16)
                        em.rec((P, "ad", o_sl, t), kb.sAD[kslot])
            em.on("gpsimd", gp)

            def te(t_eng):
                for g in range(4):
                    while kb._rs_pend:
                        _flush_rsmm(kb, em, t_eng, P)
                    for bi, (b, kind, hl) in enumerate(BLOCKS):
                        em.w(t_eng, kb.sPL, (P, "wc", g, bi))
                        em.w(t_eng, kb.sPL, (P, "biasc", g))
                        if kind == "k":
                            for tw in range(4):
                                ps = [kb.pA, kb.pB][tw % 2]
                                if tw >= 2:
                                    em.w(t_eng, kb.sAC, (P, "ksq", g, hl, tw - 2))
                                elif hl == 1:
                                    em.w(t_eng, kb.sAC, (P, "ksq", g, 0, 2 + tw))
                                elif g > 0:
                                    em.w(t_eng, kb.sAC,
                                         (P, "exp", 2 * g - 1, 39))
                                elif hf == 1:
                                    em.w(t_eng, kb.sAC,
                                         (("a1", 0), "woev", 3, 14 + tw))
                                ins = None
                                for k in range(NK):
                                    ins = t_eng.matmul(
                                        ps[:], wslot[bi % 2][:, k * 128:(k + 1) * 128],
                                        kb.xn[:, k * D + tw * TW:
                                              k * D + (tw + 1) * TW],
                                        start=(k == 0), stop=False)
                                ins = t_eng.matmul(
                                    ps[:], bias_b[0:1, (256 + hl * 128):
                                                  (384 + hl * 128)],
                                    kb.ones_row[0:1, 0:TW],
                                    start=False, stop=True)
                                kb.sMM.inc(ins)
                                em.rec((P, "kmm", g, hl, tw), kb.sMM)
                                em.w(t_eng, kb.sAC, (P, "ksq", g, hl, tw))
                                if tw > 0:
                                    em.w(t_eng, kb.sDV, (P, "kred", g, hl, tw - 1))
                                elif g > 0 or hf > 0:
                                    pg = (P, "sumev_last", g - 1, 1) if g > 0 \
                                        else (("a1", 0), "sumev_last", 3, 1)
                                    em.w(t_eng, kb.sAC, pg)
                                    pk = (P, "qred", g - 1, hl, 3) if g > 0 \
                                        else (("a1", 0), "qred", 3, hl, 3)
                                    em.w(t_eng, kb.sDV, pk)
                                kb.sMM.inc(t_eng.matmul(
                                    [kb.pF, kb.pT][hl][0:1, 0:TW],
                                    kb.ones_col[:, 0:1], ktmp[:, 0:TW],
                                    start=True, stop=True))
                                em.rec((P, "knmm", g, hl, tw), kb.sMM)
                        elif kind == "q":
                            for tw in range(4):
                                ps = [kb.pA, kb.pB][tw % 2]
                                if tw >= 2:
                                    em.w(t_eng, kb.sAC, (P, "qsq", g, hl, tw - 2))
                                elif hl:
                                    em.w(t_eng, kb.sAC, (P, "qsq", g, 0, 2 + tw))
                                else:
                                    em.w(t_eng, kb.sAC, (P, "ksq", g, 1, 2 + tw))
                                ins = None
                                for k in range(NK):
                                    ins = t_eng.matmul(
                                        ps[:], wslot[bi % 2][:, k * 128:(k + 1) * 128],
                                        kb.xn[:, k * D + tw * TW:
                                              k * D + (tw + 1) * TW],
                                        start=(k == 0), stop=False)
                                ins = t_eng.matmul(
                                    ps[:], bias_b[0:1, hl * 128:(hl + 1) * 128],
                                    kb.ones_row[0:1, 0:TW],
                                    start=False, stop=True)
                                kb.sMM.inc(ins)
                                em.rec((P, "qmm", g, hl, tw), kb.sMM)
                                em.w(t_eng, kb.sAC, (P, "qsq", g, hl, tw))
                                if tw > 0:
                                    em.w(t_eng, kb.sDV, (P, "qred", g, hl, tw - 1))
                                else:
                                    em.w(t_eng, kb.sDV, (P, "kred", g, hl, 3))
                                kb.sMM.inc(t_eng.matmul(
                                    [kb.pF, kb.pT][hl][0:1, 0:TW],
                                    kb.ones_col[:, 0:1], ktmp[:, 0:TW],
                                    start=True, stop=True))
                                em.rec((P, "qnmm", g, hl, tw), kb.sMM)
                            em.w(t_eng, kb.sAC, (P, "csqrt", g, hl))
                            if (g, hl) != (0, 0):
                                em.w(t_eng, kb.sAC, (P, "negb", g, 0) if hl
                                     else (P, "negb", g - 1, 1))
                            kb.sMM.inc(t_eng.matmul(
                                kb.pC[0:128, 0:1], kb.onesf_row[0:1, 0:128],
                                st[0:1, 2 + hl:3 + hl], start=True, stop=True))
                            em.rec((P, "negbmm", g, hl), kb.sMM)
                        else:
                            if hl == 0:
                                continue
                            em.w(t_eng, kb.sPL, (P, "wc", g, 5))
                            for tc in range(16):
                                ps = [kb.pA, kb.pB][tc % 2]
                                if tc >= 2:
                                    em.w(t_eng, kb.sAC, (P, "vev", g, tc - 2))
                                else:
                                    em.w(t_eng, kb.sAC, (P, "qsq", g, 1, 2 + tc))
                                ins = None
                                for k in range(NK):
                                    ins = t_eng.matmul(
                                        ps[:, 0:256],
                                        kb.xn[:, k * D + tc * 128:
                                              k * D + (tc + 1) * 128],
                                        vw[:, k * 256:(k + 1) * 256],
                                        start=(k == 0), stop=False)
                                ins = t_eng.matmul(
                                    ps[:, 0:256], kb.ones_row[0:1, 0:128],
                                    bias_b[0:1, 512:768],
                                    start=False, stop=True)
                                kb.sMM.inc(ins)
                                em.rec((P, "vmm", g, tc), kb.sMM)
                    for hl in range(2):
                        h = 2 * g + hl
                        hcfg = {
                            "dh": 128,
                            "dh_acc": 128,
                            "first_h": 0,
                            "first_hqt": (0, 0),
                            "odd_shift": False,
                            "pt": pt3,
                            "k_ap": (lambda hl_: lambda kc: kbuf[hl_][
                                :, kc * 128:(kc + 1) * 128])(hl),
                            "v_ap": (lambda hl_: lambda kc: AR[
                                :, AT_VG + kc * 260 + hl_ * 128:
                                AT_VG + kc * 260 + (hl_ + 1) * 128])(hl),
                            "q_ap": (lambda hl_: lambda qt: qslot[hl_][
                                :, qt * TW:(qt + 1) * TW])(hl),
                            "ksh_dep": None,
                            "sum_free": (lambda g_: lambda hh, qt: (
                                (kb.sAC, (P, "sumev", hh, qt - 2)) if qt >= 2
                                else (kb.sAC, (P, "sumev", hh - 1, 2 + qt))
                                if hh % 2 else
                                (kb.sDV, (P, "qred", g_, qt, 3))))(g),
                        }
                        nb = (kb.sAC, (P, "negb", g, 1))
                        hcfg["sbank_free"] = [
                            (kb.sAC, (P, "vev", g, 15)),
                            (kb.sAC, (P, "vev", g, 15)), nb] \
                            if hl == 0 else \
                            [(kb.sAC, (P, "exp", h - 1, 39)),
                             (kb.sAC, (P, "exp", h - 1, 37)),
                             (kb.sAC, (P, "exp", h - 1, 38))]
                        _head_pe(kb, em, t_eng, P, h, hcfg)
                while kb._rs_pend:
                    _flush_rsmm(kb, em, t_eng, P)
                for o_sl in range(4):
                    em.w(t_eng, kb.sPL, (P, "wc_wo", o_sl, 1))
                    if o_sl == 0:
                        em.w(t_eng, kb.sDV, (P, "ctxev", 7, 3))
                    for t in range(NT128):
                        ps = [kb.pA, kb.pB][t % 2]
                        if o_sl == 0 and t < 2:
                            em.w(t_eng, kb.sAC, (P, "exp", 7, 39))
                        else:
                            pv = (o_sl, t - 2) if t >= 2 else (o_sl - 1, 14 + t)
                            em.w(t_eng, kb.sAC, (P, "woev", *pv))
                        ins = None
                        for c in range(8):
                            ins = t_eng.matmul(
                                ps[:],
                                AR[:, AT_CTX + c * D + t * 128:
                                   AT_CTX + c * D + (t + 1) * 128],
                                wslot[c // 4][:, (c % 4) * TW:(c % 4 + 1) * TW],
                                start=(c == 0), stop=(c == 7 and hf == 1))
                        if hf == 0:
                            if o_sl == 0 and t == 0:
                                em.w(t_eng, kb.sPL, (P, "boc"))
                            ins = t_eng.matmul(
                                ps[:], kb.ones_row[0:1, 0:128],
                                bo_b[0:1, o_sl * TW:(o_sl + 1) * TW],
                                start=False, stop=True)
                        kb.sMM.inc(ins)
                        em.rec((P, "womm", o_sl, t), kb.sMM)
            em.on("tensor", te)

            def acf(a_raw):
                a = E(a_raw, kb.sAC)
                for g in range(4):
                    for bi, (b, kind, hl) in enumerate(BLOCKS):
                        if kind == "k":
                            for tw in range(4):
                                ps = [kb.pA, kb.pB][tw % 2]
                                em.w(a, kb.sMM, (P, "kmm", g, hl, tw))
                                if tw == 0 and g > 0:
                                    em.w(a, kb.sMM, (P, "smm", 2 * (g - 1) + hl, 39))
                                a.activation(kbuf[hl][:, tw * TW:(tw + 1) * TW],
                                             ps[:], AF.Copy)
                                em.rec((P, "kev", g, hl, tw), kb.sAC)
                                pv = (P, "knmm", g, hl, tw - 1) if tw else \
                                    ((P, "knmm", g, 0, 3) if hl else
                                     ((P, "qnmm", g - 1, 1, 3) if g else None))
                                if pv:
                                    em.w(a, kb.sMM, pv)
                                a.activation(ktmp[:, 0:TW], ps[:], AF.Square)
                                em.rec((P, "ksq", g, hl, tw), kb.sAC)
                        elif kind == "q":
                            for tw in range(4):
                                ps = [kb.pA, kb.pB][tw % 2]
                                em.w(a, kb.sMM, (P, "qmm", g, hl, tw))
                                if tw == 0 and g > 0:
                                    em.w(a, kb.sMM, (P, "smm", 2 * (g - 1) + hl, 39))
                                a.activation(qslot[hl][:, tw * TW:(tw + 1) * TW],
                                             ps[:], AF.Copy, scale=SC)
                                em.rec((P, "qev", g, hl, tw), kb.sAC)
                                pv = (P, "qnmm", g, hl, tw - 1) if tw else \
                                    (P, "qnmm", g, 0, 3) if hl else \
                                    (P, "knmm", g, 1, 3)
                                em.w(a, kb.sMM, pv)
                                a.activation(ktmp[:, 0:TW], ps[:], AF.Square,
                                             scale=SC)
                                em.rec((P, "qsq", g, hl, tw), kb.sAC)
                            em.w(a, kb.sDV, (P, "qred", g, hl, 3))
                            if (g, hl) != (0, 0):
                                pv = (P, "negbmm", g, hl - 1) if hl else \
                                    (P, "negbmm", g - 1, 1)
                                em.w(a, kb.sMM, pv)
                            a.activation(st[0:1, 2 + hl:3 + hl],
                                         st[0:1, hl:hl + 1], AF.Sqrt,
                                         scale=st[0:1, 8 + hl:9 + hl])
                            em.rec((P, "csqrt", g, hl), kb.sAC)
                            em.w(a, kb.sMM, (P, "negbmm", g, hl))
                            a.activation(st[:, 13 + hl:14 + hl], kb.pC[:, 0:1],
                                         AF.Copy, scale=-1.0)
                            em.rec((P, "negb", g, hl), kb.sAC)
                        else:
                            if hl == 0:
                                continue
                            for tc in range(16):
                                ps = [kb.pA, kb.pB][tc % 2]
                                em.w(a, kb.sMM, (P, "vmm", g, tc))
                                if tc < 2 and g > 0:
                                    em.w(a, kb.sMM, (P, "cs", 2 * g - 1, 39))
                                a.activation(
                                    bass.AP(kb.AR, AT_VG + tc * 260,
                                            [[57344, 128], [1, 256]]),
                                    ps[:, 0:256], AF.Copy)
                                em.rec((P, "vev", g, tc), kb.sAC)
                    for hl in range(2):
                        h = 2 * g + hl
                        hcfg = {"first_h": 0, "first_hqt": (0, 0), "pt": pt3,
                                "a_raw": a_raw, "shtmp": None,
                                "sum_free": True,
                                "negb_ap": st[:, 13 + hl:14 + hl]}
                        _head_act(kb, em, a, P, h, hcfg)
                        em.rec((P, "sumev_last", g, hl), kb.sAC)
                for o_sl in range(4):
                    for t in range(NT128):
                        em.w(a, kb.sMM, (P, "womm", o_sl, t))
                        gi = o_sl * 16 + t
                        kslot = (adbase + gi) % 2
                        if gi >= 2:
                            em.w(a, kb.sAD[kslot], (P, "ad", *divmod(gi - 2, 16)))
                        else:
                            em.w(a, kb.sAD[0], (P, "adpre0"))
                            em.w(a, kb.sAD[1], (P, "adpre1"))
                        a.activation(ev_f[kslot], [kb.pA, kb.pB][t % 2][:],
                                     AF.Copy)
                        em.rec((P, "woev", o_sl, t), kb.sAC)
            em.on("scalar", acf)

            def ve(v_raw):
                v = E(v_raw, kb.sDV)
                for g in range(4):
                    for hl in range(2):
                        for tw in range(4):
                            em.w(v, kb.sMM, (P, "knmm", g, hl, tw))
                            if (g, hl, tw) != (0, 0, 0):
                                pv = (P, "kred", g, hl, tw - 1) if tw else (
                                    (P, "kred", g, hl - 1, 3) if hl else
                                    (P, "kred", g - 1, 1, 3))
                                em.w(v, kb.sDV, pv)
                            if tw == 0 and g > 0:
                                em.w(v, kb.sAC, (P, "csqrt", g - 1, hl))
                            v.reduce_max(st[0:1, 12:13],
                                         [kb.pF, kb.pT][hl][0:1, 0:TW], AX.X)
                            em.rec((P, "kredr", g, hl, tw), kb.sDV)
                            em.w(v, kb.sDV, (P, "kredr", g, hl, tw))
                            if tw == 0:
                                v.tensor_copy(st[0:1, 8 + hl:9 + hl],
                                              st[0:1, 12:13])
                            else:
                                v.tensor_max(st[0:1, 8 + hl:9 + hl],
                                             st[0:1, 8 + hl:9 + hl],
                                             st[0:1, 12:13])
                            em.rec((P, "kred", g, hl, tw), kb.sDV)
                    for hl in range(2):
                        for tw in range(4):
                            em.w(v, kb.sMM, (P, "qnmm", g, hl, tw))
                            pv = (P, "qred", g, hl, tw - 1) if tw else (
                                (P, "qred", g, hl - 1, 3) if hl else
                                (P, "kred", g, 1, 3))
                            em.w(v, kb.sDV, pv)
                            if tw == 0 and g > 0:
                                em.w(v, kb.sAC, (P, "csqrt", g - 1, hl))
                            v.reduce_max(st[0:1, 12:13],
                                         [kb.pF, kb.pT][hl][0:1, 0:TW], AX.X)
                            em.rec((P, "qredr", g, hl, tw), kb.sDV)
                            em.w(v, kb.sDV, (P, "qredr", g, hl, tw))
                            if tw == 0:
                                v.tensor_copy(st[0:1, hl:hl + 1],
                                              st[0:1, 12:13])
                            else:
                                v.tensor_max(st[0:1, hl:hl + 1],
                                             st[0:1, hl:hl + 1],
                                             st[0:1, 12:13])
                            em.rec((P, "qred", g, hl, tw), kb.sDV)
                    for hl in range(2):
                        h = 2 * g + hl
                        hcfg = {
                            "dh": 128,
                            "first_hqt": (0, 0),
                            "pt": pt3,
                            "ctx_dst": lambda hh, qt: AR[
                                0:128, AT_CTX + hh * D + qt * TW:
                                AT_CTX + hh * D + (qt + 1) * TW],
                        }
                        _head_dve(kb, em, v, P, h, hcfg)
            em.on("vector", ve)


def _yev_dep(kb, P, ex, o_sl, tloc):
    if ex == 0:
        return (kb.sAC, (P, "yevA", ex, o_sl, tloc))
    return (kb.sDV, (P, "yev", ex, o_sl, tloc))


def phase_moe(kb, em):
    AR, ARf = kb.AR, kb.ARf
    ms = kb.moe_s
    wfc = [AR[:, MO_WFC + i * D:MO_WFC + (i + 1) * D] for i in range(2)]
    wpj = [AR[:, MO_WPJ + i * 1024:MO_WPJ + (i + 1) * 1024] for i in range(2)]
    tmp_f = [ARf[:, MO_TMP // 2 + i * TW:MO_TMP // 2 + (i + 1) * TW]
             for i in range(2)]
    pss8 = None

    for hf in range(2):
        P = ("mo", hf)
        with em.block(f"moe_{hf}"):
            em.rec((P, "adpre0"), kb.sAD[0])
            em.rec((P, "adpre1"), kb.sAD[1])
            adbase = kb._ad_idx
            pss8 = [kb.pA, kb.pB, kb.pC, kb.pD, kb.pE, kb.pF, kb.pR, kb.pT]

            def sy(e):
                def load(par, c0, c1, src, key):
                    kb.sWL[par].inc(e.dma_start(
                        out=kb.stage[:, par * D + c0:par * D + c1], in_=src), 16)
                    em.rec(key, kb.sWL[par])

                u = 0
                for ex in range(8):
                    for ob in range(16):
                        par = u % 2
                        if u >= 2:
                            em.w(e, kb.sPL, (P, "wc", "fc", ex, ob - 2)
                                 if ob >= 2 else (P, "wc", "pj", ex - 1, 3, 6 + ob))
                        elif hf == 1:
                            em.w(e, kb.sPL, (("mo", 0), "wc", "pj", 7, 3, 6 + ob))
                        load(par, 0, D, kb.d["l1_fc"][ex, ob],
                             (P, "wl", "fc", ex, ob))
                        u += 1
                    for o_sl in range(4):
                        for sh in range(8):
                            par = u % 2
                            if sh >= 2 or o_sl > 0:
                                pv = (o_sl, sh - 2) if sh >= 2 else (o_sl - 1, 6 + sh)
                                em.w(e, kb.sPL, (P, "wc", "pj", ex, *pv))
                            else:
                                em.w(e, kb.sPL, (P, "wc", "fc", ex, 14 + sh))
                            for c in range(2):
                                load(par, c * TW, (c + 1) * TW,
                                     kb.d["l1_pj"][ex, o_sl, sh * 2 + c],
                                     (P, "wl", "pj", ex, o_sl, sh, c))
                            u += 1
            em.on("sync", sy)

            def gp(g_raw):
                g_ = E(g_raw, kb.sPL)
                u = 0
                for ex in range(8):
                    for ob in range(16):
                        par = u % 2
                        em.w(g_, kb.sWL[par], (P, "wl", "fc", ex, ob))
                        if ex > 0 or ob >= 2:
                            pv = (P, "hmm", ex, ob - 2, 1) if ob >= 2 else \
                                (P, "hmm", ex - 1, 14 + ob, 1)
                            em.w(g_, kb.sMM, pv)
                        elif hf == 1:
                            em.w(g_, kb.sMM, (("mo", 0), "hmm", 7, 14 + ob, 1))
                        g_.tensor_copy(wfc[par][:], kb.stage[:, par * D:(par + 1) * D])
                        em.rec((P, "wc", "fc", ex, ob), kb.sPL)
                        u += 1
                    for o_sl in range(4):
                        for sh in range(8):
                            par = u % 2
                            em.w(g_, kb.sWL[par], (P, "wl", "pj", ex, o_sl, sh, 1))
                            gi2 = o_sl * 8 + sh
                            if gi2 >= 2:
                                pv = divmod(gi2 - 2, 8)
                                em.w(g_, kb.sMM, (P, "ymm", ex, pv[0], pv[1], 7))
                            else:
                                em.w(g_, kb.sMM, (P, "hmm", ex, 14 + gi2, 1))
                            g_.tensor_copy(wpj[par][:],
                                           kb.stage[:, par * D:par * D + 1024])
                            em.rec((P, "wc", "pj", ex, o_sl, sh), kb.sPL)
                            u += 1
                    if ex == 7:
                        for tloc in range(8):
                            kslot = kb._ad_idx % 2
                            kb._ad_idx += 1
                            dep = _yev_dep(kb, P, 7, 3, tloc)
                            em.w(g_raw, dep[0], dep[1])
                            if tloc == 0:
                                em.w(g_raw, kb.sAD[0], (P, "adpre0"))
                                em.w(g_raw, kb.sAD[1], (P, "adpre1"))
                                g_raw.wait_ge(kb.sXS.h, 32)
                            t128 = hf * 8 + tloc
                            kb.sAD[kslot].inc(g_raw.dma_start(
                                out=kb.d["out"][t128 * 128:(t128 + 1) * 128, :],
                                in_=ARf[:, tloc * D:(tloc + 1) * D],
                                accum_op=ALU.add), 16)
                            em.rec((P, "ad", tloc), kb.sAD[kslot])
            em.on("gpsimd", gp)

            def te(t_eng):
                for tloc in range(8):
                    if tloc > 0:
                        em.w(t_eng, kb.sDV, (P, "lg", tloc - 1))
                    ins = None
                    for k in range(NK):
                        ins = t_eng.matmul(
                            kb.pR[:, 0:8],
                            kb.xn[:, k * D + hf * 1024 + tloc * 128:
                                  k * D + hf * 1024 + (tloc + 1) * 128],
                            kb.router_b[:, k * 8:(k + 1) * 8],
                            start=(k == 0), stop=(k == NK - 1))
                    kb.sMM.inc(ins)
                    em.rec((P, "rmm", tloc), kb.sMM)
                for ex in range(8):
                    for ob in range(16):
                        par = ob % 2 if False else (ex * 16 + ob) % 2
                        em.w(t_eng, kb.sPL, (P, "wc", "fc", ex, ob))
                        for tw in range(2):
                            ps = [kb.pA, kb.pB][tw]
                            if ex > 0 or ob > 0:
                                if ob:
                                    em.w(t_eng, kb.sDV, (P, "hrel", ex, ob - 1, tw))
                                else:
                                    dep = _yev_dep(kb, P, ex - 1, 3, 6 + tw)
                                    em.w(t_eng, dep[0], dep[1])
                            ins = None
                            for k in range(NK):
                                ins = t_eng.matmul(
                                    ps[:], wfc[par][:, k * 128:(k + 1) * 128],
                                    kb.xn[:, k * D + hf * 1024 + tw * TW:
                                          k * D + hf * 1024 + (tw + 1) * TW],
                                    start=(k == 0), stop=(k == NK - 1))
                            kb.sMM.inc(ins)
                            em.rec((P, "hmm", ex, ob, tw), kb.sMM)
                    for o_sl in range(4):
                        for sh in range(8):
                            par = (o_sl * 8 + sh) % 2
                            em.w(t_eng, kb.sPL, (P, "wc", "pj", ex, o_sl, sh))
                            if sh == 0 and o_sl == 0:
                                em.w(t_eng, kb.sAC, (P, "h2", ex, 15, 1))
                                if ex == 0:
                                    em.w(t_eng, kb.sDV, (P, "wtok", 7))
                            for tloc in range(8):
                                ps = pss8[tloc]
                                if sh == 0:
                                    if o_sl > 0:
                                        dep = _yev_dep(kb, P, ex, o_sl - 1, tloc)
                                        em.w(t_eng, dep[0], dep[1])
                                    elif ex > 0:
                                        dep = _yev_dep(kb, P, ex - 1, 3, tloc)
                                        em.w(t_eng, dep[0], dep[1])
                                    elif tloc == 6:
                                        em.w(t_eng, kb.sDV, (P, "lg", 7))
                                    if ex == 0 and o_sl == 0 and tloc < 2:
                                        em.w(t_eng, kb.sDV,
                                             (P, "hrel", ex, 15, tloc))
                                ins = None
                                for c in range(2):
                                    ic = sh * 2 + c
                                    ins = t_eng.matmul(
                                        ps[:],
                                        AR[:, MO_H2 + ic * 1024 + tloc * 128:
                                           MO_H2 + ic * 1024 + (tloc + 1) * 128],
                                        wpj[par][:, c * TW:(c + 1) * TW],
                                        start=(ic == 0), stop=(ic == 15))
                                kb.sMM.inc(ins)
                                em.rec((P, "ymm", ex, o_sl, sh, tloc), kb.sMM)
            em.on("tensor", te)

            def ve(v_raw):
                v = E(v_raw, kb.sDV)
                for tloc in range(8):
                    em.w(v, kb.sMM, (P, "rmm", tloc))
                    wv = lambda key: em.w(v, kb.sDV, key)
                    v.tensor_copy(ms[:, 0:8], kb.pR[:, 0:8])
                    em.rec((P, "lg", tloc), kb.sDV)
                    wv((P, "lg", tloc))
                    v.reduce_max(ms[:, 32:33], ms[:, 0:8], AX.X)
                    em.rec((P, "m1", tloc), kb.sDV)
                    wv((P, "m1", tloc))
                    v.tensor_scalar(ms[:, 8:16], ms[:, 0:8], ms[:, 32:33],
                                    None, ALU.is_equal)
                    em.rec((P, "eq1", tloc), kb.sDV)
                    wv((P, "eq1", tloc))
                    v.scalar_tensor_tensor(ms[:, 16:24], ms[:, 8:16], -1e30,
                                           ms[:, 0:8], ALU.mult, ALU.add)
                    em.rec((P, "lg2", tloc), kb.sDV)
                    wv((P, "lg2", tloc))
                    v.reduce_max(ms[:, 33:34], ms[:, 16:24], AX.X)
                    em.rec((P, "m2", tloc), kb.sDV)
                    wv((P, "m2", tloc))
                    v.tensor_sub(ms[:, 34:35], ms[:, 32:33], ms[:, 33:34])
                    em.rec((P, "dd", tloc), kb.sDV)
                    wv((P, "dd", tloc))
                    v.tensor_scalar(ms[:, 24:32], ms[:, 16:24], ms[:, 33:34],
                                    None, ALU.is_equal)
                    em.rec((P, "eq2", tloc), kb.sDV)
                    em.w(v, kb.sAC, (P, "p12", tloc))
                    v.tensor_scalar(ms[:, 40:48], ms[:, 8:16], ms[:, 35:36],
                                    None, ALU.mult)
                    em.rec((P, "t1", tloc), kb.sDV)
                    wv((P, "t1", tloc))
                    v.scalar_tensor_tensor(
                        ms[:, 64 + tloc * 8:72 + tloc * 8], ms[:, 24:32],
                        ms[:, 36:37], ms[:, 40:48], ALU.mult, ALU.add)
                    em.rec((P, "wtok", tloc), kb.sDV)
                for ex in range(8):
                    for ob in range(16):
                        for tw in range(2):
                            em.w(v, kb.sMM, (P, "hmm", ex, ob, tw))
                            if ex > 0 or ob > 0:
                                em.w(v, kb.sAC, (P, "h2", ex, ob - 1, tw)
                                     if ob else (P, "h2", ex - 1, 15, tw))
                            v.tensor_scalar_max(tmp_f[tw], [kb.pA, kb.pB][tw][:],
                                                0.0)
                            em.rec((P, "hrel", ex, ob, tw), kb.sDV)
                    for o_sl in range(4):
                        for tloc in range(8):
                            if ex == 0:
                                continue
                            em.w(v, kb.sMM, (P, "ymm", ex, o_sl, 7, tloc))
                            em.w(v, kb.sDV, (P, "wtok", tloc))
                            if ex == 1:
                                em.w(v, kb.sAC, (P, "yevA", 0, o_sl, tloc))
                            else:
                                em.w(v, kb.sDV, (P, "yev", ex - 1, o_sl, tloc))
                            ydst = ARf[:, tloc * D + o_sl * TW:
                                       tloc * D + (o_sl + 1) * TW]
                            v.scalar_tensor_tensor(
                                ydst, pss8[tloc][:],
                                ms[:, 64 + tloc * 8 + ex:65 + tloc * 8 + ex],
                                ydst, ALU.mult, ALU.add)
                            em.rec((P, "yev", ex, o_sl, tloc), kb.sDV)
            em.on("vector", ve)

            def acf(a_raw):
                a = E(a_raw, kb.sAC)
                for tloc in range(8):
                    em.w(a, kb.sDV, (P, "dd", tloc))
                    if tloc > 0:
                        em.w(a, kb.sDV, (P, "t1", tloc - 1))
                        em.w(a, kb.sDV, (P, "wtok", tloc - 1))
                    a.activation(ms[:, 35:36], ms[:, 34:35], AF.Sigmoid)
                    a.activation(ms[:, 36:37], ms[:, 34:35], AF.Sigmoid,
                                 scale=-1.0)
                    em.rec((P, "p12", tloc), kb.sAC)
                for ex in range(8):
                    for ob in range(16):
                        for tw in range(2):
                            em.w(a, kb.sDV, (P, "hrel", ex, ob, tw))
                            a.activation(
                                AR[:, MO_H2 + ob * 1024 + tw * TW:
                                   MO_H2 + ob * 1024 + (tw + 1) * TW],
                                tmp_f[tw], AF.Square)
                            em.rec((P, "h2", ex, ob, tw), kb.sAC)
                    if ex == 0:
                        for o_sl in range(4):
                            for tloc in range(8):
                                em.w(a, kb.sMM, (P, "ymm", 0, o_sl, 7, tloc))
                                em.w(a, kb.sDV, (P, "wtok", tloc))
                                if hf == 1 and o_sl == 0 and tloc == 0:
                                    em.w(a, kb.sAD[0], (P, "adpre0"))
                                    em.w(a, kb.sAD[1], (P, "adpre1"))
                                ydst = ARf[:, tloc * D + o_sl * TW:
                                           tloc * D + (o_sl + 1) * TW]
                                a.activation(ydst, pss8[tloc][:], AF.Copy,
                                             scale=ms[:, 64 + tloc * 8:
                                                      65 + tloc * 8])
                                em.rec((P, "yevA", 0, o_sl, tloc), kb.sAC)
            em.on("scalar", acf)


def phase_final(kb, em):
    with em.block("final"):
        def gp(g_raw):
            em.rec(("fin", "ad0"), kb.sAD[0])
            em.rec(("fin", "ad1"), kb.sAD[1])
            em.w(g_raw, kb.sAD[0], ("fin", "ad0"))
            em.w(g_raw, kb.sAD[1], ("fin", "ad1"))
        em.on("gpsimd", gp)




def _debug_dump(kb, em):
    with em.block("fin"):
        def sy(e):
            for c in range(NK):
                kb.sIN[0].inc(e.dma_start(
                    out=kb.d["out"][c * 128:(c + 1) * 128, 0:D // 2],
                    in_=kb.xn.bitcast(F32)[:, c * D // 2:(c + 1) * D // 2]), 16)
            if not em.dry:
                e.wait_ge(kb.sIN[0].h, kb.sIN[0].v)
        em.on("sync", sy)


def _phase_list(upto):
    phases = [phase_setup,
              lambda kb, em: phase_ln(kb, em, kb.d["x0"], "ln0")]
    if upto == "ln0":
        phases.append(_debug_dump)
        return phases
    phases.append(phase_ffn0)
    if upto == "ffn0":
        phases.append(_debug_dump)
        return phases
    phases.append(phase_attn0)
    if upto == "attn0":
        phases.append(_debug_dump)
        return phases
    phases.append(lambda kb, em: phase_ln(kb, em, kb.d["xs"], "ln1",
                                          wait_ad=True, copy_out=True))
    if upto != "nomoe":
        phases.append(phase_moe)
    phases.append(phase_attn1)
    phases.append(phase_final)
    return phases


def build_nc(upto=None):
    kb = KB()
    kb.nc = bass.Bass("TRN2", target_bir_lowering=False)
    declare_dram(kb)
    with ExitStack() as stack:
        _alloc(kb, stack)
        phases = _phase_list(upto)
        em = Em(kb)
        kb._ad_idx = 0
        kb._ad_base = 0
        kb._rs_pend = []
        kb._rsmm_prev = None
        kb._csh_pend = []
        for ph in phases:
            kb._ad_base = kb._ad_idx
            ph(kb, em)
        assert not kb._rs_pend and not kb._csh_pend
        for s_ in kb.all_sems:
            s_.v = 0
        em.dry = False
        kb._ad_idx = 0
        kb._ad_base = 0
        kb._rs_pend = []
        kb._rsmm_prev = None
        kb._csh_pend = []
        for ph in phases:
            kb._ad_base = kb._ad_idx
            ph(kb, em)
    return kb


def kernel(**inputs):
    ins = host_prep({k: np.asarray(v) for k, v in inputs.items()})
    x_full = np.asarray(inputs["L0_attn_x"], dtype=np.float32)
    kb = build_nc()
    in_maps = []
    for core in range(8):
        m = dict(ins)
        m["x0"] = np.ascontiguousarray(x_full[core])
        in_maps.append(m)
    res = run_bass_kernel_spmd(kb.nc, in_maps, core_ids=list(range(8)))
    return np.stack([r["out"] for r in res.results]).astype(np.float32)


# version 04
# speedup vs baseline: 1.4279x; 1.0616x over previous
import numpy as np
from contextlib import ExitStack, contextmanager

import concourse.bass as bass
import concourse.mybir as mybir
from concourse.bass_utils import run_bass_kernel_spmd

F32 = mybir.dt.float32
BF16 = mybir.dt.bfloat16
AF = mybir.ActivationFunctionType
ALU = mybir.AluOpType
AX = mybir.AxisListType

ATTN0_HALVES = 2
D = 2048
S = 2048
NK = 16
TW = 512
NT128 = S // 128
I0 = 5632
IC0 = I0 // 128
EPS = 1e-5



class Sem:
    def __init__(self, h):
        self.h = h
        self.v = 0

    def inc(self, instr, n=1):
        instr.then_inc(self.h, n)
        self.v += n
        return self.v


class _FakeIns:
    def then_inc(self, h, n=1):
        return self


class _FakeEng:
    def __getattr__(self, name):
        return lambda *a, **k: _FakeIns()


class E:

    def __init__(self, eng, tick):
        self.e = eng
        self.t = tick

    def __getattr__(self, name):
        f = getattr(self.e, name)

        def g(*a, **k):
            return self.t.inc(f(*a, **k))

        return g


class Em:

    def __init__(self, kb):
        self.kb = kb
        self.T = {}
        self.dry = True

    @contextmanager
    def block(self, name):
        if self.dry:
            self._blk = None
            yield
        else:
            with self.kb.nc.Block(name) as b:
                self._blk = b
                yield

    def on(self, engname, f):
        if self.dry:
            f(_FakeEng())
        else:
            getattr(self._blk, engname)(f)

    def rec(self, key, sem):
        if self.dry:
            self.T[key] = sem.v
        else:
            assert self.T.get(key) == sem.v, (key, self.T.get(key), sem.v)
        return sem.v

    def w(self, eng, sem, key, delta=0):
        if self.dry:
            return
        e = eng.e if isinstance(eng, E) else eng
        v = self.T[key] + delta
        if v > 0:
            e.wait_ge(sem.h, v)

    def wv(self, eng, sem, val):
        if self.dry or val <= 0:
            return
        e = eng.e if isinstance(eng, E) else eng
        e.wait_ge(sem.h, val)


class KB:
    pass


def _alloc(kb, stack):
    nc = kb.nc
    sb = lambda name, shape, dt: stack.enter_context(nc.sbuf_tensor(name, shape, dt))
    ps = lambda name, shape, dt: stack.enter_context(nc.psum_tensor(name, shape, dt))
    sem = lambda name: Sem(stack.enter_context(nc.semaphore(name)))

    kb.xn = sb("xn", [128, NK * D], BF16)
    kb.AR = sb("AR", [128, 57344], BF16)
    kb.ARf = kb.AR.bitcast(F32)
    kb.stage = sb("stage", [128, 4096], F32)
    kb.ones_col = sb("ones_col", [128, 1], BF16)
    kb.ones_row = sb("ones_row", [1, 512], BF16)
    kb.mones_row = sb("mones_row", [1, 128], BF16)
    kb.onesf_row = sb("onesf_row", [1, 128], F32)
    kb.ident = sb("ident", [128, 128], BF16)
    kb.masks = sb("masks", [128, 4 * TW], BF16)
    kb.gs_us = sb("gs_us", [128, 2 * IC0], F32)
    kb.stats = sb("stats", [128, 16], F32)
    kb.consts = sb("consts", [128, 4], F32)
    kb.ds_row = sb("ds_row", [1, D], F32)
    kb.router_b = sb("router_b", [128, NK * 8], BF16)
    kb.moe_s = sb("moe_s", [128, 192], F32)
    kb.iota_p = sb("iota_p", [128, 4], F32)

    kb.pA = ps("pA", [128, TW], F32)
    kb.pB = ps("pB", [128, TW], F32)
    kb.pC = ps("pC", [128, TW], F32)
    kb.pD = ps("pD", [128, TW], F32)
    kb.pE = ps("pE", [128, TW], F32)
    kb.pF = ps("pF", [128, TW], F32)
    kb.pR = ps("pR", [128, TW], F32)
    kb.pT = ps("pT", [128, TW], F32)
    kb.pTb = kb.pT.bitcast(BF16)

    kb.sIN = [sem("sIN0"), sem("sIN1")]
    kb.sWL = [sem("sWL0"), sem("sWL1")]
    kb.sSHK = sem("sSHK")
    kb.sSHQ = sem("sSHQ")
    kb.sSHC = sem("sSHC")
    kb.sMM = sem("sMM")
    kb.sAC = sem("sAC")
    kb.sDV = sem("sDV")
    kb.sPL = sem("sPL")
    kb.sAD = [sem("sAD0"), sem("sAD1")]
    kb.sXS = sem("sXS")
    kb.all_sems = ([kb.sIN[0], kb.sIN[1], kb.sWL[0], kb.sWL[1], kb.sSHK,
                    kb.sSHQ, kb.sSHC, kb.sMM, kb.sAC, kb.sDV, kb.sPL,
                    kb.sAD[0], kb.sAD[1], kb.sXS])


LN_XNT = 0
LN_SQD = 2048
LN_IOT = 8192

AT_CTX = 0
AT_KG = 16384
AT_VG = 24576
AT_Q = 28736
AT_PT = 34880
AT_KT = 35904
AT_W = 36928
AT_SUM = 41024
AT_RS = 42048
AT_RSB = 43072
AT_BIAF = 44096
AT_BIAB = 45632
AT_BOB = 46400
AT_EV = 48448
AT_PT3 = 50496

MO_YACC = 0
MO_H2 = 32768
MO_WFC = 49152
MO_WPJ = 53248
MO_TMP = 55296

FF_HDN = 0
FF_W = 45056
FF_DS = 53248
FF_SIL = 54272
FF_TMP = 55296


def host_prep(inputs):
    asc = np.ascontiguousarray

    def lhsT_blocks(wt):
        Fdim, O = wt.shape
        a = wt.reshape(Fdim // 128, 128, O // 128, 128)
        return asc(a.transpose(2, 1, 0, 3).reshape(O // 128, 128, Fdim))

    def rhs_blocks(w):
        K, O = w.shape
        a = w.reshape(K // 128, 128, O // 512, 512)
        return asc(a.transpose(2, 0, 1, 3))

    ins = {}
    ins["l0_wq"] = lhsT_blocks(asc(inputs["L0_attn_Wq"].T))
    ins["l0_wk"] = lhsT_blocks(asc(inputs["L0_attn_Wk"].T))
    ins["l0_wv"] = asc(inputs["L0_attn_Wv"].T)
    ins["l0_wo"] = rhs_blocks(asc(inputs["L0_attn_Wo"].T))
    ins["l0_wg"] = lhsT_blocks(asc(inputs["L0_ffn_gate_w"].T))
    ins["l0_wu"] = lhsT_blocks(asc(inputs["L0_ffn_up_w"].T))
    ins["l0_wd"] = rhs_blocks(asc(inputs["L0_ffn_down_w"].T))
    ins["l0_gs"] = asc(inputs["L0_ffn_gate_s"].reshape(IC0, 128).T)
    ins["l0_us"] = asc(inputs["L0_ffn_up_s"].reshape(IC0, 128).T)
    ins["l0_ds"] = asc(inputs["L0_ffn_down_s"].reshape(1, D))

    wqkv = inputs["L1_attn_Wqkv"].reshape(D, 3, 16, 128)
    groups = []
    for g in range(8):
        h0, h1 = 2 * g, 2 * g + 1
        blk = np.concatenate(
            [wqkv[:, 0, h0], wqkv[:, 0, h1], wqkv[:, 1, h0], wqkv[:, 1, h1],
             wqkv[:, 2, h0], wqkv[:, 2, h1]], axis=1)
        groups.append(lhsT_blocks(asc(blk)))
    ins["l1_wqkv"] = asc(np.stack(groups))
    bq = inputs["L1_attn_bqkv"].reshape(3, 16, 128)
    ins["l1_bqkv"] = asc(np.stack(
        [np.concatenate([bq[0, 2 * g], bq[0, 2 * g + 1], bq[1, 2 * g],
                         bq[1, 2 * g + 1], bq[2, 2 * g], bq[2, 2 * g + 1]])
         for g in range(8)]))
    ins["l1_wo"] = rhs_blocks(asc(inputs["L1_attn_Wo"]))
    ins["l1_bo"] = asc(inputs["L1_attn_bo"].reshape(1, D))
    ins["l1_router"] = asc(inputs["L1_ffn_w_router"].reshape(NK, 128, 8)
                           .transpose(1, 0, 2).reshape(128, NK * 8))
    ins["l1_fc"] = asc(np.stack([lhsT_blocks(inputs["L1_ffn_w_fc"][e])
                                 for e in range(8)]))
    ins["l1_pj"] = asc(np.stack([rhs_blocks(inputs["L1_ffn_w_proj"][e])
                                 for e in range(8)]))
    return ins


def declare_dram(kb):
    nc = kb.nc
    dr = lambda name, shape, kind="ExternalInput": nc.dram_tensor(
        name, list(shape), F32, kind=kind).ap()
    kb.d = {
        "x0": dr("x0", (S, D)),
        "l0_wq": dr("l0_wq", (16, 128, D)),
        "l0_wk": dr("l0_wk", (4, 128, D)),
        "l0_wv": dr("l0_wv", (D, 512)),
        "l0_wo": dr("l0_wo", (4, NK, 128, 512)),
        "l0_wg": dr("l0_wg", (IC0, 128, D)),
        "l0_wu": dr("l0_wu", (IC0, 128, D)),
        "l0_wd": dr("l0_wd", (4, IC0, 128, 512)),
        "l0_gs": dr("l0_gs", (128, IC0)),
        "l0_us": dr("l0_us", (128, IC0)),
        "l0_ds": dr("l0_ds", (1, D)),
        "l1_wqkv": dr("l1_wqkv", (8, 6, 128, D)),
        "l1_bqkv": dr("l1_bqkv", (8, 768)),
        "l1_wo": dr("l1_wo", (4, NK, 128, 512)),
        "l1_bo": dr("l1_bo", (1, D)),
        "l1_router": dr("l1_router", (128, NK * 8)),
        "l1_fc": dr("l1_fc", (8, 16, 128, D)),
        "l1_pj": dr("l1_pj", (8, 4, 16, 128, 512)),
        "xs": dr("xs", (S, D), kind="Internal"),
        "out": dr("out", (S, D), kind="ExternalOutput"),
    }




def phase_setup(kb, em):
    iotaf = kb.ARf[:, LN_IOT // 2:LN_IOT // 2 + TW]
    with em.block("setup"):
        def gp(g_raw):
            g = E(g_raw, kb.sPL)
            kb.sXS.inc(g_raw.dma_start(out=kb.d["xs"], in_=kb.d["x0"]), 16)
            g.memset(kb.ones_col[:], 1.0)
            g.memset(kb.ones_row[:], 1.0)
            g.memset(kb.mones_row[:], -1.0)
            g.memset(kb.onesf_row[:], 1.0)
            g.memset(kb.consts[:, 0:1], EPS)
            g.iota(iotaf, [[1, TW]], channel_multiplier=0,
                   allow_small_or_imprecise_dtypes=True)
            g.iota(kb.iota_p[:, 0:1], [[0, 1]], channel_multiplier=1,
                   allow_small_or_imprecise_dtypes=True)
            em.rec("setup.pl", kb.sPL)
        em.on("gpsimd", gp)

        def ve(v_raw):
            v = E(v_raw, kb.sDV)
            em.w(v_raw, kb.sPL, "setup.pl")
            for m in range(1, 4):
                v.tensor_scalar_add(kb.iota_p[:, m:m + 1], kb.iota_p[:, 0:1],
                                    float(128 * m))
            em.rec("setup.iop", kb.sDV)
            em.w(v_raw, kb.sDV, "setup.iop")
            v.tensor_scalar(kb.ident[:], iotaf[:, 0:128], kb.iota_p[:, 0:1],
                            None, ALU.is_equal)
            for m in range(4):
                v.tensor_scalar(kb.masks[:, m * TW:(m + 1) * TW], iotaf,
                                kb.iota_p[:, m:m + 1], None, ALU.is_ge)
            em.rec("setup.dv", kb.sDV)
        em.on("vector", ve)

        def sy(e):
            kb.sIN[0].inc(e.dma_start(out=kb.gs_us[:, 0:IC0], in_=kb.d["l0_gs"]), 16)
            kb.sIN[0].inc(e.dma_start(out=kb.gs_us[:, IC0:2 * IC0],
                                      in_=kb.d["l0_us"]), 16)
            kb.sIN[0].inc(e.dma_start(out=kb.stage[:, 0:NK * 8],
                                      in_=kb.d["l1_router"]), 16)
            kb.sIN[0].inc(e.dma_start(out=kb.ds_row[:], in_=kb.d["l0_ds"]), 16)
            em.rec("setup.in", kb.sIN[0])
        em.on("sync", sy)

        def gp2(g_raw):
            g = E(g_raw, kb.sPL)
            em.w(g_raw, kb.sIN[0], "setup.in")
            g.tensor_copy(kb.router_b[:], kb.stage[:, 0:NK * 8])
        em.on("gpsimd", gp2)


def phase_ln(kb, em, src, tag, wait_ad=False, copy_out=False):
    st = kb.stats
    P = tag
    with em.block(f"ln_{tag}"):
        if wait_ad:
            em.rec((P, "adpre0"), kb.sAD[0])
            em.rec((P, "adpre1"), kb.sAD[1])

        def sy(e):
            for t in range(NT128):
                par = t % 2
                if t == 0 and wait_ad:
                    em.w(e, kb.sAD[0], (P, "adpre0"))
                    em.w(e, kb.sAD[1], (P, "adpre1"))
                if t >= 2:
                    em.w(e, kb.sDV, (P, "affine", t - 2))
                    em.w(e, kb.sAC, (P, "sq", t - 2))
                kb.sIN[par].inc(e.dma_start(
                    out=kb.stage[:, par * D:(par + 1) * D],
                    in_=src[t * 128:(t + 1) * 128, :]), 16)
                em.rec((P, "load", t), kb.sIN[par])
        em.on("sync", sy)

        if copy_out:
            def gpc(g_raw):
                em.w(g_raw, kb.sAD[0], (P, "adpre0"))
                em.w(g_raw, kb.sAD[1], (P, "adpre1"))
                kb.sXS.inc(g_raw.dma_start(out=kb.d["out"], in_=kb.d["xs"]), 16)
            em.on("gpsimd", gpc)

        def acf(a_raw):
            a = E(a_raw, kb.sAC)
            for t in range(NT128):
                par = t % 2
                em.w(a, kb.sIN[par], (P, "load", t))
                if t >= 1:
                    em.w(a, kb.sDV, (P, "var", t - 1))
                a.activation(kb.AR[:, LN_SQD:LN_SQD + D],
                             kb.stage[:, par * D:(par + 1) * D],
                             AF.Square, accum_out=st[:, 1:2])
                em.rec((P, "sq", t), kb.sAC)
                em.w(a, kb.sDV, (P, "var", t))
                a.activation(st[:, 4:5], st[:, 3:4], AF.Sqrt,
                             bias=kb.consts[:, 0:1])
                em.rec((P, "sqrt", t), kb.sAC)
        em.on("scalar", acf)

        def ve(v_raw):
            v = E(v_raw, kb.sDV)
            for t in range(NT128):
                par = t % 2
                em.w(v, kb.sIN[par], (P, "load", t))
                if t >= 1:
                    em.w(v, kb.sAC, (P, "sqrt", t - 1))
                v.reduce_sum(st[:, 0:1], kb.stage[:, par * D:(par + 1) * D], AX.X)
                em.rec((P, "rsum", t), kb.sDV)
                em.w(v, kb.sDV, (P, "rsum", t))
                v.tensor_scalar_mul(st[:, 2:3], st[:, 0:1], 1.0 / D)
                em.w(v, kb.sAC, (P, "sq", t))
                v.tensor_scalar_mul(st[:, 3:4], st[:, 1:2], 1.0 / D)
                em.rec((P, "m", t), kb.sDV)
                em.w(v, kb.sDV, (P, "m", t))
                v.tensor_mul(st[:, 5:6], st[:, 2:3], st[:, 2:3])
                em.rec((P, "m2", t), kb.sDV)
                em.w(v, kb.sDV, (P, "m2", t))
                v.tensor_sub(st[:, 3:4], st[:, 3:4], st[:, 5:6])
                em.rec((P, "var", t), kb.sDV)
                em.w(v, kb.sAC, (P, "sqrt", t))
                v.reciprocal(st[:, 6:7], st[:, 4:5])
                em.rec((P, "recip", t), kb.sDV)
                em.w(v, kb.sDV, (P, "recip", t))
                if t >= 1:
                    em.w(v, kb.sMM, (P, "tr", t - 1, 3))
                v.tensor_scalar(
                    kb.AR[:, LN_XNT:LN_XNT + D],
                    kb.stage[:, par * D:(par + 1) * D],
                    st[:, 2:3], st[:, 6:7], ALU.subtract, ALU.mult)
                em.rec((P, "affine", t), kb.sDV)
                for b in range(4):
                    em.w(v, kb.sMM, (P, "tr", t, b))
                    v.tensor_copy(
                        bass.AP(kb.xn, (4 * b) * D + t * 128,
                                [[NK * D, 128], [D, 4], [1, 128]]),
                        kb.pTb[:, 0:TW])
                    em.rec((P, "evac", t, b), kb.sDV)
        em.on("vector", ve)

        def te(t_eng):
            for t in range(NT128):
                em.w(t_eng, kb.sDV, (P, "affine", t))
                for b in range(4):
                    gb = 4 * t + b
                    if gb >= 1:
                        pt, pb = divmod(gb - 1, 4)
                        em.w(t_eng, kb.sDV, (P, "evac", pt, pb))
                    ins = None
                    for j in range(4):
                        ins = t_eng.transpose(
                            kb.pTb[:, j * 128:(j + 1) * 128],
                            kb.AR[:, LN_XNT + (4 * b + j) * 128:
                                  LN_XNT + (4 * b + j + 1) * 128],
                            kb.ident[:])
                    kb.sMM.inc(ins)
                    em.rec((P, "tr", t, b), kb.sMM)
        em.on("tensor", te)


def phase_ffn0(kb, em):
    ARf = kb.ARf
    sil_f = ARf[:, FF_SIL // 2:FF_SIL // 2 + TW]
    ds_f = ARf[:, FF_DS // 2:FF_DS // 2 + TW]
    tmp_f = [ARf[:, FF_TMP // 2 + i * TW:FF_TMP // 2 + (i + 1) * TW]
             for i in range(2)]
    wslot = [kb.AR[:, FF_W + i * D:FF_W + (i + 1) * D] for i in range(4)]
    NSL = 11

    P = "f0"
    with em.block("ffn0"):
        def sy(e):
            for hf in range(2):
                for ob in range(IC0):
                    j = hf * IC0 + ob
                    if ob >= 1:
                        em.w(e, kb.sPL, (P, "castA", hf, ob - 1, 1))
                    elif hf == 1:
                        em.w(e, kb.sAC, (P, "castB", 0, 3, 1, NSL - 2))
                        em.w(e, kb.sDV, (P, "castB", 0, 3, 1, NSL - 1))
                    kb.sWL[0].inc(e.dma_start(
                        out=kb.stage[:, 0:D], in_=kb.d["l0_wg"][ob]), 16)
                    em.rec((P, "loadA", hf, ob, 0), kb.sWL[0])
                    kb.sWL[1].inc(e.dma_start(
                        out=kb.stage[:, D:2 * D], in_=kb.d["l0_wu"][ob]), 16)
                    em.rec((P, "loadA", hf, ob, 1), kb.sWL[1])
                for o_sl in range(4):
                    for tg in range(2):
                        for sl in range(NSL):
                            gi = (o_sl * 2 + tg) * NSL + sl
                            par = gi % 2
                            if gi == 0:
                                em.w(e, kb.sPL, (P, "castA", hf, IC0 - 1, 1))
                                em.w(e, kb.sPL, (P, "castA", hf, IC0 - 2, 1))
                            elif gi == 1:
                                pass
                            else:
                                po, pt_, psl = _b_unit(gi - 2, NSL)
                                em.w(e, [kb.sAC, kb.sDV][gi % 2],
                                     (P, "castB", hf, po, pt_, psl))
                            for c in range(4):
                                kb.sWL[par].inc(e.dma_start(
                                    out=kb.stage[:, par * D + c * TW:
                                                 par * D + (c + 1) * TW],
                                    in_=kb.d["l0_wd"][o_sl, sl * 4 + c]), 16)
                            em.rec((P, "loadB", hf, o_sl, tg, sl), kb.sWL[par])
        em.on("sync", sy)

        def gp(g_raw):
            g = E(g_raw, kb.sPL)
            for hf in range(2):
                for ob in range(IC0):
                    j = hf * IC0 + ob
                    par2 = 2 * (j % 2)
                    em.w(g, kb.sWL[0], (P, "loadA", hf, ob, 0))
                    if j >= 2:
                        phf, pob = divmod(j - 2, IC0)
                        em.w(g, kb.sMM, (P, "mmA", phf, pob, 3))
                    if hf == 1 and ob < 2:
                        em.w(g, kb.sMM, (P, "mmB", 0, 3, 1, NSL - 1, 3))
                    g.tensor_copy(wslot[par2][:], kb.stage[:, 0:D])
                    em.rec((P, "castA", hf, ob, 0), kb.sPL)
                    em.w(g, kb.sWL[1], (P, "loadA", hf, ob, 1))
                    g.tensor_copy(wslot[par2 + 1][:], kb.stage[:, D:2 * D])
                    em.rec((P, "castA", hf, ob, 1), kb.sPL)
                for o_sl in range(4):
                    for tg in range(2):
                        for tl in range(4):
                            kslot = kb._ad_idx % 2
                            kb._ad_idx += 1
                            em.w(g_raw, kb.sDV, (P, "ev", hf, o_sl, tg, tl))
                            if hf == 0 and o_sl == 0 and tg == 0 and tl == 0:
                                g_raw.wait_ge(kb.sXS.h, 16)
                            t128 = hf * 8 + tg * 4 + tl
                            kb.sAD[kslot].inc(g_raw.dma_start(
                                out=kb.d["xs"][t128 * 128:(t128 + 1) * 128,
                                               o_sl * TW:(o_sl + 1) * TW],
                                in_=tmp_f[kslot], accum_op=ALU.add), 16)
                            em.rec((P, "ad", hf, o_sl, tg, tl), kb.sAD[kslot])
        em.on("gpsimd", gp)

        def te(t_eng):
            pss = [kb.pA, kb.pB, kb.pC, kb.pD]
            for hf in range(2):
                for ob in range(IC0):
                    j = hf * IC0 + ob
                    par2 = 2 * (j % 2)
                    em.w(t_eng, kb.sPL, (P, "castA", hf, ob, 1))
                    for grp in range(4):
                        wt = wslot[par2 + (0 if grp < 2 else 1)]
                        tw = grp % 2
                        ps = pss[grp]
                        if j >= 1:
                            if grp < 2:
                                em.w(t_eng, kb.sAC, (P, "silu", hf, ob - 1, tw)
                                     if ob >= 1 else (P, "silu", hf - 1, IC0 - 1, tw))
                            else:
                                em.w(t_eng, kb.sDV, (P, "stt", hf, ob - 1, tw)
                                     if ob >= 1 else (P, "stt", hf - 1, IC0 - 1, tw))
                        if hf == 1 and ob < 2:
                            em.w(t_eng, kb.sDV, (P, "ev", 0, 3, 1, grp))
                        ins = None
                        for k in range(NK):
                            ins = t_eng.matmul(
                                ps[:], wt[:, k * 128:(k + 1) * 128],
                                kb.xn[:, k * D + hf * 1024 + tw * TW:
                                      k * D + hf * 1024 + (tw + 1) * TW],
                                start=(k == 0), stop=(k == NK - 1))
                        kb.sMM.inc(ins)
                        em.rec((P, "mmA", hf, ob, grp), kb.sMM)
                for o_sl in range(4):
                    if o_sl > 0 or hf > 0:
                        em.w(t_eng, kb.sDV,
                             (P, "dscp", hf, o_sl - 1) if o_sl > 0
                             else (P, "dscp", hf - 1, 3))
                    kb.sMM.inc(t_eng.matmul(
                        kb.pR[:], kb.onesf_row[:],
                        kb.ds_row[0:1, o_sl * TW:(o_sl + 1) * TW],
                        start=True, stop=True))
                    em.rec((P, "dsmm", hf, o_sl), kb.sMM)
                    for tg in range(2):
                        for sl in range(NSL):
                            gi = (o_sl * 2 + tg) * NSL + sl
                            par = gi % 2
                            em.w(t_eng, [kb.sAC, kb.sDV][gi % 2],
                                 (P, "castB", hf, o_sl, tg, sl))
                            for tl in range(4):
                                ps = pss[tl]
                                if sl == 0 and (tg > 0 or o_sl > 0):
                                    prev = (hf, o_sl, tg - 1, tl) if tg > 0 \
                                        else (hf, o_sl - 1, 1, tl)
                                    em.w(t_eng, kb.sDV, (P, "ev", *prev))
                                elif sl == 0 and o_sl == 0 and tg == 0:
                                    if tl >= 2:
                                        em.w(t_eng, kb.sDV,
                                             (P, "stt", hf, IC0 - 1, tl - 2))
                                    else:
                                        em.w(t_eng, kb.sAC,
                                             (P, "silu", hf, IC0 - 1, tl))
                                    if hf == 1:
                                        em.w(t_eng, kb.sDV, (P, "ev", 0, 3, 1, tl))
                                ins = None
                                for c in range(4):
                                    ic = sl * 4 + c
                                    ins = t_eng.matmul(
                                        ps[:],
                                        kb.AR[:, FF_HDN + ic * 1024 + (tg * 4 + tl) * 128:
                                              FF_HDN + ic * 1024 + (tg * 4 + tl + 1) * 128],
                                        wslot[par][:, c * TW:(c + 1) * TW],
                                        start=(ic == 0), stop=(ic == IC0 - 1))
                                kb.sMM.inc(ins)
                                em.rec((P, "mmB", hf, o_sl, tg, sl, tl), kb.sMM)
        em.on("tensor", te)

        def acf(a_raw):
            a = E(a_raw, kb.sAC)
            for hf in range(2):
                for ob in range(IC0):
                    for tw in range(2):
                        em.w(a, kb.sMM, (P, "mmA", hf, ob, tw))
                        if ob > 0 or tw > 0 or hf > 0:
                            pj = (hf, ob, tw - 1) if tw > 0 else (
                                (hf, ob - 1, 1) if ob > 0 else (hf - 1, IC0 - 1, 1))
                            em.w(a, kb.sDV, (P, "stt", *pj))
                        a.activation(sil_f, [kb.pA, kb.pB][tw][:], AF.Silu,
                                     scale=kb.gs_us[:, ob:ob + 1])
                        em.rec((P, "silu", hf, ob, tw), kb.sAC)
                for o_sl in range(4):
                    for tg in range(2):
                        for sl in range(NSL):
                            gi = (o_sl * 2 + tg) * NSL + sl
                            if gi % 2:
                                continue
                            em.w(a, kb.sWL[0], (P, "loadB", hf, o_sl, tg, sl))
                            if gi >= 2:
                                po, pt_, psl = _b_unit(gi - 2, NSL)
                                em.w(a, kb.sMM, (P, "mmB", hf, po, pt_, psl, 3))
                            else:
                                em.w(a, kb.sMM, (P, "mmA", hf, IC0 - 2 + gi, 3))
                            a.activation(wslot[0][:], kb.stage[:, 0:D], AF.Copy)
                            em.rec((P, "castB", hf, o_sl, tg, sl), kb.sAC)
        em.on("scalar", acf)

        def ve(v_raw):
            v = E(v_raw, kb.sDV)
            for hf in range(2):
                for ob in range(IC0):
                    for tw in range(2):
                        em.w(v, kb.sMM, (P, "mmA", hf, ob, 2 + tw))
                        em.w(v, kb.sAC, (P, "silu", hf, ob, tw))
                        v.scalar_tensor_tensor(
                            kb.AR[:, FF_HDN + ob * 1024 + tw * TW:
                                  FF_HDN + ob * 1024 + (tw + 1) * TW],
                            [kb.pC, kb.pD][tw][:],
                            kb.gs_us[:, IC0 + ob:IC0 + ob + 1],
                            sil_f, ALU.mult, ALU.mult)
                        em.rec((P, "stt", hf, ob, tw), kb.sDV)

                dn = [0]

                def dve_casts(hi):
                    while dn[0] <= min(hi, 8 * NSL - 1):
                        gi = dn[0]
                        dn[0] += 1
                        if gi % 2 == 0:
                            continue
                        co, r = divmod(gi, 2 * NSL)
                        ct, sl = divmod(r, NSL)
                        em.w(v, kb.sWL[1], (P, "loadB", hf, co, ct, sl))
                        if gi >= 2:
                            po, pt_, psl = _b_unit(gi - 2, NSL)
                            em.w(v, kb.sMM, (P, "mmB", hf, po, pt_, psl, 3))
                        else:
                            em.w(v, kb.sMM, (P, "mmA", hf, IC0 - 2 + gi, 3))
                        v.tensor_copy(wslot[1][:], kb.stage[:, D:2 * D])
                        em.rec((P, "castB", hf, co, ct, sl), kb.sDV)

                for o_sl in range(4):
                    em.w(v, kb.sMM, (P, "dsmm", hf, o_sl))
                    if o_sl > 0 or hf > 0:
                        pk = (hf, o_sl - 1, 1, 3) if o_sl > 0 else (hf - 1, 3, 1, 3)
                        em.w(v, kb.sDV, (P, "ev", *pk))
                    v.tensor_copy(ds_f, kb.pR[:])
                    em.rec((P, "dscp", hf, o_sl), kb.sDV)
                    for tg in range(2):
                        dve_casts((o_sl * 2 + tg + 1) * NSL)
                        for tl in range(4):
                            kidx = ((hf * 4 + o_sl) * 2 + tg) * 4 + tl
                            kslot = (kb._ad_base + kidx) % 2
                            em.w(v, kb.sDV, (P, "dscp", hf, o_sl))
                            em.w(v, kb.sMM, (P, "mmB", hf, o_sl, tg, NSL - 1, tl))
                            if kidx >= 2:
                                pk = kidx - 2
                                pi = _ev_unit(pk)
                                em.w(v, kb.sAD[kslot], (P, "ad", *pi))
                            v.tensor_tensor(tmp_f[kslot],
                                            [kb.pA, kb.pB, kb.pC, kb.pD][tl][:],
                                            ds_f, ALU.mult)
                            em.rec((P, "ev", hf, o_sl, tg, tl), kb.sDV)
        em.on("vector", ve)


def _b_unit(gi, NSL):
    o_sl, r = divmod(gi, 2 * NSL)
    tg, sl = divmod(r, NSL)
    return o_sl, tg, sl


def _ev_unit(k):
    hf, r = divmod(k, 32)
    o_sl, r2 = divmod(r, 8)
    tg, tl = divmod(r2, 4)
    return hf, o_sl, tg, tl




def _flush_rsmm(kb, em, t_eng, P):
    h, qt = kb._rs_pend.pop(0)
    em.w(t_eng, kb.sDV, (P, "rs", h, qt))
    pq = kb._rsmm_prev
    if pq is not None:
        em.w(t_eng, kb.sDV, (pq[0], "rsbcp", pq[1], pq[2]))
    kb.sMM.inc(t_eng.matmul(
        kb.pR[:], kb.onesf_row[0:1, 0:128],
        kb.ARf[0:1, AT_RS // 2:AT_RS // 2 + TW],
        start=True, stop=True))
    em.rec((P, "rsmm", h, qt), kb.sMM)
    kb._rsmm_prev = (P, h, qt)


_CHUNKS = [(qt, kc) for qt in range(4) for kc in range(4 * qt + 4)]


def _head_pe(kb, em, t_eng, P, h, cfg):
    S3 = [kb.pA, kb.pB, kb.pC]
    pt = cfg["pt"]
    N = len(_CHUNKS)
    LAG = 3
    for j in range(N + LAG):
        if j < N:
            qt, kc = _CHUNKS[j]
            s_ps = S3[j % 3]
            if j >= 3:
                em.w(t_eng, kb.sAC, (P, "exp", h, j - 3))
            else:
                dep = cfg["sbank_free"][j]
                em.w(t_eng, dep[0], dep[1])
            if cfg["odd_shift"] and j == 0:
                em.w(t_eng, kb.sSHQ, (P, "qsh", h // 2, 3))
            if j == 0 and cfg.get("ksh_dep"):
                em.w(t_eng, kb.sSHK, cfg["ksh_dep"])
            kb.sMM.inc(t_eng.matmul(s_ps[:], cfg["k_ap"](kc), cfg["q_ap"](qt),
                                    start=True, stop=True))
            em.rec((P, "smm", h, j), kb.sMM)
        i = j - LAG
        if 0 <= i < N:
            qt, kc = _CHUNKS[i]
            nk = 4 * qt + 4
            ctx_ps = [kb.pD, kb.pE][qt % 2]
            if kb._rs_pend and (kc == 2 or (qt == 0 and kc == 0)):
                _flush_rsmm(kb, em, t_eng, P)
            if kc >= 4 * qt:
                em.w(t_eng, kb.sDV, (P, "mask", h, i))
            else:
                em.w(t_eng, kb.sAC, (P, "exp", h, i))
            if kc == 0:
                if (h, qt) != cfg["first_hqt"]:
                    pq = (h, qt - 2) if qt >= 2 else (h - 1, qt + 2)
                    if pq[0] >= 0:
                        em.w(t_eng, kb.sDV, (P, "ctxev", *pq))
                if cfg["sum_free"] is not None:
                    dep = cfg["sum_free"](h, qt)
                    em.w(t_eng, dep[0], dep[1])
            da = cfg["dh_acc"]
            ins = t_eng.matmul(ctx_ps[0:da, 0:TW], cfg["v_ap"](kc),
                               pt[i % 3], start=(kc == 0), stop=(kc == nk - 1))
            if cfg["sum_free"] is not None:
                sum_ps = [kb.pF, kb.pT][qt % 2]
                ins = t_eng.matmul(sum_ps[0:1, 0:TW], kb.ones_col[:, 0:1],
                                   pt[i % 3], start=(kc == 0),
                                   stop=(kc == nk - 1))
            kb.sMM.inc(ins)
            em.rec((P, "cs", h, i), kb.sMM)
            if kc == nk - 1:
                kb._rs_pend.append((h, qt))


def _flush_csh(kb, em, a, a_raw, shtmp):
    P, h, qt, dst = kb._csh_pend.pop(0)
    em.w(a, kb.sDV, (P, "ctxev", h, qt))
    kb.sSHC.inc(a_raw.dma_start(out=dst, in_=shtmp[0:64, 0:TW]), 16)
    em.rec((P, "csh", h, qt), kb.sSHC)


def _head_act(kb, em, a, P, h, cfg):
    si = 0
    for qt in range(4):
        nk = 4 * qt + 4
        for kc in range(nk):
            if kc == 3 and kb._csh_pend:
                _flush_csh(kb, em, a, cfg["a_raw"], cfg["shtmp"])
            em.w(a, kb.sMM, (P, "smm", h, si))
            if si >= 3:
                em.w(a, kb.sMM, (P, "cs", h, si - 3))
            elif h > cfg["first_h"]:
                em.w(a, kb.sMM, (P, "cs", h - 1, {0: 39, 1: 37, 2: 38}[si]))
            a.activation(cfg["pt"][si % 3],
                         [kb.pA, kb.pB, kb.pC][si % 3][:], AF.Exp,
                         bias=cfg["negb_ap"])
            em.rec((P, "exp", h, si), kb.sAC)
            si += 1
        em.w(a, kb.sMM, (P, "cs", h, si - 1))
        if (h, qt) != cfg["first_hqt"]:
            pq = (h, qt - 1) if qt >= 1 else (h - 1, 3)
            em.w(a, kb.sDV, (P, "rs", *pq))
        if cfg["sum_free"] is None:
            a.activation(kb.ARf[0:1, AT_SUM // 2:AT_SUM // 2 + TW],
                         [kb.pD, kb.pE][qt % 2][64:65, 0:TW], AF.Copy)
        else:
            a.activation(kb.ARf[0:1, AT_SUM // 2:AT_SUM // 2 + TW],
                         [kb.pF, kb.pT][qt % 2][0:1, 0:TW], AF.Copy)
        em.rec((P, "sumev", h, qt), kb.sAC)
        if cfg.get("csh_dst"):
            kb._csh_pend.append((P, h, qt, cfg["csh_dst"](h, qt)))


def _head_dve(kb, em, v, P, h, cfg):
    si = 0
    for qt in range(4):
        ctx_ps = [kb.pD, kb.pE][qt % 2]
        nk = 4 * qt + 4
        for kc in range(nk):
            if kc >= 4 * qt:
                m = kc - 4 * qt
                em.w(v, kb.sAC, (P, "exp", h, si))
                pt = cfg["pt"][si % 3]
                v.tensor_tensor(pt, pt,
                                kb.masks[:, m * TW:(m + 1) * TW], ALU.mult)
                em.rec((P, "mask", h, si), kb.sDV)
            si += 1
        em.w(v, kb.sAC, (P, "sumev", h, qt))
        if (h, qt) != cfg["first_hqt"]:
            pq = (h, qt - 1) if qt >= 1 else (h - 1, 3)
            em.w(v, kb.sMM, (P, "rsmm", *pq))
        v.reciprocal(kb.ARf[0:1, AT_RS // 2:AT_RS // 2 + TW],
                     kb.ARf[0:1, AT_SUM // 2:AT_SUM // 2 + TW])
        em.rec((P, "rs", h, qt), kb.sDV)
        em.w(v, kb.sMM, (P, "rsmm", h, qt))
        if (h, qt) != cfg["first_hqt"]:
            pq = (h, qt - 1) if qt >= 1 else (h - 1, 3)
            em.w(v, kb.sDV, (P, "ctxev", *pq))
        v.tensor_copy(kb.ARf[:, AT_RSB // 2:AT_RSB // 2 + TW], kb.pR[:])
        em.rec((P, "rsbcp", h, qt), kb.sDV)
        em.w(v, kb.sDV, (P, "rsbcp", h, qt))
        em.w(v, kb.sMM, (P, "cs", h, si - 1))
        dh = cfg["dh"]
        dst = cfg["ctx_dst"](h, qt)
        if dst is None:
            pv = cfg["shtmp_prev"](h, qt)
            if pv is not None:
                em.w(v, pv[0], pv[1])
            v.tensor_tensor(
                kb.AR[0:dh, AT_KT + TW:AT_KT + 2 * TW],
                ctx_ps[0:dh, 0:TW],
                kb.ARf[0:dh, AT_RSB // 2:AT_RSB // 2 + TW], ALU.mult)
        else:
            v.tensor_tensor(dst, ctx_ps[0:dh, 0:TW],
                            kb.ARf[0:dh, AT_RSB // 2:AT_RSB // 2 + TW],
                            ALU.mult)
        em.rec((P, "ctxev", h, qt), kb.sDV)


def phase_attn0(kb, em):
    AR, ARf = kb.AR, kb.ARf
    kbuf = [AR[:, AT_KG + i * D:AT_KG + (i + 1) * D] for i in range(4)]
    vw = AR[:, AT_W:AT_W + 4096]
    wslot = [AR[:, AT_W + i * D:AT_W + (i + 1) * D] for i in range(2)]
    qpair = [AR[:, AT_Q + i * D:AT_Q + (i + 1) * D] for i in range(2)]
    qodd = AR[:, AT_Q + 2 * D:AT_Q + 3 * D]
    ktmp = AR[:, AT_KT:AT_KT + TW]
    shtmp = AR[:, AT_KT + TW:AT_KT + 2 * TW]
    pt3 = [AR[:, AT_PT:AT_PT + TW], AR[:, AT_PT + TW:AT_PT + 2 * TW],
           AR[:, AT_PT3:AT_PT3 + TW]]
    ev_f = [ARf[:, AT_EV // 2 + i * TW:AT_EV // 2 + (i + 1) * TW]
            for i in range(2)]
    st = kb.stats

    for hf in range(2):
        P = ("a0", hf)
        with em.block(f"attn0_{hf}"):
            em.rec((P, "adpre0"), kb.sAD[0])
            em.rec((P, "adpre1"), kb.sAD[1])
            adbase = kb._ad_idx

            def sy(e):
                def load(par, c0, c1, src, key):
                    kb.sWL[par].inc(e.dma_start(
                        out=kb.stage[:, par * D + c0:par * D + c1], in_=src), 16)
                    em.rec(key, kb.sWL[par])

                for kp in range(2):
                    load(kp, 0, D, kb.d["l0_wk"][2 * hf + kp], (P, "wl", "k", kp))
                for c in range(16):
                    em.w(e, kb.sPL, (P, "wc", "k", c % 2) if c < 2
                         else (P, "wc", "v", c - 2))
                    load(c % 2, 0, 256,
                         kb.d["l0_wv"][c * 128:(c + 1) * 128,
                                       hf * 256:(hf + 1) * 256],
                         (P, "wl", "v", c))
                for p in range(8):
                    em.w(e, kb.sPL, (P, "wc", "v", 14 + (p % 2)) if p < 2
                         else (P, "wc", "q", p - 2))
                    load(p % 2, 0, D, kb.d["l0_wq"][8 * hf + p], (P, "wl", "q", p))
                for o_sl in range(4):
                    for sh in range(2):
                        em.w(e, kb.sPL, (P, "wc", "q", 6 + sh) if o_sl == 0
                             else (P, "wc", "wo", o_sl - 1, sh))
                        for c in range(4):
                            load(sh, c * TW, (c + 1) * TW,
                                 kb.d["l0_wo"][o_sl, hf * 8 + sh * 4 + c],
                                 (P, "wl", "wo", o_sl, sh, c))
            em.on("sync", sy)

            def gp(g_raw):
                g = E(g_raw, kb.sPL)
                for i in range(4):
                    g.memset(kbuf[i][64:128, :], 0.0)
                if hf == 0:
                    g.memset(qodd[64:128, :], 0.0)
                g.memset(bass.AP(kb.AR, AT_VG + 64,
                                 [[57344, 128], [260, 16], [65, 4]]), 1.0)
                em.rec((P, "kz"), kb.sPL)
                for kp in range(2):
                    em.w(g, kb.sWL[kp], (P, "wl", "k", kp))
                    g.tensor_copy(wslot[kp][:], kb.stage[:, kp * D:(kp + 1) * D])
                    em.rec((P, "wc", "k", kp), kb.sPL)
                for c in range(16):
                    em.w(g, kb.sWL[c % 2], (P, "wl", "v", c))
                    em.w(g, kb.sMM, (P, "kmm", c // 8, 3))
                    g.tensor_copy(vw[:, c * 256:(c + 1) * 256],
                                  kb.stage[:, (c % 2) * D:(c % 2) * D + 256])
                    em.rec((P, "wc", "v", c), kb.sPL)
                for p in range(8):
                    em.w(g, kb.sWL[p % 2], (P, "wl", "q", p))
                    em.w(g, kb.sMM, (P, "vmm", 15) if p < 2
                         else (P, "qmm", p - 2, 3))
                    g.tensor_copy(wslot[p % 2][:],
                                  kb.stage[:, (p % 2) * D:(p % 2 + 1) * D])
                    em.rec((P, "wc", "q", p), kb.sPL)
                for o_sl in range(4):
                    for sh in range(2):
                        em.w(g, kb.sWL[sh], (P, "wl", "wo", o_sl, sh, 3))
                        em.w(g, kb.sMM, (P, "qmm", 6 + sh, 3) if o_sl == 0
                             else (P, "womm", o_sl - 1, 15))
                        g.tensor_copy(wslot[sh][:],
                                      kb.stage[:, sh * D:(sh + 1) * D])
                        em.rec((P, "wc", "wo", o_sl, sh), kb.sPL)
                    for t in range(NT128):
                        kslot = kb._ad_idx % 2
                        kb._ad_idx += 1
                        em.w(g_raw, kb.sAC, (P, "woev", o_sl, t))
                        if o_sl == 0 and t == 0:
                            em.w(g_raw, kb.sAD[0], (P, "adpre0"))
                            em.w(g_raw, kb.sAD[1], (P, "adpre1"))
                        kb.sAD[kslot].inc(g_raw.dma_start(
                            out=kb.d["xs"][t * 128:(t + 1) * 128,
                                           o_sl * TW:(o_sl + 1) * TW],
                            in_=ev_f[kslot], accum_op=ALU.add), 16)
                        em.rec((P, "ad", o_sl, t), kb.sAD[kslot])
            em.on("gpsimd", gp)

            def te(t_eng):
                for kp in range(2):
                    em.w(t_eng, kb.sPL, (P, "wc", "k", kp))
                    em.w(t_eng, kb.sPL, (P, "kz"))
                    for tw in range(4):
                        ps = [kb.pA, kb.pB][tw % 2]
                        if tw >= 2:
                            em.w(t_eng, kb.sAC, (P, "ksq", kp, tw - 2, 1))
                        elif kp == 1:
                            em.w(t_eng, kb.sAC, (P, "ksq", 0, 2 + tw, 1))
                        ins = None
                        for k in range(NK):
                            ins = t_eng.matmul(
                                ps[:], wslot[kp][:, k * 128:(k + 1) * 128],
                                kb.xn[:, k * D + tw * TW:k * D + (tw + 1) * TW],
                                start=(k == 0), stop=(k == NK - 1))
                        kb.sMM.inc(ins)
                        em.rec((P, "kmm", kp, tw), kb.sMM)
                        for par in range(2):
                            em.w(t_eng, kb.sAC, (P, "ksq", kp, tw, par))
                            if (kp, tw) != (0, 0):
                                pv = (kp, tw - 1, par) if tw else (kp - 1, 3, par)
                                em.w(t_eng, kb.sDV, (P, "kred", *pv))
                            lo, hi = (0, 64) if par == 0 else (64, 128)
                            kb.sMM.inc(t_eng.matmul(
                                [kb.pF, kb.pT][par][0:1, 0:TW],
                                kb.ones_col[lo:hi, 0:1], ktmp[lo:hi, 0:TW],
                                start=True, stop=True))
                            em.rec((P, "knmm", kp, tw, par), kb.sMM)
                for tc in range(16):
                    if tc == 0:
                        em.w(t_eng, kb.sPL, (P, "wc", "v", 15))
                    ps = [kb.pA, kb.pB][tc % 2]
                    if tc >= 2:
                        em.w(t_eng, kb.sAC, (P, "vev", tc - 2))
                    else:
                        em.w(t_eng, kb.sAC, (P, "ksq", 1, 2 + tc, 1))
                    ins = None
                    for k in range(NK):
                        ins = t_eng.matmul(
                            ps[:, 0:256],
                            kb.xn[:, k * D + tc * 128:k * D + (tc + 1) * 128],
                            vw[:, k * 256:(k + 1) * 256],
                            start=(k == 0), stop=(k == NK - 1))
                    kb.sMM.inc(ins)
                    em.rec((P, "vmm", tc), kb.sMM)
                for p in range(8):
                    kvl = p // 2
                    while kb._rs_pend:
                        _flush_rsmm(kb, em, t_eng, P)
                    em.w(t_eng, kb.sPL, (P, "wc", "q", p))
                    for tw in range(4):
                        ps = [kb.pA, kb.pB][tw % 2]
                        if tw >= 2:
                            em.w(t_eng, kb.sAC, (P, "qsq", p, tw - 2, 1))
                        elif p == 0:
                            em.w(t_eng, kb.sAC, (P, "vev", 14 + tw))
                        else:
                            em.w(t_eng, kb.sAC, (P, "exp", 2 * p - 1, 39))
                        ins = None
                        for k in range(NK):
                            ins = t_eng.matmul(
                                ps[:], wslot[p % 2][:, k * 128:(k + 1) * 128],
                                kb.xn[:, k * D + tw * TW:k * D + (tw + 1) * TW],
                                start=(k == 0), stop=(k == NK - 1))
                        kb.sMM.inc(ins)
                        em.rec((P, "qmm", p, tw), kb.sMM)
                        for par in range(2):
                            em.w(t_eng, kb.sAC, (P, "qsq", p, tw, par))
                            if tw > 0:
                                em.w(t_eng, kb.sDV, (P, "qred", p, tw - 1, par))
                            elif p > 0:
                                em.w(t_eng, kb.sDV, (P, "qred", p - 1, 3, par))
                            else:
                                em.w(t_eng, kb.sDV, (P, "kred", 1, 3, par))
                            lo, hi = (0, 64) if par == 0 else (64, 128)
                            kb.sMM.inc(t_eng.matmul(
                                [kb.pF, kb.pT][par][0:1, 0:TW],
                                kb.ones_col[lo:hi, 0:1], ktmp[lo:hi, 0:TW],
                                start=True, stop=True))
                            em.rec((P, "qnmm", p, tw, par), kb.sMM)
                    for par in range(2):
                        em.w(t_eng, kb.sAC, (P, "csqrt", p, par))
                        if p > 0 and par == 0:
                            em.w(t_eng, kb.sAC, (P, "exp", 2 * p - 1, 38))
                        kb.sMM.inc(t_eng.matmul(
                            kb.pC[0:128, 0:1], kb.onesf_row[0:1, 0:128],
                            st[0:1, 2 + par:3 + par], start=True, stop=True))
                        em.rec((P, "negbmm", p, par), kb.sMM)
                    for h in (2 * p, 2 * p + 1):
                        hcfg = {
                            "dh": 64,
                            "dh_acc": 65,
                            "first_h": 0,
                            "first_hqt": (0, 0),
                            "pt": pt3,
                            "sum_free": None,
                            "k_ap": (lambda kvl_: lambda kc: kbuf[kvl_][
                                :, kc * 128:(kc + 1) * 128])(kvl),
                            "v_ap": (lambda kvl_: lambda kc: AR[
                                :, AT_VG + kc * 260 + kvl_ * 65:
                                AT_VG + kc * 260 + kvl_ * 65 + 65])(kvl),
                            "ksh_dep": ((P, "ksh", kvl // 2, 3)
                                        if kvl % 2 else None),
                        }
                        if h % 2 == 0:
                            hcfg["q_ap"] = (lambda p_: lambda qt: qpair[p_ % 2][
                                :, qt * TW:(qt + 1) * TW])(p)
                            hcfg["odd_shift"] = False
                        else:
                            hcfg["q_ap"] = lambda qt: qodd[:, qt * TW:(qt + 1) * TW]
                            hcfg["odd_shift"] = True
                        nb = (kb.sAC, (P, "negb", p, 1))
                        hcfg["sbank_free"] = [nb, nb, nb] if h % 2 == 0 \
                            else [(kb.sAC, (P, "exp", h - 1, 39)),
                                  (kb.sAC, (P, "exp", h - 1, 37)),
                                  (kb.sAC, (P, "exp", h - 1, 38))]
                        _head_pe(kb, em, t_eng, P, h, hcfg)
                while kb._rs_pend:
                    _flush_rsmm(kb, em, t_eng, P)
                for o_sl in range(4):
                    em.w(t_eng, kb.sPL, (P, "wc", "wo", o_sl, 1))
                    if o_sl == 0:
                        em.w(t_eng, kb.sDV, (P, "ctxev", 15, 3))
                        em.w(t_eng, kb.sSHC, (P, "csh", 15, 3))
                    for t in range(NT128):
                        ps = [kb.pA, kb.pB][t % 2]
                        if o_sl == 0 and t < 2:
                            em.w(t_eng, kb.sAC, (P, "exp", 15, 39))
                        else:
                            pv = (o_sl, t - 2) if t >= 2 else (o_sl - 1, 14 + t)
                            em.w(t_eng, kb.sAC, (P, "woev", *pv))
                        ins = None
                        for c in range(8):
                            ins = t_eng.matmul(
                                ps[:],
                                AR[:, AT_CTX + c * D + t * 128:
                                   AT_CTX + c * D + (t + 1) * 128],
                                wslot[c // 4][:, (c % 4) * TW:(c % 4 + 1) * TW],
                                start=(c == 0), stop=(c == 7))
                        kb.sMM.inc(ins)
                        em.rec((P, "womm", o_sl, t), kb.sMM)
            em.on("tensor", te)

            def acf(a_raw):
                a = E(a_raw, kb.sAC)
                em.rec((P, "xk"), kb.sAC)
                for kp in range(2):
                    for tw in range(4):
                        ps = [kb.pA, kb.pB][tw % 2]
                        em.w(a, kb.sMM, (P, "kmm", kp, tw))
                        a.activation(kbuf[2 * kp][0:64, tw * TW:(tw + 1) * TW],
                                     ps[0:64, :], AF.Copy)
                        if (kp, tw) != (0, 0):
                            pv = (kp, tw - 1) if tw else (kp - 1, 3)
                            em.w(a, kb.sSHK, (P, "ksh", *pv))
                        elif hf == 1:
                            em.w(a, kb.sSHC, (("a0", 0), "csh", 15, 3))
                        a.activation(shtmp[64:128, 0:TW], ps[64:128, :], AF.Copy)
                        em.rec((P, "kodd", kp, tw), kb.sAC)
                        em.w(a, kb.sAC, (P, "kodd", kp, tw))
                        kb.sSHK.inc(a_raw.dma_start(
                            out=kbuf[2 * kp + 1][0:64, tw * TW:(tw + 1) * TW],
                            in_=shtmp[64:128, 0:TW]), 16)
                        em.rec((P, "ksh", kp, tw), kb.sSHK)
                        for par in range(2):
                            if (kp, tw, par) != (0, 0, 0):
                                pv = (kp, tw, 0) if par else (
                                    (kp, tw - 1, 1) if tw else (kp - 1, 3, 1))
                                em.w(a, kb.sMM, (P, "knmm", *pv))
                            lo, hi = (0, 64) if par == 0 else (64, 128)
                            a.activation(ktmp[lo:hi, 0:TW], ps[lo:hi, :],
                                         AF.Square)
                            em.rec((P, "ksq", kp, tw, par), kb.sAC)
                for tc in range(16):
                    ps = [kb.pA, kb.pB][tc % 2]
                    em.w(a, kb.sMM, (P, "vmm", tc))
                    a.activation(
                        bass.AP(kb.AR, AT_VG + tc * 260,
                                [[57344, 128], [65, 4], [1, 64]]),
                        bass.AP(kb.pA if tc % 2 == 0 else kb.pB, 0,
                                [[512, 128], [64, 4], [1, 64]]),
                        AF.Copy)
                    em.rec((P, "vev", tc), kb.sAC)
                for p in range(8):
                    kvl = p // 2
                    for tw in range(4):
                        ps = [kb.pA, kb.pB][tw % 2]
                        em.w(a, kb.sMM, (P, "qmm", p, tw))
                        if p >= 2:
                            em.w(a, kb.sMM, (P, "smm", 2 * (p - 2) + 1, 39))
                            em.w(a, kb.sSHQ, (P, "qsh", p - 2, 3))
                        a.activation(qpair[p % 2][:, tw * TW:(tw + 1) * TW],
                                     ps[:], AF.Copy, scale=0.125)
                        em.rec((P, "qev", p, tw), kb.sAC)
                        if p >= 1 and tw == 0:
                            em.w(a, kb.sMM, (P, "smm", 2 * p - 1, 39))
                        em.w(a, kb.sAC, (P, "qev", p, tw))
                        kb.sSHQ.inc(a_raw.dma_start(
                            out=qodd[0:64, tw * TW:(tw + 1) * TW],
                            in_=qpair[p % 2][64:128, tw * TW:(tw + 1) * TW]), 16)
                        em.rec((P, "qsh", p, tw), kb.sSHQ)
                        for par in range(2):
                            pv = (P, "qnmm", p, tw - 1, par) if tw else \
                                (P, "qnmm", p - 1, 3, par) if p else \
                                (P, "knmm", 1, 3, par)
                            em.w(a, kb.sMM, pv)
                            lo, hi = (0, 64) if par == 0 else (64, 128)
                            a.activation(ktmp[lo:hi, 0:TW], ps[lo:hi, :],
                                         AF.Square, scale=0.125)
                            em.rec((P, "qsq", p, tw, par), kb.sAC)
                    for par in range(2):
                        em.w(a, kb.sDV, (P, "qred", p, 3, par))
                        if p > 0:
                            em.w(a, kb.sMM, (P, "negbmm", p - 1, par))
                        a.activation(st[0:1, 2 + par:3 + par],
                                     st[0:1, par:par + 1], AF.Sqrt,
                                     scale=st[0:1, 8 + kvl:9 + kvl])
                        em.rec((P, "csqrt", p, par), kb.sAC)
                        em.w(a, kb.sMM, (P, "negbmm", p, par))
                        a.activation(st[:, 13 + par:14 + par], kb.pC[:, 0:1],
                                     AF.Copy, scale=-1.0)
                        em.rec((P, "negb", p, par), kb.sAC)
                    for h in (2 * p, 2 * p + 1):
                        hcfg = {
                            "first_h": 0, "first_hqt": (0, 0), "pt": pt3,
                            "a_raw": a_raw, "shtmp": shtmp,
                            "sum_free": None,
                            "negb_ap": st[:, 13 + h % 2:14 + h % 2],
                        }
                        if h % 2 == 1:
                            hcfg["csh_dst"] = lambda hh, qt: AR[
                                64:128, AT_CTX + (hh // 2) * D + qt * TW:
                                AT_CTX + (hh // 2) * D + (qt + 1) * TW]
                        _head_act(kb, em, a, P, h, hcfg)
                while kb._csh_pend:
                    _flush_csh(kb, em, a, a_raw, shtmp)
                for o_sl in range(4):
                    for t in range(NT128):
                        em.w(a, kb.sMM, (P, "womm", o_sl, t))
                        gi = o_sl * 16 + t
                        kslot = (adbase + gi) % 2
                        if gi >= 2:
                            em.w(a, kb.sAD[kslot], (P, "ad", *divmod(gi - 2, 16)))
                        else:
                            em.w(a, kb.sAD[0], (P, "adpre0"))
                            em.w(a, kb.sAD[1], (P, "adpre1"))
                        a.activation(ev_f[kslot], [kb.pA, kb.pB][t % 2][:],
                                     AF.Copy)
                        em.rec((P, "woev", o_sl, t), kb.sAC)
            em.on("scalar", acf)

            def ve(v_raw):
                v = E(v_raw, kb.sDV)
                for kp in range(2):
                    for tw in range(4):
                        for par in range(2):
                            kvl = 2 * kp + par
                            em.w(v, kb.sMM, (P, "knmm", kp, tw, par))
                            if (kp, tw, par) != (0, 0, 0):
                                pv = (kp, tw, 0) if par else (
                                    (kp, tw - 1, 1) if tw else (kp - 1, 3, 1))
                                em.w(v, kb.sDV, (P, "kred", *pv))
                            v.reduce_max(st[0:1, 12:13],
                                         [kb.pF, kb.pT][par][0:1, 0:TW], AX.X)
                            em.rec((P, "kredr", kp, tw, par), kb.sDV)
                            em.w(v, kb.sDV, (P, "kredr", kp, tw, par))
                            if tw == 0:
                                v.tensor_copy(st[0:1, 8 + kvl:9 + kvl],
                                              st[0:1, 12:13])
                            else:
                                v.tensor_max(st[0:1, 8 + kvl:9 + kvl],
                                             st[0:1, 8 + kvl:9 + kvl],
                                             st[0:1, 12:13])
                            em.rec((P, "kred", kp, tw, par), kb.sDV)
                for p in range(8):
                    for tw in range(4):
                        for par in range(2):
                            em.w(v, kb.sMM, (P, "qnmm", p, tw, par))
                            pv = (P, "qred", p, tw, 0) if par else (
                                (P, "qred", p, tw - 1, 1) if tw else (
                                    (P, "qred", p - 1, 3, 1) if p else
                                    (P, "kred", 1, 3, 1)))
                            em.w(v, kb.sDV, pv)
                            if tw == 0 and p > 0:
                                em.w(v, kb.sAC, (P, "csqrt", p - 1, par))
                            v.reduce_max(st[0:1, 12:13],
                                         [kb.pF, kb.pT][par][0:1, 0:TW], AX.X)
                            em.rec((P, "qredr", p, tw, par), kb.sDV)
                            em.w(v, kb.sDV, (P, "qredr", p, tw, par))
                            if tw == 0:
                                v.tensor_copy(st[0:1, par:par + 1],
                                              st[0:1, 12:13])
                            else:
                                v.tensor_max(st[0:1, par:par + 1],
                                             st[0:1, par:par + 1],
                                             st[0:1, 12:13])
                            em.rec((P, "qred", p, tw, par), kb.sDV)
                    for h in (2 * p, 2 * p + 1):
                        hcfg = {"dh": 64, "first_hqt": (0, 0), "pt": pt3}
                        if h % 2 == 0:
                            hcfg["ctx_dst"] = lambda hh, qt: AR[
                                0:64, AT_CTX + (hh // 2) * D + qt * TW:
                                AT_CTX + (hh // 2) * D + (qt + 1) * TW]
                        else:
                            hcfg["ctx_dst"] = lambda hh, qt: None
                            hcfg["shtmp_prev"] = lambda hh, qt: (
                                (kb.sSHC, (P, "csh", hh, qt - 1)) if qt else
                                ((kb.sSHC, (P, "csh", hh - 2, 3)) if hh >= 3
                                 else (kb.sSHK, (P, "ksh", 1, 3))))
                        _head_dve(kb, em, v, P, h, hcfg)
            em.on("vector", ve)


def phase_attn1(kb, em):
    AR, ARf = kb.AR, kb.ARf
    kbuf = [AR[:, AT_KG + i * D:AT_KG + (i + 1) * D] for i in range(2)]
    vw = AR[:, AT_W:AT_W + 4096]
    wslot = [AR[:, AT_W + i * D:AT_W + (i + 1) * D] for i in range(2)]
    qslot = [AR[:, AT_Q + i * D:AT_Q + (i + 1) * D] for i in range(2)]
    ktmp = AR[:, AT_KT:AT_KT + TW]
    pt3 = [AR[:, AT_PT:AT_PT + TW], AR[:, AT_PT + TW:AT_PT + 2 * TW],
           AR[:, AT_PT3:AT_PT3 + TW]]
    bias_f = ARf[0:1, AT_BIAF // 2:AT_BIAF // 2 + 768]
    bias_b = AR[0:1, AT_BIAB:AT_BIAB + 768]
    bo_b = AR[0:1, AT_BOB:AT_BOB + D]
    ev_f = [ARf[:, AT_EV // 2 + i * TW:AT_EV // 2 + (i + 1) * TW]
            for i in range(2)]
    st = kb.stats
    SC = float(1.0 / np.sqrt(128.0))
    BLOCKS = [(2, "k", 0), (3, "k", 1), (0, "q", 0), (1, "q", 1),
              (4, "v", 0), (5, "v", 1)]

    for hf in range(2):
        P = ("a1", hf)
        with em.block(f"attn1_{hf}"):
            em.rec((P, "adpre0"), kb.sAD[0])
            em.rec((P, "adpre1"), kb.sAD[1])
            adbase = kb._ad_idx

            def sy(e):
                def load(par, c0, c1, src, key):
                    kb.sWL[par].inc(e.dma_start(
                        out=kb.stage[:, par * D + c0:par * D + c1], in_=src), 16)
                    em.rec(key, kb.sWL[par])

                if hf == 0:
                    em.w(e, kb.sAD[0], (P, "adpre0"))
                    em.w(e, kb.sAD[1], (P, "adpre1"))
                    kb.sIN[1].inc(e.dma_start(
                        out=ARf[0:1, AT_EV // 2:AT_EV // 2 + D],
                        in_=kb.d["l1_bo"]), 16)
                    em.rec((P, "bol"), kb.sIN[1])
                for g in range(4):
                    grp = 4 * hf + g
                    if g > 0 or hf > 0:
                        pg = (P, "biasc", g - 1) if g > 0 else \
                            (("a1", 0), "biasc", 3)
                        em.w(e, kb.sPL, pg)
                    kb.sIN[0].inc(e.dma_start(
                        out=bias_f, in_=kb.d["l1_bqkv"][grp:grp + 1, :]), 16)
                    em.rec((P, "biasl", g), kb.sIN[0])
                    for bi, (b, kind, hl) in enumerate(BLOCKS):
                        par = bi % 2
                        if bi >= 2:
                            em.w(e, kb.sPL, (P, "wc", g, bi - 2))
                        elif g > 0:
                            em.w(e, kb.sPL, (P, "wc", g - 1, 4 + bi))
                        elif hf == 1:
                            em.w(e, kb.sPL, (("a1", 0), "wc_wo", 3, bi))
                        load(par, 0, D, kb.d["l1_wqkv"][grp, b],
                             (P, "wl", g, bi))
                for o_sl in range(4):
                    for sh in range(2):
                        em.w(e, kb.sPL, (P, "wc", 3, 4 + sh) if o_sl == 0
                             else (P, "wc_wo", o_sl - 1, sh))
                        for c in range(4):
                            load(sh, c * TW, (c + 1) * TW,
                                 kb.d["l1_wo"][o_sl, hf * 8 + sh * 4 + c],
                                 (P, "wl_wo", o_sl, sh, c))
            em.on("sync", sy)

            if hf == 0:
                def gp_bo(g_raw):
                    g_ = E(g_raw, kb.sPL)
                    em.w(g_, kb.sIN[1], (P, "bol"))
                    g_.tensor_copy(bo_b[:], ARf[0:1, AT_EV // 2:AT_EV // 2 + D])
                    em.rec((P, "boc"), kb.sPL)
                em.on("gpsimd", gp_bo)

            def gp(g_raw):
                g_ = E(g_raw, kb.sPL)
                for g in range(4):
                    em.w(g_, kb.sIN[0], (P, "biasl", g))
                    if g > 0 or hf > 0:
                        pg = (P, "vmm", g - 1, 15) if g > 0 else \
                            (("a1", 0), "vmm", 3, 15)
                        em.w(g_, kb.sMM, pg)
                    g_.tensor_copy(bias_b[:], bias_f)
                    em.rec((P, "biasc", g), kb.sPL)
                    for bi, (b, kind, hl) in enumerate(BLOCKS):
                        par = bi % 2
                        em.w(g_, kb.sWL[par], (P, "wl", g, bi))
                        if kind == "v":
                            em.w(g_, kb.sMM, (P, "qmm", g, 1, 3))
                            if g > 0:
                                em.w(g_, kb.sMM, (P, "vmm", g - 1, 15))
                            g_.tensor_copy(
                                bass.AP(kb.AR, AT_W + hl * 128,
                                        [[57344, 128], [256, NK], [1, 128]]),
                                kb.stage[:, par * D:(par + 1) * D])
                        else:
                            if bi >= 2:
                                pb, pkind, phl = BLOCKS[bi - 2]
                                em.w(g_, kb.sMM,
                                     (P, "kmm" if pkind == "k" else "qmm",
                                      g, phl, 3))
                            elif g > 0:
                                em.w(g_, kb.sMM, (P, "vmm", g - 1, 15))
                            elif hf == 1:
                                em.w(g_, kb.sMM, (("a1", 0), "womm", 3, 15))
                            g_.tensor_copy(wslot[par][:],
                                           kb.stage[:, par * D:(par + 1) * D])
                        em.rec((P, "wc", g, bi), kb.sPL)
                for o_sl in range(4):
                    for sh in range(2):
                        em.w(g_, kb.sWL[sh], (P, "wl_wo", o_sl, sh, 3))
                        em.w(g_, kb.sMM, (P, "vmm", 3, 15) if o_sl == 0
                             else (P, "womm", o_sl - 1, 15))
                        g_.tensor_copy(wslot[sh][:],
                                       kb.stage[:, sh * D:(sh + 1) * D])
                        em.rec((P, "wc_wo", o_sl, sh), kb.sPL)
                    for t in range(NT128):
                        kslot = kb._ad_idx % 2
                        kb._ad_idx += 1
                        em.w(g_raw, kb.sAC, (P, "woev", o_sl, t))
                        if o_sl == 0 and t == 0:
                            em.w(g_raw, kb.sAD[0], (P, "adpre0"))
                            em.w(g_raw, kb.sAD[1], (P, "adpre1"))
                            g_raw.wait_ge(kb.sXS.h, 32)
                        kb.sAD[kslot].inc(g_raw.dma_start(
                            out=kb.d["out"][t * 128:(t + 1) * 128,
                                            o_sl * TW:(o_sl + 1) * TW],
                            in_=ev_f[kslot], accum_op=ALU.add), 16)
                        em.rec((P, "ad", o_sl, t), kb.sAD[kslot])
            em.on("gpsimd", gp)

            def te(t_eng):
                for g in range(4):
                    while kb._rs_pend:
                        _flush_rsmm(kb, em, t_eng, P)
                    for bi, (b, kind, hl) in enumerate(BLOCKS):
                        em.w(t_eng, kb.sPL, (P, "wc", g, bi))
                        em.w(t_eng, kb.sPL, (P, "biasc", g))
                        if kind == "k":
                            for tw in range(4):
                                ps = [kb.pA, kb.pB][tw % 2]
                                if tw >= 2:
                                    em.w(t_eng, kb.sAC, (P, "ksq", g, hl, tw - 2))
                                elif hl == 1:
                                    em.w(t_eng, kb.sAC, (P, "ksq", g, 0, 2 + tw))
                                elif g > 0:
                                    em.w(t_eng, kb.sAC,
                                         (P, "exp", 2 * g - 1, 39))
                                elif hf == 1:
                                    em.w(t_eng, kb.sAC,
                                         (("a1", 0), "woev", 3, 14 + tw))
                                ins = None
                                for k in range(NK):
                                    ins = t_eng.matmul(
                                        ps[:], wslot[bi % 2][:, k * 128:(k + 1) * 128],
                                        kb.xn[:, k * D + tw * TW:
                                              k * D + (tw + 1) * TW],
                                        start=(k == 0), stop=False)
                                ins = t_eng.matmul(
                                    ps[:], bias_b[0:1, (256 + hl * 128):
                                                  (384 + hl * 128)],
                                    kb.ones_row[0:1, 0:TW],
                                    start=False, stop=True)
                                kb.sMM.inc(ins)
                                em.rec((P, "kmm", g, hl, tw), kb.sMM)
                                em.w(t_eng, kb.sAC, (P, "ksq", g, hl, tw))
                                if tw > 0:
                                    em.w(t_eng, kb.sDV, (P, "kred", g, hl, tw - 1))
                                elif g > 0 or hf > 0:
                                    pg = (P, "sumev_last", g - 1, 1) if g > 0 \
                                        else (("a1", 0), "sumev_last", 3, 1)
                                    em.w(t_eng, kb.sAC, pg)
                                    pk = (P, "qred", g - 1, hl, 3) if g > 0 \
                                        else (("a1", 0), "qred", 3, hl, 3)
                                    em.w(t_eng, kb.sDV, pk)
                                kb.sMM.inc(t_eng.matmul(
                                    [kb.pF, kb.pT][hl][0:1, 0:TW],
                                    kb.ones_col[:, 0:1], ktmp[:, 0:TW],
                                    start=True, stop=True))
                                em.rec((P, "knmm", g, hl, tw), kb.sMM)
                        elif kind == "q":
                            for tw in range(4):
                                ps = [kb.pA, kb.pB][tw % 2]
                                if tw >= 2:
                                    em.w(t_eng, kb.sAC, (P, "qsq", g, hl, tw - 2))
                                elif hl:
                                    em.w(t_eng, kb.sAC, (P, "qsq", g, 0, 2 + tw))
                                else:
                                    em.w(t_eng, kb.sAC, (P, "ksq", g, 1, 2 + tw))
                                ins = None
                                for k in range(NK):
                                    ins = t_eng.matmul(
                                        ps[:], wslot[bi % 2][:, k * 128:(k + 1) * 128],
                                        kb.xn[:, k * D + tw * TW:
                                              k * D + (tw + 1) * TW],
                                        start=(k == 0), stop=False)
                                ins = t_eng.matmul(
                                    ps[:], bias_b[0:1, hl * 128:(hl + 1) * 128],
                                    kb.ones_row[0:1, 0:TW],
                                    start=False, stop=True)
                                kb.sMM.inc(ins)
                                em.rec((P, "qmm", g, hl, tw), kb.sMM)
                                em.w(t_eng, kb.sAC, (P, "qsq", g, hl, tw))
                                if tw > 0:
                                    em.w(t_eng, kb.sDV, (P, "qred", g, hl, tw - 1))
                                else:
                                    em.w(t_eng, kb.sDV, (P, "kred", g, hl, 3))
                                kb.sMM.inc(t_eng.matmul(
                                    [kb.pF, kb.pT][hl][0:1, 0:TW],
                                    kb.ones_col[:, 0:1], ktmp[:, 0:TW],
                                    start=True, stop=True))
                                em.rec((P, "qnmm", g, hl, tw), kb.sMM)
                            em.w(t_eng, kb.sAC, (P, "csqrt", g, hl))
                            if (g, hl) != (0, 0):
                                em.w(t_eng, kb.sAC, (P, "negb", g, 0) if hl
                                     else (P, "negb", g - 1, 1))
                            kb.sMM.inc(t_eng.matmul(
                                kb.pC[0:128, 0:1], kb.onesf_row[0:1, 0:128],
                                st[0:1, 2 + hl:3 + hl], start=True, stop=True))
                            em.rec((P, "negbmm", g, hl), kb.sMM)
                        else:
                            if hl == 0:
                                continue
                            em.w(t_eng, kb.sPL, (P, "wc", g, 5))
                            for tc in range(16):
                                ps = [kb.pA, kb.pB][tc % 2]
                                if tc >= 2:
                                    em.w(t_eng, kb.sAC, (P, "vev", g, tc - 2))
                                else:
                                    em.w(t_eng, kb.sAC, (P, "qsq", g, 1, 2 + tc))
                                ins = None
                                for k in range(NK):
                                    ins = t_eng.matmul(
                                        ps[:, 0:256],
                                        kb.xn[:, k * D + tc * 128:
                                              k * D + (tc + 1) * 128],
                                        vw[:, k * 256:(k + 1) * 256],
                                        start=(k == 0), stop=False)
                                ins = t_eng.matmul(
                                    ps[:, 0:256], kb.ones_row[0:1, 0:128],
                                    bias_b[0:1, 512:768],
                                    start=False, stop=True)
                                kb.sMM.inc(ins)
                                em.rec((P, "vmm", g, tc), kb.sMM)
                    for hl in range(2):
                        h = 2 * g + hl
                        hcfg = {
                            "dh": 128,
                            "dh_acc": 128,
                            "first_h": 0,
                            "first_hqt": (0, 0),
                            "odd_shift": False,
                            "pt": pt3,
                            "k_ap": (lambda hl_: lambda kc: kbuf[hl_][
                                :, kc * 128:(kc + 1) * 128])(hl),
                            "v_ap": (lambda hl_: lambda kc: AR[
                                :, AT_VG + kc * 260 + hl_ * 128:
                                AT_VG + kc * 260 + (hl_ + 1) * 128])(hl),
                            "q_ap": (lambda hl_: lambda qt: qslot[hl_][
                                :, qt * TW:(qt + 1) * TW])(hl),
                            "ksh_dep": None,
                            "sum_free": (lambda g_: lambda hh, qt: (
                                (kb.sAC, (P, "sumev", hh, qt - 2)) if qt >= 2
                                else (kb.sAC, (P, "sumev", hh - 1, 2 + qt))
                                if hh % 2 else
                                (kb.sDV, (P, "qred", g_, qt, 3))))(g),
                        }
                        nb = (kb.sAC, (P, "negb", g, 1))
                        hcfg["sbank_free"] = [
                            (kb.sAC, (P, "vev", g, 15)),
                            (kb.sAC, (P, "vev", g, 15)), nb] \
                            if hl == 0 else \
                            [(kb.sAC, (P, "exp", h - 1, 39)),
                             (kb.sAC, (P, "exp", h - 1, 37)),
                             (kb.sAC, (P, "exp", h - 1, 38))]
                        _head_pe(kb, em, t_eng, P, h, hcfg)
                while kb._rs_pend:
                    _flush_rsmm(kb, em, t_eng, P)
                for o_sl in range(4):
                    em.w(t_eng, kb.sPL, (P, "wc_wo", o_sl, 1))
                    if o_sl == 0:
                        em.w(t_eng, kb.sDV, (P, "ctxev", 7, 3))
                    for t in range(NT128):
                        ps = [kb.pA, kb.pB][t % 2]
                        if o_sl == 0 and t < 2:
                            em.w(t_eng, kb.sAC, (P, "exp", 7, 39))
                        else:
                            pv = (o_sl, t - 2) if t >= 2 else (o_sl - 1, 14 + t)
                            em.w(t_eng, kb.sAC, (P, "woev", *pv))
                        ins = None
                        for c in range(8):
                            ins = t_eng.matmul(
                                ps[:],
                                AR[:, AT_CTX + c * D + t * 128:
                                   AT_CTX + c * D + (t + 1) * 128],
                                wslot[c // 4][:, (c % 4) * TW:(c % 4 + 1) * TW],
                                start=(c == 0), stop=(c == 7 and hf == 1))
                        if hf == 0:
                            if o_sl == 0 and t == 0:
                                em.w(t_eng, kb.sPL, (P, "boc"))
                            ins = t_eng.matmul(
                                ps[:], kb.ones_row[0:1, 0:128],
                                bo_b[0:1, o_sl * TW:(o_sl + 1) * TW],
                                start=False, stop=True)
                        kb.sMM.inc(ins)
                        em.rec((P, "womm", o_sl, t), kb.sMM)
            em.on("tensor", te)

            def acf(a_raw):
                a = E(a_raw, kb.sAC)
                for g in range(4):
                    for bi, (b, kind, hl) in enumerate(BLOCKS):
                        if kind == "k":
                            for tw in range(4):
                                ps = [kb.pA, kb.pB][tw % 2]
                                em.w(a, kb.sMM, (P, "kmm", g, hl, tw))
                                if tw == 0 and g > 0:
                                    em.w(a, kb.sMM, (P, "smm", 2 * (g - 1) + hl, 39))
                                a.activation(kbuf[hl][:, tw * TW:(tw + 1) * TW],
                                             ps[:], AF.Copy)
                                em.rec((P, "kev", g, hl, tw), kb.sAC)
                                pv = (P, "knmm", g, hl, tw - 1) if tw else \
                                    ((P, "knmm", g, 0, 3) if hl else
                                     ((P, "qnmm", g - 1, 1, 3) if g else None))
                                if pv:
                                    em.w(a, kb.sMM, pv)
                                a.activation(ktmp[:, 0:TW], ps[:], AF.Square)
                                em.rec((P, "ksq", g, hl, tw), kb.sAC)
                        elif kind == "q":
                            for tw in range(4):
                                ps = [kb.pA, kb.pB][tw % 2]
                                em.w(a, kb.sMM, (P, "qmm", g, hl, tw))
                                if tw == 0 and g > 0:
                                    em.w(a, kb.sMM, (P, "smm", 2 * (g - 1) + hl, 39))
                                a.activation(qslot[hl][:, tw * TW:(tw + 1) * TW],
                                             ps[:], AF.Copy, scale=SC)
                                em.rec((P, "qev", g, hl, tw), kb.sAC)
                                pv = (P, "qnmm", g, hl, tw - 1) if tw else \
                                    (P, "qnmm", g, 0, 3) if hl else \
                                    (P, "knmm", g, 1, 3)
                                em.w(a, kb.sMM, pv)
                                a.activation(ktmp[:, 0:TW], ps[:], AF.Square,
                                             scale=SC)
                                em.rec((P, "qsq", g, hl, tw), kb.sAC)
                            em.w(a, kb.sDV, (P, "qred", g, hl, 3))
                            if (g, hl) != (0, 0):
                                pv = (P, "negbmm", g, hl - 1) if hl else \
                                    (P, "negbmm", g - 1, 1)
                                em.w(a, kb.sMM, pv)
                            a.activation(st[0:1, 2 + hl:3 + hl],
                                         st[0:1, hl:hl + 1], AF.Sqrt,
                                         scale=st[0:1, 8 + hl:9 + hl])
                            em.rec((P, "csqrt", g, hl), kb.sAC)
                            em.w(a, kb.sMM, (P, "negbmm", g, hl))
                            a.activation(st[:, 13 + hl:14 + hl], kb.pC[:, 0:1],
                                         AF.Copy, scale=-1.0)
                            em.rec((P, "negb", g, hl), kb.sAC)
                        else:
                            if hl == 0:
                                continue
                            for tc in range(16):
                                ps = [kb.pA, kb.pB][tc % 2]
                                em.w(a, kb.sMM, (P, "vmm", g, tc))
                                if tc < 2 and g > 0:
                                    em.w(a, kb.sMM, (P, "cs", 2 * g - 1, 39))
                                a.activation(
                                    bass.AP(kb.AR, AT_VG + tc * 260,
                                            [[57344, 128], [1, 256]]),
                                    ps[:, 0:256], AF.Copy)
                                em.rec((P, "vev", g, tc), kb.sAC)
                    for hl in range(2):
                        h = 2 * g + hl
                        hcfg = {"first_h": 0, "first_hqt": (0, 0), "pt": pt3,
                                "a_raw": a_raw, "shtmp": None,
                                "sum_free": True,
                                "negb_ap": st[:, 13 + hl:14 + hl]}
                        _head_act(kb, em, a, P, h, hcfg)
                        em.rec((P, "sumev_last", g, hl), kb.sAC)
                for o_sl in range(4):
                    for t in range(NT128):
                        em.w(a, kb.sMM, (P, "womm", o_sl, t))
                        gi = o_sl * 16 + t
                        kslot = (adbase + gi) % 2
                        if gi >= 2:
                            em.w(a, kb.sAD[kslot], (P, "ad", *divmod(gi - 2, 16)))
                        else:
                            em.w(a, kb.sAD[0], (P, "adpre0"))
                            em.w(a, kb.sAD[1], (P, "adpre1"))
                        a.activation(ev_f[kslot], [kb.pA, kb.pB][t % 2][:],
                                     AF.Copy)
                        em.rec((P, "woev", o_sl, t), kb.sAC)
            em.on("scalar", acf)

            def ve(v_raw):
                v = E(v_raw, kb.sDV)
                for g in range(4):
                    for hl in range(2):
                        for tw in range(4):
                            em.w(v, kb.sMM, (P, "knmm", g, hl, tw))
                            if (g, hl, tw) != (0, 0, 0):
                                pv = (P, "kred", g, hl, tw - 1) if tw else (
                                    (P, "kred", g, hl - 1, 3) if hl else
                                    (P, "kred", g - 1, 1, 3))
                                em.w(v, kb.sDV, pv)
                            if tw == 0 and g > 0:
                                em.w(v, kb.sAC, (P, "csqrt", g - 1, hl))
                            v.reduce_max(st[0:1, 12:13],
                                         [kb.pF, kb.pT][hl][0:1, 0:TW], AX.X)
                            em.rec((P, "kredr", g, hl, tw), kb.sDV)
                            em.w(v, kb.sDV, (P, "kredr", g, hl, tw))
                            if tw == 0:
                                v.tensor_copy(st[0:1, 8 + hl:9 + hl],
                                              st[0:1, 12:13])
                            else:
                                v.tensor_max(st[0:1, 8 + hl:9 + hl],
                                             st[0:1, 8 + hl:9 + hl],
                                             st[0:1, 12:13])
                            em.rec((P, "kred", g, hl, tw), kb.sDV)
                    for hl in range(2):
                        for tw in range(4):
                            em.w(v, kb.sMM, (P, "qnmm", g, hl, tw))
                            pv = (P, "qred", g, hl, tw - 1) if tw else (
                                (P, "qred", g, hl - 1, 3) if hl else
                                (P, "kred", g, 1, 3))
                            em.w(v, kb.sDV, pv)
                            if tw == 0 and g > 0:
                                em.w(v, kb.sAC, (P, "csqrt", g - 1, hl))
                            v.reduce_max(st[0:1, 12:13],
                                         [kb.pF, kb.pT][hl][0:1, 0:TW], AX.X)
                            em.rec((P, "qredr", g, hl, tw), kb.sDV)
                            em.w(v, kb.sDV, (P, "qredr", g, hl, tw))
                            if tw == 0:
                                v.tensor_copy(st[0:1, hl:hl + 1],
                                              st[0:1, 12:13])
                            else:
                                v.tensor_max(st[0:1, hl:hl + 1],
                                             st[0:1, hl:hl + 1],
                                             st[0:1, 12:13])
                            em.rec((P, "qred", g, hl, tw), kb.sDV)
                    for hl in range(2):
                        h = 2 * g + hl
                        hcfg = {
                            "dh": 128,
                            "first_hqt": (0, 0),
                            "pt": pt3,
                            "ctx_dst": lambda hh, qt: AR[
                                0:128, AT_CTX + hh * D + qt * TW:
                                AT_CTX + hh * D + (qt + 1) * TW],
                        }
                        _head_dve(kb, em, v, P, h, hcfg)
            em.on("vector", ve)


def _yev_dep(kb, P, ex, o_sl, tloc):
    if ex == 0:
        return (kb.sAC, (P, "yevA", ex, o_sl, tloc))
    return (kb.sDV, (P, "yev", ex, o_sl, tloc))


def phase_moe(kb, em):
    AR, ARf = kb.AR, kb.ARf
    ms = kb.moe_s
    wfc = [AR[:, MO_WFC + i * D:MO_WFC + (i + 1) * D] for i in range(2)]
    wpj = [AR[:, MO_WPJ + i * 1024:MO_WPJ + (i + 1) * 1024] for i in range(2)]
    tmp_f = [ARf[:, MO_TMP // 2 + i * TW:MO_TMP // 2 + (i + 1) * TW]
             for i in range(2)]
    pss8 = None

    for hf in range(2):
        P = ("mo", hf)
        with em.block(f"moe_{hf}"):
            em.rec((P, "adpre0"), kb.sAD[0])
            em.rec((P, "adpre1"), kb.sAD[1])
            adbase = kb._ad_idx
            pss8 = [kb.pA, kb.pB, kb.pC, kb.pD, kb.pE, kb.pF, kb.pR, kb.pT]

            def sy(e):
                def load(par, c0, c1, src, key):
                    kb.sWL[par].inc(e.dma_start(
                        out=kb.stage[:, par * D + c0:par * D + c1], in_=src), 16)
                    em.rec(key, kb.sWL[par])

                u = 0
                for ex in range(8):
                    for ob in range(16):
                        par = u % 2
                        if u >= 2:
                            if ob >= 2:
                                em.w(e, kb.sAC, (P, "wc", "fc", ex, ob - 2))
                            else:
                                em.w(e, kb.sPL, (P, "wc", "pj", ex - 1, 3, 6 + ob))
                        elif hf == 1:
                            em.w(e, kb.sPL, (("mo", 0), "wc", "pj", 7, 3, 6 + ob))
                        load(par, 0, D, kb.d["l1_fc"][ex, ob],
                             (P, "wl", "fc", ex, ob))
                        u += 1
                    for o_sl in range(4):
                        for sh in range(8):
                            par = u % 2
                            if sh >= 2 or o_sl > 0:
                                pv = (o_sl, sh - 2) if sh >= 2 else (o_sl - 1, 6 + sh)
                                em.w(e, kb.sPL, (P, "wc", "pj", ex, *pv))
                            else:
                                em.w(e, kb.sAC, (P, "wc", "fc", ex, 14 + sh))
                            for c in range(2):
                                load(par, c * TW, (c + 1) * TW,
                                     kb.d["l1_pj"][ex, o_sl, sh * 2 + c],
                                     (P, "wl", "pj", ex, o_sl, sh, c))
                            u += 1
            em.on("sync", sy)

            def gp(g_raw):
                g_ = E(g_raw, kb.sPL)
                u = 0
                for ex in range(8):
                    u += 16
                    for o_sl in range(4):
                        for sh in range(8):
                            par = u % 2
                            em.w(g_, kb.sWL[par], (P, "wl", "pj", ex, o_sl, sh, 1))
                            gi2 = o_sl * 8 + sh
                            if gi2 >= 2:
                                pv = divmod(gi2 - 2, 8)
                                em.w(g_, kb.sMM, (P, "ymm", ex, pv[0], pv[1], 7))
                            else:
                                em.w(g_, kb.sMM, (P, "hmm", ex, 14 + gi2, 1))
                            g_.tensor_copy(wpj[par][:],
                                           kb.stage[:, par * D:par * D + 1024])
                            em.rec((P, "wc", "pj", ex, o_sl, sh), kb.sPL)
                            u += 1
                    if ex == 7:
                        for tloc in range(8):
                            kslot = kb._ad_idx % 2
                            kb._ad_idx += 1
                            dep = _yev_dep(kb, P, 7, 3, tloc)
                            em.w(g_raw, dep[0], dep[1])
                            if tloc == 0:
                                em.w(g_raw, kb.sAD[0], (P, "adpre0"))
                                em.w(g_raw, kb.sAD[1], (P, "adpre1"))
                                g_raw.wait_ge(kb.sXS.h, 32)
                            t128 = hf * 8 + tloc
                            kb.sAD[kslot].inc(g_raw.dma_start(
                                out=kb.d["out"][t128 * 128:(t128 + 1) * 128, :],
                                in_=ARf[:, tloc * D:(tloc + 1) * D],
                                accum_op=ALU.add), 16)
                            em.rec((P, "ad", tloc), kb.sAD[kslot])
            em.on("gpsimd", gp)

            def te(t_eng):
                for tloc in range(8):
                    if tloc > 0:
                        em.w(t_eng, kb.sDV, (P, "lg", tloc - 1))
                    ins = None
                    for k in range(NK):
                        ins = t_eng.matmul(
                            kb.pR[:, 0:8],
                            kb.xn[:, k * D + hf * 1024 + tloc * 128:
                                  k * D + hf * 1024 + (tloc + 1) * 128],
                            kb.router_b[:, k * 8:(k + 1) * 8],
                            start=(k == 0), stop=(k == NK - 1))
                    kb.sMM.inc(ins)
                    em.rec((P, "rmm", tloc), kb.sMM)
                for ex in range(8):
                    for ob in range(16):
                        par = ob % 2 if False else (ex * 16 + ob) % 2
                        em.w(t_eng, kb.sAC, (P, "wc", "fc", ex, ob))
                        for tw in range(2):
                            ps = [kb.pA, kb.pB][tw]
                            if ex > 0 or ob > 0:
                                if ob:
                                    em.w(t_eng, kb.sDV, (P, "hrel", ex, ob - 1, tw))
                                else:
                                    dep = _yev_dep(kb, P, ex - 1, 3, 6 + tw)
                                    em.w(t_eng, dep[0], dep[1])
                            ins = None
                            for k in range(NK):
                                ins = t_eng.matmul(
                                    ps[:], wfc[par][:, k * 128:(k + 1) * 128],
                                    kb.xn[:, k * D + hf * 1024 + tw * TW:
                                          k * D + hf * 1024 + (tw + 1) * TW],
                                    start=(k == 0), stop=(k == NK - 1))
                            kb.sMM.inc(ins)
                            em.rec((P, "hmm", ex, ob, tw), kb.sMM)
                    for o_sl in range(4):
                        for sh in range(8):
                            par = (o_sl * 8 + sh) % 2
                            em.w(t_eng, kb.sPL, (P, "wc", "pj", ex, o_sl, sh))
                            if sh == 0 and o_sl == 0:
                                em.w(t_eng, kb.sAC, (P, "h2", ex, 15, 1))
                                if ex == 0:
                                    em.w(t_eng, kb.sDV, (P, "wtok", 7))
                            for tloc in range(8):
                                ps = pss8[tloc]
                                if sh == 0:
                                    if o_sl > 0:
                                        dep = _yev_dep(kb, P, ex, o_sl - 1, tloc)
                                        em.w(t_eng, dep[0], dep[1])
                                    elif ex > 0:
                                        dep = _yev_dep(kb, P, ex - 1, 3, tloc)
                                        em.w(t_eng, dep[0], dep[1])
                                    elif tloc == 6:
                                        em.w(t_eng, kb.sDV, (P, "lg", 7))
                                    if ex == 0 and o_sl == 0 and tloc < 2:
                                        em.w(t_eng, kb.sDV,
                                             (P, "hrel", ex, 15, tloc))
                                ins = None
                                for c in range(2):
                                    ic = sh * 2 + c
                                    ins = t_eng.matmul(
                                        ps[:],
                                        AR[:, MO_H2 + ic * 1024 + tloc * 128:
                                           MO_H2 + ic * 1024 + (tloc + 1) * 128],
                                        wpj[par][:, c * TW:(c + 1) * TW],
                                        start=(ic == 0), stop=(ic == 15))
                                kb.sMM.inc(ins)
                                em.rec((P, "ymm", ex, o_sl, sh, tloc), kb.sMM)
            em.on("tensor", te)

            def ve(v_raw):
                v = E(v_raw, kb.sDV)
                for tloc in range(8):
                    em.w(v, kb.sMM, (P, "rmm", tloc))
                    wv = lambda key: em.w(v, kb.sDV, key)
                    v.tensor_copy(ms[:, 0:8], kb.pR[:, 0:8])
                    em.rec((P, "lg", tloc), kb.sDV)
                    wv((P, "lg", tloc))
                    v.reduce_max(ms[:, 32:33], ms[:, 0:8], AX.X)
                    em.rec((P, "m1", tloc), kb.sDV)
                    wv((P, "m1", tloc))
                    v.tensor_scalar(ms[:, 8:16], ms[:, 0:8], ms[:, 32:33],
                                    None, ALU.is_equal)
                    em.rec((P, "eq1", tloc), kb.sDV)
                    wv((P, "eq1", tloc))
                    v.scalar_tensor_tensor(ms[:, 16:24], ms[:, 8:16], -1e30,
                                           ms[:, 0:8], ALU.mult, ALU.add)
                    em.rec((P, "lg2", tloc), kb.sDV)
                    wv((P, "lg2", tloc))
                    v.reduce_max(ms[:, 33:34], ms[:, 16:24], AX.X)
                    em.rec((P, "m2", tloc), kb.sDV)
                    wv((P, "m2", tloc))
                    v.tensor_sub(ms[:, 34:35], ms[:, 32:33], ms[:, 33:34])
                    em.rec((P, "dd", tloc), kb.sDV)
                    wv((P, "dd", tloc))
                    v.tensor_scalar(ms[:, 24:32], ms[:, 16:24], ms[:, 33:34],
                                    None, ALU.is_equal)
                    em.rec((P, "eq2", tloc), kb.sDV)
                    em.w(v, kb.sAC, (P, "p12", tloc))
                    v.tensor_scalar(ms[:, 40:48], ms[:, 8:16], ms[:, 35:36],
                                    None, ALU.mult)
                    em.rec((P, "t1", tloc), kb.sDV)
                    wv((P, "t1", tloc))
                    v.scalar_tensor_tensor(
                        ms[:, 64 + tloc * 8:72 + tloc * 8], ms[:, 24:32],
                        ms[:, 36:37], ms[:, 40:48], ALU.mult, ALU.add)
                    em.rec((P, "wtok", tloc), kb.sDV)
                for ex in range(8):
                    for ob in range(16):
                        for tw in range(2):
                            em.w(v, kb.sMM, (P, "hmm", ex, ob, tw))
                            if ex > 0 or ob > 0:
                                em.w(v, kb.sAC, (P, "h2", ex, ob - 1, tw)
                                     if ob else (P, "h2", ex - 1, 15, tw))
                            v.tensor_scalar_max(tmp_f[tw], [kb.pA, kb.pB][tw][:],
                                                0.0)
                            em.rec((P, "hrel", ex, ob, tw), kb.sDV)
                    for o_sl in range(4):
                        for tloc in range(8):
                            if ex == 0:
                                continue
                            em.w(v, kb.sMM, (P, "ymm", ex, o_sl, 7, tloc))
                            em.w(v, kb.sDV, (P, "wtok", tloc))
                            if ex == 1:
                                em.w(v, kb.sAC, (P, "yevA", 0, o_sl, tloc))
                            else:
                                em.w(v, kb.sDV, (P, "yev", ex - 1, o_sl, tloc))
                            ydst = ARf[:, tloc * D + o_sl * TW:
                                       tloc * D + (o_sl + 1) * TW]
                            v.scalar_tensor_tensor(
                                ydst, pss8[tloc][:],
                                ms[:, 64 + tloc * 8 + ex:65 + tloc * 8 + ex],
                                ydst, ALU.mult, ALU.add)
                            em.rec((P, "yev", ex, o_sl, tloc), kb.sDV)
            em.on("vector", ve)

            def acf(a_raw):
                a = E(a_raw, kb.sAC)
                for tloc in range(8):
                    em.w(a, kb.sDV, (P, "dd", tloc))
                    if tloc > 0:
                        em.w(a, kb.sDV, (P, "t1", tloc - 1))
                        em.w(a, kb.sDV, (P, "wtok", tloc - 1))
                    a.activation(ms[:, 35:36], ms[:, 34:35], AF.Sigmoid)
                    a.activation(ms[:, 36:37], ms[:, 34:35], AF.Sigmoid,
                                 scale=-1.0)
                    em.rec((P, "p12", tloc), kb.sAC)

                fcn = [0]

                def fc_cast(upto):
                    while fcn[0] <= min(upto, 127):
                        ex2, ob2 = divmod(fcn[0], 16)
                        par = ob2 % 2
                        em.w(a, kb.sWL[par], (P, "wl", "fc", ex2, ob2))
                        if ex2 > 0 or ob2 >= 2:
                            pv = (P, "hmm", ex2, ob2 - 2, 1) if ob2 >= 2 else \
                                (P, "hmm", ex2 - 1, 14 + ob2, 1)
                            em.w(a, kb.sMM, pv)
                        elif hf == 1:
                            em.w(a, kb.sMM, (("mo", 0), "hmm", 7, 14 + ob2, 1))
                        a.activation(wfc[par][:],
                                     kb.stage[:, par * D:(par + 1) * D],
                                     AF.Copy)
                        em.rec((P, "wc", "fc", ex2, ob2), kb.sAC)
                        fcn[0] += 1

                for ex in range(8):
                    for ob in range(16):
                        fc_cast(min(ex * 16 + ob + 2, ex * 16 + 15))
                        for tw in range(2):
                            em.w(a, kb.sDV, (P, "hrel", ex, ob, tw))
                            a.activation(
                                AR[:, MO_H2 + ob * 1024 + tw * TW:
                                   MO_H2 + ob * 1024 + (tw + 1) * TW],
                                tmp_f[tw], AF.Square)
                            em.rec((P, "h2", ex, ob, tw), kb.sAC)
                    if ex == 0:
                        for o_sl in range(4):
                            for tloc in range(8):
                                em.w(a, kb.sMM, (P, "ymm", 0, o_sl, 7, tloc))
                                em.w(a, kb.sDV, (P, "wtok", tloc))
                                if hf == 1 and o_sl == 0 and tloc == 0:
                                    em.w(a, kb.sAD[0], (P, "adpre0"))
                                    em.w(a, kb.sAD[1], (P, "adpre1"))
                                ydst = ARf[:, tloc * D + o_sl * TW:
                                           tloc * D + (o_sl + 1) * TW]
                                a.activation(ydst, pss8[tloc][:], AF.Copy,
                                             scale=ms[:, 64 + tloc * 8:
                                                      65 + tloc * 8])
                                em.rec((P, "yevA", 0, o_sl, tloc), kb.sAC)
                    fc_cast(ex * 16 + 17)
            em.on("scalar", acf)


def phase_final(kb, em):
    with em.block("final"):
        def gp(g_raw):
            em.rec(("fin", "ad0"), kb.sAD[0])
            em.rec(("fin", "ad1"), kb.sAD[1])
            em.w(g_raw, kb.sAD[0], ("fin", "ad0"))
            em.w(g_raw, kb.sAD[1], ("fin", "ad1"))
        em.on("gpsimd", gp)




def _debug_dump(kb, em):
    with em.block("fin"):
        def sy(e):
            for c in range(NK):
                kb.sIN[0].inc(e.dma_start(
                    out=kb.d["out"][c * 128:(c + 1) * 128, 0:D // 2],
                    in_=kb.xn.bitcast(F32)[:, c * D // 2:(c + 1) * D // 2]), 16)
            if not em.dry:
                e.wait_ge(kb.sIN[0].h, kb.sIN[0].v)
        em.on("sync", sy)


def _phase_list(upto):
    phases = [phase_setup,
              lambda kb, em: phase_ln(kb, em, kb.d["x0"], "ln0")]
    if upto == "ln0":
        phases.append(_debug_dump)
        return phases
    phases.append(phase_ffn0)
    if upto == "ffn0":
        phases.append(_debug_dump)
        return phases
    phases.append(phase_attn0)
    if upto == "attn0":
        phases.append(_debug_dump)
        return phases
    phases.append(lambda kb, em: phase_ln(kb, em, kb.d["xs"], "ln1",
                                          wait_ad=True, copy_out=True))
    if upto != "nomoe":
        phases.append(phase_moe)
    phases.append(phase_attn1)
    phases.append(phase_final)
    return phases


def build_nc(upto=None):
    kb = KB()
    kb.nc = bass.Bass("TRN2", target_bir_lowering=False)
    declare_dram(kb)
    with ExitStack() as stack:
        _alloc(kb, stack)
        phases = _phase_list(upto)
        em = Em(kb)
        kb._ad_idx = 0
        kb._ad_base = 0
        kb._rs_pend = []
        kb._rsmm_prev = None
        kb._csh_pend = []
        for ph in phases:
            kb._ad_base = kb._ad_idx
            ph(kb, em)
        assert not kb._rs_pend and not kb._csh_pend
        for s_ in kb.all_sems:
            s_.v = 0
        em.dry = False
        kb._ad_idx = 0
        kb._ad_base = 0
        kb._rs_pend = []
        kb._rsmm_prev = None
        kb._csh_pend = []
        for ph in phases:
            kb._ad_base = kb._ad_idx
            ph(kb, em)
    return kb


def kernel(**inputs):
    ins = host_prep({k: np.asarray(v) for k, v in inputs.items()})
    x_full = np.asarray(inputs["L0_attn_x"], dtype=np.float32)
    kb = build_nc()
    in_maps = []
    for core in range(8):
        m = dict(ins)
        m["x0"] = np.ascontiguousarray(x_full[core])
        in_maps.append(m)
    res = run_bass_kernel_spmd(kb.nc, in_maps, core_ids=list(range(8)))
    return np.stack([r["out"] for r in res.results]).astype(np.float32)
